# version 15
# baseline (speedup 1.0000x reference)
import numpy as np
import ml_dtypes
from contextlib import ExitStack

import concourse.bass as bass
import concourse.mybir as mybir
from concourse.bass_utils import run_bass_kernel_spmd

F32 = mybir.dt.float32
BF16 = mybir.dt.bfloat16
AF = mybir.ActivationFunctionType
ALU = mybir.AluOpType
AX = mybir.AxisListType
BF = ml_dtypes.bfloat16

NCORE = 8
TOK = 1024
TC = 8
D = 2048
DFF = 8192
SCALE = float(128.0 ** -0.5)
INVSCALE = float(128.0 ** 0.5)
EPS_LN = 1e-5
EPS_BIT = 1e-8
BIG = 30000.0
RG = [[0, 1, 2, 3, 4, 5, 6, 7]]
FUSED = True
SELF_WAIT = False
SIM_NOCC = False
DBG_STOP = None
RACE_DETECT = False

X_OFF = 0
XNF_OFF = 16384
KV_OFF = 24576
SLAB2_OFF = 40960
SC_OFF = 45056
WF32_OFF = 47104
WBF_OFF = 48128
ROWS_OFF = 50176
SMALL_OFF = 52224
SLAB_COLS = 53200

ZNF_OFF = KV_OFF
TAB_OFF = KV_OFF + 8192
HF_OFF = KV_OFF + 10240
WH_OFF = KV_OFF + 11264
SC2_OFF = KV_OFF + 12288
SP2_OFF = KV_OFF + 14336

ENGINES = ("pe", "dve", "act", "pool", "sp")
BLOCK_ATTR = {"pe": "tensor", "dve": "vector", "act": "scalar",
              "pool": "gpsimd", "sp": "sync"}
SEM_NAMES = ("pe", "dve", "act", "pool", "sp",
             "pool_dma", "sp_dma", "act_dma",
             "wfp0", "wfp1", "wfp2", "wfp3", "wfp4",
             "wb0", "wb1", "tab", "xio", "oio",
             "cst", "cci", "agl", "b2s", "brow", "rt", "dn", "up",
             "rg0", "rg1", "rg2", "rg3", "rg4", "rg5", "rg6", "rg7",
             "wk0", "wk1", "wk2", "wk3", "wk4", "wk5", "wi2", "wi3",
             "wq0", "wq1", "wq2", "wq3", "wq4", "wq5",
             "xo0", "xo1", "xo2", "xo3", "xo4", "xo5", "xo6", "xo7")


def U(*toks):
    out = {}
    for t in toks:
        if not t:
            continue
        for s, v in t.items():
            if out.get(s, -1) < v:
                out[s] = v
    return out


class Prog:
    def __init__(self, nc):
        self.nc = nc
        self.ops = {e: [] for e in ENGINES}
        self.sems = {}
        self.cnt = {s: 0 for s in SEM_NAMES}
        self.waited = {e: {} for e in ENGINES}

    def op(self, eng, fn, waits=None, inc=1, dma=False, sem=None):
        semname = (sem or eng + "_dma") if dma else eng
        need = []
        wt = self.waited[eng]
        if SELF_WAIT and not dma and eng in ("dve", "act", "pool"):
            prev = self.cnt[eng]
            if prev and wt.get(eng, 0) < prev:
                need.append((eng, prev))
                wt[eng] = prev
        if waits:
            for s, v in waits.items():
                if s.endswith("_dma"):
                    v = self.cnt[s]
                if wt.get(s, 0) >= v:
                    continue
                need.append((s, v))
                wt[s] = v
        step = (16 if dma else inc) if inc else 0
        if step:
            self.cnt[semname] += step
        target = self.cnt[semname]

        def emit(engine, _need=tuple(need), _fn=fn, _step=step, _sem=semname):
            for s, v in _need:
                engine.wait_ge(self.sems[s], v)
            instr = _fn(engine)
            if _step:
                instr.then_inc(self.sems[_sem], _step)
        self.ops[eng].append(emit)
        return {semname: target} if step else None

    def dma(self, eng, out, in_, waits=None, sem=None):
        return self.op(eng, lambda E, o=out, i=in_: E.dma_start(out=o, in_=i),
                       waits=waits, dma=True, sem=sem)

    def wait_only(self, eng, waits):
        need = []
        wt = self.waited[eng]
        for s, v in waits.items():
            if wt.get(s, 0) >= v:
                continue
            need.append((s, v))
            wt[s] = v

        def emit(engine, _need=tuple(need)):
            for s, v in _need:
                engine.wait_ge(self.sems[s], v)
        self.ops[eng].append(emit)

    def finalize(self, block):
        for e in ENGINES:
            fns = self.ops[e]
            if not fns:
                continue

            def body(engine, _fns=tuple(fns)):
                for f in _fns:
                    f(engine)
            getattr(block, BLOCK_ATTR[e])(body)


def _xform(W):
    return np.ascontiguousarray(W.reshape(16, 128, 16, 128).transpose(0, 3, 2, 1))


def _xform_de(Wde):
    ecn = Wde.shape[1] // 128
    return np.ascontiguousarray(
        Wde.reshape(16, 128, ecn, 128).transpose(2, 1, 0, 3))


def _rhsform(Wde):
    return np.ascontiguousarray(Wde.reshape(-1, 128, Wde.shape[1]))


def _pvec(v):
    return np.ascontiguousarray(np.asarray(v).reshape(-1, 128).T)


def stage_host(inputs):
    g = {}
    perm = np.concatenate([np.arange(0, 1024, 2), np.arange(1, 1024, 2),
                           np.arange(1024, 2048)])
    g["WqX"] = _xform(np.asarray(inputs["L0_attn_Wq"])).astype(BF)
    g["WkX"] = _xform(np.asarray(inputs["L0_attn_Wk"])).astype(BF)
    g["WvX"] = _xform(np.asarray(inputs["L0_attn_Wv"])).astype(BF)
    g["WoR"] = _rhsform(np.ascontiguousarray(
        np.asarray(inputs["L0_attn_Wo"]).T)).astype(BF)
    g["bq"] = _pvec(inputs["L0_attn_bq"])
    g["bk"] = _pvec(inputs["L0_attn_bk"])
    g["bv"] = _pvec(inputs["L0_attn_bv"])
    g["bo_r"] = np.ascontiguousarray(np.asarray(inputs["L0_attn_bo"])[None, :])
    w1 = np.asarray(inputs["L0_ffn_w1"])
    w2 = np.asarray(inputs["L0_ffn_w2"])
    g["W1X"] = np.ascontiguousarray(
        w1.reshape(64, 128, 16, 128).transpose(0, 3, 2, 1))
    g["W2R"] = np.ascontiguousarray(w2.T.reshape(64, 128, 2048))
    g["b1t"] = _pvec(inputs["L0_ffn_b1"])
    g["b2_r"] = np.ascontiguousarray(np.asarray(inputs["L0_ffn_b2"])[None, :])
    Win = np.asarray(inputs["L1_attn_W_in"])
    bin_ = np.asarray(inputs["L1_attn_b_in"])
    g["WinQX"] = _xform_de(np.ascontiguousarray(Win[:, 0:2048][:, perm])).astype(BF)
    g["WinKX"] = _xform_de(np.ascontiguousarray(Win[:, 2048:4096][:, perm])).astype(BF)
    g["WinVX"] = _xform_de(np.ascontiguousarray(Win[:, 4096:6144][:, perm])).astype(BF)
    g["binQ"] = _pvec(np.ascontiguousarray(bin_[0:2048][perm]))
    g["binK"] = _pvec(np.ascontiguousarray(bin_[2048:4096][perm]))
    g["binV"] = _pvec(np.ascontiguousarray(bin_[4096:6144][perm]))
    g["WoutR"] = _rhsform(np.ascontiguousarray(
        np.asarray(inputs["L1_attn_W_out"])[perm, :])).astype(BF)
    g["bout_r"] = np.ascontiguousarray(np.asarray(inputs["L1_attn_b_out"])[None, :])
    g["gam"] = _pvec(inputs["L1_attn_gamma"])
    g["bet"] = _pvec(inputs["L1_attn_beta"])
    rw = np.asarray(inputs["L1_ffn_router_w"])
    g["RtX"] = np.ascontiguousarray(
        rw.T.reshape(16, 128, 8).transpose(1, 0, 2)).astype(BF)
    dn = np.asarray(inputs["L1_ffn_experts_down"])
    dall = dn.transpose(1, 0, 2).reshape(2048, 128)
    g["DnX"] = np.ascontiguousarray(
        dall.reshape(16, 128, 128).transpose(1, 0, 2)).astype(BF)
    g["UpA"] = np.ascontiguousarray(
        np.asarray(inputs["L1_ffn_experts_up"]).reshape(128, 2048)).astype(BF)
    g["ident"] = np.eye(128, dtype=np.float32)
    esel = np.zeros((8, 128), dtype=np.float32)
    for ei in range(8):
        esel[ei, ei * 16:(ei + 1) * 16] = 1.0
    g["esel"] = esel.astype(BF)
    g["mask1"] = np.ones((8, 1), dtype=np.float32)
    return g


def stage_core(inputs, cid, g=None):
    c = {}
    x = np.asarray(inputs["L0_attn_x"]).reshape(8192, 2048)
    c["xin"] = np.ascontiguousarray(x[cid * TOK:(cid + 1) * TOK])
    if g is None:
        w1 = np.asarray(inputs["L0_ffn_w1"])
        w2 = np.asarray(inputs["L0_ffn_w2"])
        W1X = np.ascontiguousarray(
            w1.reshape(64, 128, 16, 128).transpose(0, 3, 2, 1))
        W2R = np.ascontiguousarray(w2.T.reshape(64, 128, 2048))
    else:
        W1X, W2R = g["W1X"], g["W2R"]
    gn = 16 * 131072
    c["W1M"] = np.ascontiguousarray(
        W1X.reshape(-1)[cid * gn:(cid + 1) * gn].reshape(16, 128, 1024))
    c["W2M"] = np.ascontiguousarray(
        W2R.reshape(-1)[cid * gn:(cid + 1) * gn].reshape(16, 128, 1024))
    inv = 1.0 / (10000.0 ** (np.arange(0, 1024, 2, dtype=np.float64) / 1024.0))
    pos = np.arange(cid * TOK, (cid + 1) * TOK, dtype=np.float64)
    ang = np.outer(inv, pos)
    c["ropec"] = np.ascontiguousarray(
        np.cos(ang).astype(np.float32).astype(BF)
        .reshape(4, 128, 1024).transpose(1, 0, 2))
    c["ropes"] = np.ascontiguousarray(
        np.sin(ang).astype(np.float32).astype(BF)
        .reshape(4, 128, 1024).transpose(1, 0, 2))
    mp = np.zeros((8, 1), dtype=np.float32)
    mp[:cid] = 1.0
    c["maskp"] = mp
    return c


KEYS_L0 = ["xin", "WqX", "WkX", "WvX", "WoR", "bq", "bk", "bv", "bo_r",
           "W1X", "W2R", "W1M", "W2M", "b1t", "b2_r", "maskp", "ident"]
KEYS_L1 = ["xin", "WinQX", "WinKX", "WinVX", "WoutR", "binQ", "binK", "binV",
           "bout_r", "gam", "bet", "RtX", "DnX", "UpA", "ropec", "ropes",
           "mask1", "ident", "esel"]
KEYS_ALL = sorted(set(KEYS_L0) | set(KEYS_L1))


class Ctx:
    pass


def small_alloc():
    m = {}
    off = SMALL_OFF

    def take(name, n):
        nonlocal off
        m[name] = off
        off += n
    take("stat", 16)
    take("bq", 16); take("bk", 16); take("bv", 16)
    take("binQ", 16); take("binK", 16); take("binV", 16)
    take("b1t", 64)
    take("gam", 16); take("bet", 16)
    take("carry", 18)
    take("offs", 18)
    take("Apart", 32)
    take("Afin", 16)
    take("thr", 4)
    take("ltok", 64)
    take("wful", 64)
    take("tiny", 16)
    take("e8", 8)
    take("ones_r", 128)
    take("invs_r", 128)
    take("ident", 128)
    take("esel", 64)
    take("mask", 4)
    take("onesb", 2)
    take("mean", 4)
    take("half", 2)
    take("lnb", 56)
    assert off <= SLAB_COLS, off
    return m


def f32v(C, off, n, p0=0, p=128):
    return C.slab[p0:p0 + p, off:off + n]


def bf16v(C, off, nbf):
    return C.slab[:, off:off + nbf // 2].bitcast(BF16)


def sm1(C, name, idx=0):
    o = C.sm[name] + idx
    return C.slab[:, o:o + 1]


def bank_read(C, b, tok):
    C.bank_free[b] = U(C.bank_free[b], tok)


def mm_group(C, b, mms, waits=None, out_slice=None):
    P = C.P
    n = len(mms)
    w0 = U(C.bank_free[b], waits)
    tok = None
    for i, (lh, rh) in enumerate(mms):
        tok = P.op("pe",
                   (lambda E, o=out_slice, l=lh, r=rh, st=(i == 0),
                    sp=(i == n - 1):
                    E.matmul(out=o, lhsT=l, rhs=r, start=st, stop=sp)),
                   waits=(w0 if i == 0 else None),
                   inc=(1 if i == n - 1 else 0))
    C.bank_data[b] = tok
    C.bank_free[b] = None
    return tok


def ones_rr(C):
    return f32v(C, C.sm["ones_r"], 128, p=1)


def build_nc(part):
    nc = bass.Bass("TRN2", target_bir_lowering=False, debug=False,
                   detect_race_conditions=RACE_DETECT)
    C = Ctx()
    C.part = part
    io = {}

    def din(name, shape, dt=F32):
        io[name] = nc.dram_tensor(name, list(shape), dt, kind="ExternalInput").ap()

    do_l0 = part in ("L0", "ALL")
    do_l1 = part in ("L1", "ALL")
    din("xin", [TOK, D])
    din("ident", [128, 128])
    if do_l0:
        din("WqX", [16, 128, 16, 128], BF16); din("WkX", [16, 128, 16, 128], BF16)
        din("WvX", [16, 128, 16, 128], BF16); din("WoR", [16, 128, 2048], BF16)
        din("bq", [128, 16]); din("bk", [128, 16]); din("bv", [128, 16])
        din("bo_r", [1, 2048])
        din("W1X", [64, 128, 16, 128]); din("W2R", [64, 128, 2048])
        din("W1M", [16, 128, 1024]); din("W2M", [16, 128, 1024])
        din("b1t", [128, 64]); din("b2_r", [1, 2048])
        din("maskp", [8, 1])
    if do_l1:
        din("WinQX", [16, 128, 16, 128], BF16); din("WinKX", [16, 128, 16, 128], BF16)
        din("WinVX", [16, 128, 16, 128], BF16); din("WoutR", [16, 128, 2048], BF16)
        din("binQ", [128, 16]); din("binK", [128, 16]); din("binV", [128, 16])
        din("bout_r", [1, 2048])
        din("gam", [128, 16]); din("bet", [128, 16])
        din("RtX", [128, 16, 8], BF16); din("DnX", [128, 16, 128], BF16)
        din("UpA", [128, 2048], BF16)
        din("ropec", [128, 4, 1024], BF16); din("ropes", [128, 4, 1024], BF16)
        din("mask1", [8, 1]); din("esel", [8, 128], BF16)
    io["out"] = nc.dram_tensor("out", [TOK, D], F32, kind="ExternalOutput").ap()

    cc = {}
    if do_l0:
        cc["c1i"] = nc.dram_tensor("c1i", [128, 18], F32, kind="Internal")
        cc["c1o"] = nc.dram_tensor("c1o", [1024, 18], F32, kind="Internal")
    if do_l1:
        cc["c2i"] = nc.dram_tensor("c2i", [128, 16], F32, kind="Internal")
        cc["c2o"] = nc.dram_tensor("c2o", [1024, 16], F32, kind="Internal")

    with ExitStack() as ctx:
        e = ctx.enter_context
        slab = e(nc.sbuf_tensor("slab", [128, SLAB_COLS], F32))
        pb = [e(nc.psum_tensor("pb%d" % i, [128, 512], F32)) for i in range(8)]
        P = Prog(nc)
        for s in SEM_NAMES:
            P.sems[s] = e(nc.semaphore("sem_" + s))
        block = e(nc.Block())
        C.nc, C.P, C.slab, C.pb, C.io, C.cc = nc, P, slab, pb, io, cc
        C.sm = small_alloc()
        C.bank_data = [None] * 8
        C.bank_free = [None] * 8
        C.wb = [bf16v(C, WBF_OFF, 2048), bf16v(C, WBF_OFF + 1024, 2048)]
        C.wb_busy = [None, None]
        build_body(C)
        P.finalize(block)
    nc.finalize()
    return nc


def build_body(C):
    P, io, sm = C.P, C.io, C.sm
    do_l0 = C.part in ("L0", "ALL")
    do_l1 = C.part in ("L1", "ALL")

    xin3 = io["xin"].rearrange("(tc p) e -> tc p e", p=128)
    t_xc = []
    for tc8 in range(TC):
        t_xc.append(P.dma("sp", f32v(C, X_OFF + tc8 * 2048, 2048),
                          xin3[tc8], sem="xo%d" % tc8))
    t_x = U(*t_xc)
    C.t_xc = t_xc
    consts = [P.dma("act", f32v(C, sm["ident"], 128), io["ident"], sem="cst")]
    if do_l0:
        for nm in ("bq", "bk", "bv"):
            consts.append(P.dma("act", f32v(C, sm[nm], 16), io[nm], sem="cst"))
        consts.append(P.dma("act", f32v(C, sm["b1t"], 64), io["b1t"], sem="cst"))
        consts.append(P.dma("act", f32v(C, sm["mask"], 1, p=8), io["maskp"], sem="cst"))
    if do_l1:
        for nm in ("binQ", "binK", "binV", "gam", "bet"):
            consts.append(P.dma("act", f32v(C, sm[nm], 16), io[nm], sem="cst"))
        consts.append(P.dma("act", C.slab[0:8, sm["esel"]:sm["esel"] + 64].bitcast(BF16), io["esel"], sem="cst"))
        consts.append(P.dma("act", f32v(C, sm["mask"] + 1, 1, p=8), io["mask1"], sem="cst"))
    consts.append(P.op("dve", lambda E: E.memset(f32v(C, sm["ones_r"], 128, p=1), 1.0)))
    consts.append(P.op("dve", lambda E: E.memset(f32v(C, sm["invs_r"], 128, p=1),
                                                 INVSCALE)))
    consts.append(P.op("dve", lambda E: E.memset(
        C.slab[:, sm["onesb"]:sm["onesb"] + 1].bitcast(BF16), 1.0)))
    consts.append(P.op("dve", lambda E: E.memset(
        C.slab[:, sm["onesb"] + 1:sm["onesb"] + 2], 1.0)))
    consts.append(P.op("dve", lambda E: E.memset(f32v(C, sm["carry"], 16), 0.0)))
    consts.append(P.op("dve", lambda E: E.memset(sm1(C, "half", 0), -0.5)))
    consts.append(P.op("dve", lambda E: E.memset(sm1(C, "half", 1), 0.5)))
    C.t_consts = U(*consts)
    C.t_x = t_x

    if do_l0:
        layer0(C)
    if do_l1:
        layer1(C)

    out3 = io["out"].rearrange("(tc p) e -> tc p e", p=128)
    xdc = getattr(C, "t_xdone_c", None)
    touts = []
    for tc8 in range(TC):
        touts.append(P.dma("sp", out3[tc8], f32v(C, X_OFF + tc8 * 2048, 2048),
                           waits=(xdc[tc8] if xdc else C.t_xdone),
                           sem="xo%d" % tc8))
    P.wait_only("sp", U(*touts))


def ln_stats(C, x_chunk, slot_negmu, slot_rstd, sq_scr, waits):
    P = C.P
    s1 = sm1(C, "tiny", 0)
    s2 = sm1(C, "tiny", 1)
    t1 = P.op("dve", (lambda E, o=s1, i=x_chunk:
                      E.reduce_sum(out=o, in_=i, axis=AX.X)), waits=waits)
    t2 = P.op("act", (lambda E, o=sq_scr, i=x_chunk, a=s2:
                      E.activation(out=o, in_=i, func=AF.Square, accum_out=a)),
              waits=waits)
    mu = sm1(C, "tiny", 2)
    t3 = P.op("dve", (lambda E, o=mu, i=s1:
                      E.tensor_scalar(out=o, in0=i, scalar1=1.0 / D, scalar2=None,
                                      op0=ALU.mult)), waits=t1)
    m2 = sm1(C, "tiny", 3)
    t4 = P.op("dve", (lambda E, o=m2, i=mu: E.tensor_mul(out=o, in0=i, in1=i)),
              waits=t3)
    var = sm1(C, "tiny", 4)
    t5 = P.op("dve", (lambda E, o=var, i=s2:
                      E.tensor_scalar(out=o, in0=i, scalar1=1.0 / D, scalar2=None,
                                      op0=ALU.mult)), waits=t2)
    t6 = P.op("dve", (lambda E, o=var, i=m2:
                      E.tensor_sub(out=o, in0=o, in1=i)), waits=U(t4, t5))
    t7 = P.op("dve", (lambda E, o=var:
                      E.tensor_scalar(out=o, in0=o, scalar1=EPS_LN, scalar2=None,
                                      op0=ALU.add)), waits=t6)
    sd = sm1(C, "tiny", 5)
    t8 = P.op("act", (lambda E, o=sd, i=var:
                      E.activation(out=o, in_=i, func=AF.Sqrt)), waits=t7)
    t9 = P.op("dve", (lambda E, o=slot_rstd, i=sd: E.reciprocal(out=o, in_=i)),
              waits=t8)
    t10 = P.op("dve", (lambda E, o=slot_negmu, i=mu:
                       E.tensor_scalar(out=o, in0=i, scalar1=-1.0, scalar2=None,
                                       op0=ALU.mult)), waits=t3)
    return U(t9, t10)



def ln_batch(C, waits, per_chunk=None):
    P, sm = C.P, C.sm
    lb = sm["lnb"]
    s1 = C.slab[:, lb:lb + 8]
    s2 = C.slab[:, lb + 8:lb + 16]
    negmu = C.slab[:, lb + 16:lb + 24]
    rstd = C.slab[:, lb + 24:lb + 32]
    tmp = C.slab[:, lb + 32:lb + 40]
    dump = f32v(C, SC_OFF, 2048)
    toks = []
    prev_sq = None
    for tc in range(TC):
        xc = f32v(C, X_OFF + tc * 2048, 2048)
        wc = U(waits, per_chunk[tc] if per_chunk else None)
        t1 = P.op("dve", (lambda E, o=s1[:, tc:tc + 1], i=xc:
                          E.reduce_sum(out=o, in_=i, axis=AX.X)), waits=wc)
        t2 = P.op("act", (lambda E, o=dump, i=xc, a=s2[:, tc:tc + 1]:
                          E.activation(out=o, in_=i, func=AF.Square,
                                       accum_out=a)),
                  waits=U(wc, prev_sq))
        prev_sq = t2
        toks.append(U(t1, t2))
    t0 = U(*toks)
    ta = P.op("dve", (lambda E, o=negmu, i=s1:
                      E.tensor_scalar(out=o, in0=i, scalar1=-1.0 / D,
                                      scalar2=None, op0=ALU.mult)), waits=t0)
    tb = P.op("dve", (lambda E, o=tmp, i=negmu:
                      E.tensor_mul(out=o, in0=i, in1=i)), waits=ta)
    tc_ = P.op("dve", (lambda E, o=rstd, i=s2:
                       E.tensor_scalar(out=o, in0=i, scalar1=1.0 / D,
                                       scalar2=None, op0=ALU.mult)), waits=t0)
    td = P.op("dve", (lambda E, o=rstd, i=tmp:
                      E.tensor_sub(out=o, in0=o, in1=i)), waits=U(tb, tc_))
    te = P.op("dve", (lambda E, o=rstd:
                      E.tensor_scalar(out=o, in0=o, scalar1=EPS_LN, scalar2=None,
                                      op0=ALU.add)), waits=td)
    tf = P.op("act", (lambda E, o=rstd:
                      E.activation(out=o, in_=o, func=AF.Sqrt)), waits=te)
    tg = P.op("dve", (lambda E, o=rstd: E.reciprocal(out=o, in_=o)), waits=tf)
    return U(tg, prev_sq)


def transpose_chunk(C, src_f32, dst_bf_off, tc, waits, evac=None):
    P = C.P
    toks = []
    ident = f32v(C, C.sm["ident"], 128)
    dst = bf16v(C, dst_bf_off, 16384).rearrange("p (dc t) -> p dc t", t=1024)
    for grp in range(4):
        b = grp % 2
        w0 = U(C.bank_free[b], waits, C.t_consts)
        tmm = None
        for j in range(4):
            dc = grp * 4 + j
            tmm = P.op("pe",
                       (lambda E, o=C.pb[b][:, j * 128:(j + 1) * 128],
                        i=src_f32[:, dc * 128:(dc + 1) * 128], idn=ident:
                        E.transpose(out=o, in_=i, identity=idn)),
                       waits=(w0 if j == 0 else None), inc=(1 if j == 3 else 0))
        C.bank_data[b] = tmm
        if evac is None:
            dst_ap = dst[:, grp * 4:grp * 4 + 4, tc * 128:(tc + 1) * 128]
            src_ap = C.pb[b][:, 0:512].rearrange("p (dc t) -> p dc t", t=128)
            tev = P.op("dve", (lambda E, o=dst_ap, i=src_ap:
                               E.tensor_copy(out=o, in_=i)), waits=tmm)
        else:
            tes = []
            for j in range(4):
                dc = grp * 4 + j
                s_c, b_c = evac[dc]
                tes.append(P.op("act",
                    (lambda E, o=dst[:, dc, tc * 128:(tc + 1) * 128],
                     i=C.pb[b][:, j * 128:(j + 1) * 128], s_=s_c, bb=b_c:
                     E.activation(out=o, in_=i, func=AF.Identity,
                                  scale=s_, bias=bb)),
                    waits=(U(tmm, C.t_consts) if j == 0 else None)))
            tev = U(*tes)
        C.bank_free[b] = tev
        toks.append(tev)
    return U(*toks)


def layer0(C):
    P, io, sm = C.P, C.io, C.sm
    sc = f32v(C, SC_OFF, 2048)
    xnf = bf16v(C, XNF_OFF, 16384)
    wb, wb_busy = C.wb, C.wb_busy
    tst = ln_batch(C, C.t_consts, per_chunk=C.t_xc)
    negmu8 = C.slab[:, sm["lnb"] + 16:sm["lnb"] + 24]
    rstd8 = C.slab[:, sm["lnb"] + 24:sm["lnb"] + 32]
    nb8 = C.slab[:, sm["lnb"] + 48:sm["lnb"] + 56]
    tnb = P.op("dve", (lambda E, o=nb8, i=negmu8, j=rstd8:
                       E.tensor_mul(out=o, in0=i, in1=j)), waits=tst)
    xnf_toks = []
    for tc in range(TC):
        xc = f32v(C, X_OFF + tc * 2048, 2048)
        buf = sc if tc % 2 == 0 else f32v(C, SP2_OFF, 2048)
        tap = P.op("act", (lambda E, o=buf, i=xc,
                           a=rstd8[:, tc:tc + 1], b=nb8[:, tc:tc + 1]:
                           E.activation(out=o, in_=i, func=AF.Identity,
                                        scale=a, bias=b)),
                   waits=U(tnb, xnf_toks[tc - 2] if tc >= 2 else None))
        xnf_toks.append(transpose_chunk(C, buf, XNF_OFF, tc, tap))
    C.t_xnf = U(*xnf_toks)

    wfp = [f32v(C, ROWS_OFF, 1024), f32v(C, ROWS_OFF + 1024, 1024),
           f32v(C, WF32_OFF, 1024)]
    wfp_busy = [None, None, None]
    C.wfp, C.wfp_busy = wfp, wfp_busy
    NGM = 16
    mean_toks = []
    absdump = f32v(C, SLAB2_OFF + 512, 1024)
    w1run = C.slab[:, sm["carry"] + 16:sm["carry"] + 17]
    w2run = C.slab[:, sm["carry"] + 17:sm["carry"] + 18]
    wtmp = [sm1(C, "tiny", 11), sm1(C, "tiny", 12)]
    t_wz = P.op("dve", lambda E: E.memset(
        C.slab[:, sm["carry"] + 16:sm["carry"] + 18], 0.0))
    mean_state = {"i": 0, "p": 0, "pa": None,
                  "prun": [t_wz, t_wz], "pend": []}

    def pump_load(n):
        for _ in range(n):
            i = mean_state["i"]
            if i >= 2 * NGM:
                return
            mean_state["i"] += 1
            p = mean_state["p"]
            mean_state["p"] = (p + 1) % 3
            mi, gidx = (0, i) if i < NGM else (1, i - NGM)
            src = io["W1M"] if mi == 0 else io["W2M"]
            td = P.dma("sp", wfp[p], src[gidx], waits=wfp_busy[p],
                       sem="wfp%d" % p)
            mean_state["pend"].append((p, td, mi, i))

    def pump_drain():
        for (p, td, mi, i) in mean_state["pend"]:
            tb = i % 2
            th = P.op("act", (lambda E, i_=wfp[p], o=absdump, a=wtmp[tb]:
                              E.activation(out=o, in_=i_, func=AF.Abs,
                                           accum_out=a)),
                      waits=U(td, mean_state["pa"], mean_state["prun"][tb]))
            mean_state["pa"] = th
            run = w1run if mi == 0 else w2run
            tr2 = P.op("dve", (lambda E, o=run, i_=wtmp[tb]:
                               E.tensor_add(out=o, in0=o, in1=i_)),
                       waits=U(th, *mean_state["prun"]))
            mean_state["prun"][tb] = tr2
            wfp_busy[p] = th
            mean_toks.append(tr2)
        mean_state["pend"] = []

    zeros = f32v(C, SLAB2_OFF, 512)
    tz = P.op("dve", lambda E: E.memset(zeros, 0.0))
    kvv = f32v(C, KV_OFF, 16384)
    rslots = [WBF_OFF, WBF_OFF + 512, WBF_OFF + 1024, WBF_OFF + 1536,
              SLAB2_OFF + 2816, SLAB2_OFF + 3328]
    rbusy = [None] * 6
    ridx = [0]

    def wslot():
        i = ridx[0] % 6
        ridx[0] += 1
        return i, bf16v(C, rslots[i], 1024), "wk%d" % i

    def wload(WX, ec):
        views = []
        for hh in range(2):
            i, vw, sname = wslot()
            td = P.dma("pool", vw,
                       io[WX][ec, :, hh * 8:(hh + 1) * 8, :].rearrange(
                           "p dc f -> p (dc f)"),
                       waits=rbusy[i], sem=sname)
            views.append((i, vw, td))
        return views

    def wtiles(views):
        mms = []
        tds = []
        for hh, (i, vw, td) in enumerate(views):
            tds.append(td)
            for dcl in range(8):
                mms.append(vw[:, dcl * 128:(dcl + 1) * 128])
        return mms, U(*tds)

    gi = 0
    scan_prev = {}
    t_scan_all = []
    sc_reads = []
    prev_exp = None
    prev_t4 = None
    prev_tu = [None, None]
    for ec in range(16):
        pump_load(2)
        kviews = wload("WkX", ec)
        vviews = wload("WvX", ec)
        ktiles, tdk = wtiles(kviews)
        vtiles, tdv = wtiles(vviews)
        phi = [None, None]
        k_reads = []
        k_mms = []
        exps = []
        for s in range(2):
            bk_k = 2 + 2 * s
            mms = [(ktiles[dc],
                    xnf[:, dc * 1024 + s * 512: dc * 1024 + s * 512 + 512])
                   for dc in range(16)]
            tk = mm_group(C, bk_k, mms, waits=U(tdk, C.t_xnf),
                          out_slice=C.pb[bk_k][:, 0:512])
            k_mms.append(tk)
            bkc = sm1(C, "bk", ec)
            mslot = sc[:, s * 2048 // 8: s * 256 + 256] if False else                 sc[:, 0:512] if s == 0 else sc[:, 512:1024]
            eslot = sc[:, 1536 + 0:1536 + 256].bitcast(BF16) if False else None
            eslot = (sc[:, 0:512], sc[:, 512:1024])[s]
            t1 = P.op("dve", (lambda E, o=mslot, i=C.pb[bk_k][:, 0:512], a=bkc:
                              E.tensor_scalar(out=o, in0=i, scalar1=a, scalar2=0.0,
                                              op0=ALU.add, op1=ALU.min)),
                      waits=U(tk, C.t_consts, prev_exp, prev_t4))
            t2 = P.op("act", (lambda E, o=eslot, i=mslot:
                              E.activation(out=o, in_=i, func=AF.Exp)),
                      waits=U(t1,))
            prev_exp = t2
            exps.append((tk, t2))
        for s in range(2):
            bk_k = 2 + 2 * s
            tk, t2 = exps[s]
            bkc = sm1(C, "bk", ec)
            eslot = (sc[:, 0:512], sc[:, 512:1024])[s]
            phis = sc[:, 1024 + s * 512: 1536 + s * 512]
            t3 = P.op("dve", (lambda E, o=phis, i=C.pb[bk_k][:, 0:512], a=bkc:
                              E.tensor_scalar(out=o, in0=i, scalar1=a, scalar2=0.0,
                                              op0=ALU.add, op1=ALU.max)),
                      waits=U(tk, prev_tu[s]))
            t4 = P.op("dve", (lambda E, o=phis, i=eslot:
                              E.tensor_add(out=o, in0=o, in1=i)), waits=U(t2, t3))
            prev_t4 = t4
            bank_read(C, bk_k, U(t3, t4))
            phi[s] = (phis, t4)
            k_reads.append(t4)
        for (i, vw, td) in kviews:
            rbusy[i] = U(*k_mms)
        v_reads = []
        for s in range(2):
            bk_v = 3 + 2 * s
            mms = [(vtiles[dc],
                    xnf[:, dc * 1024 + s * 512: dc * 1024 + s * 512 + 512])
                   for dc in range(16)]
            tv = mm_group(C, bk_v, mms, waits=U(tdv, C.t_xnf),
                          out_slice=C.pb[bk_v][:, 0:512])
            v_reads.append(tv)
            kvsl = kvv[:, ec * 1024 + s * 512: ec * 1024 + s * 512 + 512]
            bvc = sm1(C, "bv", ec)
            phis, tphi = phi[s]
            tu = P.op("dve", (lambda E, o=kvsl, i=C.pb[bk_v][:, 0:512], a=bvc,
                              h=phis:
                              E.scalar_tensor_tensor(out=o, in0=i, scalar=a, in1=h,
                                                     op0=ALU.add, op1=ALU.mult)),
                      waits=U(tv, tphi, C.t_consts))
            prev_tu[s] = tu
            bank_read(C, bk_v, tu)
            sc_reads.append(tu)
            csl = C.slab[:, sm["carry"] + ec: sm["carry"] + ec + 1]
            tsc = P.op("dve", (lambda E, o=kvsl, z=zeros, ini=csl:
                               E.tensor_tensor_scan(out=o, data0=o, data1=z,
                                                    initial=ini, op0=ALU.add,
                                                    op1=ALU.add)),
                       waits=U(tu, tz, C.t_consts, scan_prev.get(ec)))
            tcar = P.op("dve", (lambda E, o=csl,
                                i=kvv[:, ec * 1024 + s * 512 + 511:
                                      ec * 1024 + s * 512 + 512]:
                                E.tensor_copy(out=o, in_=i)), waits=tsc)
            scan_prev[ec] = tcar
            t_scan_all.append(tcar)
        for (i, vw, td) in vviews:
            rbusy[i] = U(*v_reads)
        pump_drain()
    C.t_scan = U(*t_scan_all)
    C.t_sc_attn = U(*sc_reads)
    t_meandone = U(*mean_toks)

    carry = f32v(C, sm["carry"], 18)
    c1i, c1o = C.cc["c1i"], C.cc["c1o"]
    td = P.dma("pool", c1i.ap(), carry, waits=U(C.t_scan, t_meandone),
               sem="cci")
    if SIM_NOCC:
        tl = []
        for r in range(8):
            tl.append(P.dma("pool", c1o.ap()[r * 128:(r + 1) * 128, :],
                            c1i.ap(), waits=td, sem="agl"))
        C.t_cc1 = U(*tl)
    else:
        C.t_cc1 = P.op("pool", (lambda E: E.collective_compute(
            "AllGather", ALU.bypass, replica_groups=RG,
            ins=[c1i.ap().opt()], outs=[c1o.ap().opt()])), waits=td)

    s_ag = f32v(C, SC_OFF, 2304, p=8)
    tld = P.dma("pool", s_ag, c1o.ap().rearrange("(r p) c -> r (p c)", p=128),
                waits=U(C.t_cc1, C.t_sc_attn), sem="agl")
    maskp = C.slab[0:8, sm["mask"]:sm["mask"] + 1]
    ones8 = C.slab[0:8, sm["onesb"] + 1:sm["onesb"] + 2]
    w0 = U(tld, C.bank_free[6], C.t_consts)
    tmo = None
    s_ag3 = s_ag.rearrange("r (p c) -> r p c", c=18)
    for cch in range(18):
        rr = maskp if cch < 16 else ones8
        tmo = P.op("pe",
                   (lambda E, o=C.pb[6][:, cch:cch + 1],
                    l=s_ag3[:, :, cch], r=rr:
                    E.matmul(out=o, lhsT=l, rhs=r, start=True, stop=True)),
                   waits=(w0 if cch == 0 else None), inc=(1 if cch == 17 else 0))
    C.bank_data[6] = tmo
    offs = f32v(C, sm["offs"], 18)
    tof = P.op("dve", (lambda E, o=offs, i=C.pb[6][:, 0:18]:
                       E.tensor_copy(out=o, in_=i)), waits=tmo)
    C.bank_free[6] = tof
    C.t_offs = tof
    C.t_agread = U(tof, tld)
    wfp_busy[2] = U(wfp_busy[2], C.t_agread)

    onesc = C.slab[:, sm["onesb"] + 1:sm["onesb"] + 2]
    w1vec = C.slab[:, sm["offs"] + 16:sm["offs"] + 17]
    w2vec = C.slab[:, sm["offs"] + 17:sm["offs"] + 18]
    w0 = U(C.bank_free[6], C.t_consts)
    tmm = P.op("pe", (lambda E, o=C.pb[6][0:1, 0:1], l=w1vec, r=onesc:
                      E.matmul(out=o, lhsT=l, rhs=r, start=True, stop=True)),
               waits=w0)
    C.bank_data[6] = tmm
    mean_s = C.slab[0:1, sm["mean"]:sm["mean"] + 1]
    t1m = P.op("dve", (lambda E, o=mean_s, i=C.pb[6][0:1, 0:1]:
                       E.tensor_scalar(out=o, in0=i, scalar1=1.0 / (DFF * D),
                                       scalar2=EPS_BIT, op0=ALU.mult,
                                       op1=ALU.max)), waits=tmm)
    C.bank_free[6] = t1m
    t2m = P.op("dve", (lambda E, o=mean_s:
                       E.tensor_scalar(out=o, in0=o, scalar1=0.5, scalar2=None,
                                       op0=ALU.mult)), waits=t1m)
    w0 = U(C.bank_free[7], t2m, C.t_consts)
    tbc = P.op("pe", (lambda E, o=C.pb[7][:, 0:1], l=ones_rr(C), r=mean_s:
                      E.matmul(out=o, lhsT=l, rhs=r, start=True, stop=True)),
               waits=w0)
    C.bank_data[7] = tbc
    thrp1 = C.slab[:, sm["thr"]:sm["thr"] + 1]
    thrn1 = C.slab[:, sm["thr"] + 1:sm["thr"] + 2]
    t3m = P.op("dve", (lambda E, o=thrp1, i=C.pb[7][:, 0:1]:
                       E.tensor_copy(out=o, in_=i)), waits=tbc)
    C.bank_free[7] = t3m
    t4m = P.op("dve", (lambda E, o=thrn1, i=thrp1:
                       E.tensor_scalar(out=o, in0=i, scalar1=-1.0, scalar2=None,
                                       op0=ALU.mult)), waits=t3m)
    C.t_thr1 = U(t3m, t4m)

    w0 = U(C.bank_free[6], C.t_consts)
    tmm2 = P.op("pe", (lambda E, o=C.pb[6][0:1, 0:1], l=w2vec, r=onesc:
                       E.matmul(out=o, lhsT=l, rhs=r, start=True, stop=True)),
                waits=w0)
    C.bank_data[6] = tmm2
    mean2_s = C.slab[0:1, sm["mean"] + 1: sm["mean"] + 2]
    t1i = P.op("dve", (lambda E, o=mean2_s, i=C.pb[6][0:1, 0:1]:
                       E.tensor_scalar(out=o, in0=i, scalar1=1.0 / (DFF * D),
                                       scalar2=EPS_BIT, op0=ALU.mult,
                                       op1=ALU.max)), waits=tmm2)
    C.bank_free[6] = t1i
    t2i = P.op("dve", (lambda E, o=mean2_s:
                       E.reciprocal(out=o, in_=o)), waits=t1i)
    w0 = U(C.bank_free[7], t2i, C.t_consts)
    tbc2 = P.op("pe", (lambda E, o=C.pb[7][:, 0:1], l=ones_rr(C), r=mean2_s:
                       E.matmul(out=o, lhsT=l, rhs=r, start=True, stop=True)),
                waits=w0)
    C.bank_data[7] = tbc2
    inv2s = C.slab[:, sm["thr"] + 2:sm["thr"] + 3]
    t3i = P.op("dve", (lambda E, o=inv2s, i=C.pb[7][:, 0:1]:
                       E.tensor_copy(out=o, in_=i)), waits=tbc2)
    C.bank_free[7] = t3i
    C.t_thr = U(C.t_thr1, t3i)

    rbusy = list(rbusy)
    ridx = [0]

    def ctx_tile(dc, tch):
        s, i0 = tch // 4, (tch % 4) * 64
        return bf16v(C, KV_OFF + dc * 1024 + s * 512 + i0, 128)
    prev_exp = None
    prev_t4q = None
    prev_t5 = None
    ctx_t = []
    for ec in range(16):
        qtiles = []
        tds = []
        for hh in range(2):
            i = ridx[0] % 6
            ridx[0] += 1
            vw = bf16v(C, rslots[i], 1024)
            tdg = P.dma("sp", vw,
                        io["WqX"][ec, :, hh * 8:(hh + 1) * 8, :].rearrange(
                            "p dc f -> p (dc f)"),
                        waits=rbusy[i], sem="wq%d" % i)
            tds.append((i, tdg))
            for dcl in range(8):
                qtiles.append(vw[:, dcl * 128:(dcl + 1) * 128])
        tdq = U(*[t for _, t in tds])
        q_mms = []
        for s in range(2):
            bq_b = (0, 1, 2, 3, 4, 5)[(2 * ec + s) % 6]
            mms = [(qtiles[dc],
                    xnf[:, dc * 1024 + s * 512: dc * 1024 + s * 512 + 512])
                   for dc in range(16)]
            tq = mm_group(C, bq_b, mms, waits=U(tdq, C.t_xnf),
                          out_slice=C.pb[bq_b][:, 0:512])
            q_mms.append(tq)
            bqc = sm1(C, "bq", ec)
            mslot = sc[:, 0:512]; eslot = sc[:, 512:1024]; phis = sc[:, 1024:1536]
            t1 = P.op("dve", (lambda E, o=mslot, i=C.pb[bq_b][:, 0:512], a=bqc:
                              E.tensor_scalar(out=o, in0=i, scalar1=a, scalar2=0.0,
                                              op0=ALU.add, op1=ALU.min)),
                      waits=U(tq, C.t_agread, prev_exp))
            t2 = P.op("act", (lambda E, o=eslot, i=mslot:
                              E.activation(out=o, in_=i, func=AF.Exp)),
                      waits=U(t1, prev_t4q))
            prev_exp = t2
            t3 = P.op("dve", (lambda E, o=phis, i=C.pb[bq_b][:, 0:512], a=bqc:
                              E.tensor_scalar(out=o, in0=i, scalar1=a, scalar2=0.0,
                                              op0=ALU.add, op1=ALU.max)),
                      waits=U(tq, prev_t5))
            t4 = P.op("dve", (lambda E, o=phis, i=eslot:
                              E.tensor_add(out=o, in0=o, in1=i)), waits=U(t2, t3))
            prev_t4q = t4
            bank_read(C, bq_b, U(t1, t3))
            kvsl = kvv[:, ec * 1024 + s * 512: ec * 1024 + s * 512 + 512]
            ctxsl = kvsl.bitcast(BF16)[:, 0:512]
            oc = C.slab[:, sm["offs"] + ec: sm["offs"] + ec + 1]
            t5 = P.op("dve", (lambda E, o=ctxsl, i=kvsl, a=oc, h=phis:
                              E.scalar_tensor_tensor(out=o, in0=i, scalar=a,
                                                     in1=h, op0=ALU.add,
                                                     op1=ALU.mult)),
                      waits=U(t4, C.t_offs))
            prev_t5 = t5
            ctx_t.append(t5)
        for i, _ in tds:
            rbusy[i] = U(*q_mms)
    t_ctx = U(*ctx_t)
    bo_seg = f32v(C, SLAB2_OFF, 512, p=1)
    invs = f32v(C, sm["invs_r"], 128, p=1)
    t_boseg = None
    t_boseg_rd = None
    wo_x = [None] * 8
    t_wo_last = []
    for eg in range(4):
        t_boseg = P.dma("act", bo_seg, io["bo_r"][:, eg * 512:(eg + 1) * 512],
                        waits=U(t_ctx, t_boseg_rd), sem="brow")
        for ecp in range(8):
            i = ridx[0] % 6
            ridx[0] += 1
            vw = bf16v(C, rslots[i], 1024).rearrange("p (d e) -> p d e", e=512)
            tdw = P.dma("sp", vw,
                        io["WoR"][ecp * 2:ecp * 2 + 2, :, eg * 512:(eg + 1) * 512]
                        .rearrange("d p e -> p d e"),
                        waits=rbusy[i], sem="wq%d" % i)
            last = None
            for dcl in range(2):
                dc = ecp * 2 + dcl
                for tch in range(8):
                    st = (dc == 0)
                    w = U(C.bank_free[tch] if st else None,
                          t_ctx if st else None, tdw)
                    last = P.op("pe",
                        (lambda E, o=C.pb[tch][:, 0:512],
                         l=ctx_tile(dc, tch),
                         r=vw[:, dcl, :], s_=st:
                         E.matmul(out=o, lhsT=l, rhs=r, start=s_, stop=False)),
                        waits=w, inc=(1 if tch == 7 else 0))
            rbusy[i] = last
        seg_rd = []
        for tch in range(8):
            tokmm = P.op("pe", (lambda E, o=C.pb[tch][:, 0:512], l=invs,
                                r=bo_seg:
                                E.matmul(out=o, lhsT=l, rhs=r, start=False,
                                         stop=True)),
                         waits=U(t_boseg, C.t_consts), inc=1)
            seg_rd.append(tokmm)
            C.bank_data[tch] = tokmm
            xsl = f32v(C, X_OFF + tch * 2048 + eg * 512, 512)
            tev = P.op("dve", (lambda E, o=xsl, i=C.pb[tch][:, 0:512]:
                               E.scalar_tensor_tensor(out=o, in0=i, scalar=SCALE,
                                                      in1=o, op0=ALU.mult,
                                                      op1=ALU.add)),
                       waits=U(tokmm, C.t_xnf, wo_x[tch]))
            C.bank_free[tch] = tev
            wo_x[tch] = tev
            t_wo_last.append(tev)
        t_boseg_rd = U(*seg_rd)
    t_wo_done = U(*t_wo_last)
    C.t_ffn_x = wo_x
    C.t_hfree0 = t_wo_done
    C.t_sc_q = U(t_ctx, t_wo_done)
    C.t_twos = P.op("dve", lambda E: E.memset(invs, 2.0), waits=t_wo_done)

    if DBG_STOP == "wo":
        C.t_xdone = t_wo_done
        C.t_rows_free = C.t_xdone
        C.t_kv_read = U(t_ctx, t_wo_done)
        return
    ffn(C)
    C.t_xdone = U(*C.t_ffn_x)
    if DBG_STOP == "ffn":
        pass
    C.t_rows_free = C.t_xdone
    C.t_kv_read = U(t_ctx, t_wo_done)


def ffn(C):
    P, io, sm = C.P, C.io, C.sm
    xnf = bf16v(C, XNF_OFF, 16384)
    wfp = list(C.wfp) + [f32v(C, KV_OFF + 3072, 1024),
                         f32v(C, KV_OFF + 4096, 1024)]
    wfp_busy = list(C.wfp_busy) + [C.t_sc_q, C.t_sc_q]
    wbX = [bf16v(C, KV_OFF + 5120, 2048), bf16v(C, KV_OFF + 6144, 2048)]
    wbX_busy = [C.t_sc_q, C.t_sc_q]
    f2w = [bf16v(C, WBF_OFF + i * 512, 1024) for i in range(4)] + \
          [bf16v(C, KV_OFF + 1024 + i * 512, 1024) for i in range(4)]
    f2w_busy = [U(C.wb_busy[0], C.t_sc_q), U(C.wb_busy[0], C.t_sc_q),
                U(C.wb_busy[1], C.t_sc_q), U(C.wb_busy[1], C.t_sc_q),
                C.t_sc_q, C.t_sc_q, C.t_sc_q, C.t_sc_q]
    thrp1 = C.slab[:, sm["thr"]:sm["thr"] + 1]
    thrn1 = C.slab[:, sm["thr"] + 1:sm["thr"] + 2]
    inv2 = C.slab[:, sm["thr"] + 2:sm["thr"] + 3]
    hslab = bf16v(C, SLAB2_OFF, 8192)
    bscrF = bf16v(C, KV_OFF + 512, 1024)
    sgn = [bf16v(C, SC_OFF + i * 512, 1024) for i in range(4)]
    sgn_busy = [C.t_sc_q] * 4
    b2seg = f32v(C, KV_OFF, 512, p=1)

    st_ = {"gfp": 0, "prev_tc": C.t_sc_q, "t_b2seg_rd": None,
           "t1": {}, "t2": {}}
    t_hfree = C.t_hfree0

    def wfp_slot():
        pf = st_["gfp"] % 5
        st_["gfp"] += 1
        return pf

    def tern1(g, fcl):
        fc = g * 8 + fcl
        pw = fcl % 2
        w_prev = wbX_busy[pw]
        tern_tok = []
        for hh in range(2):
            pf = wfp_slot()
            hsl_in = io["W1X"][fc, :, hh * 8:(hh + 1) * 8, :].rearrange(
                "p dc f -> p (dc f)")
            td = P.dma("sp", wfp[pf], hsl_in, waits=wfp_busy[pf],
                       sem="wfp%d" % pf)
            dst = wbX[pw][:, hh * 1024:(hh + 1) * 1024]
            ta = P.op("dve", (lambda E, o=dst, i=wfp[pf], a=thrp1:
                              E.tensor_scalar(out=o, in0=i, scalar1=a,
                                              scalar2=None, op0=ALU.is_gt)),
                      waits=U(td, C.t_thr, w_prev))
            tb = P.op("dve", (lambda E, o=bscrF, i=wfp[pf], a=thrn1:
                              E.tensor_scalar(out=o, in0=i, scalar1=a,
                                              scalar2=None, op0=ALU.is_lt)),
                      waits=U(td, C.t_thr, st_["prev_tc"]))
            tc_ = P.op("dve", (lambda E, o=dst, i=bscrF:
                               E.tensor_sub(out=o, in0=o, in1=i)),
                       waits=U(ta, tb))
            st_["prev_tc"] = tc_
            wfp_busy[pf] = U(ta, tb)
            tern_tok.append(tc_)
        wbX_busy[pw] = None
        st_["t1"][(g, fcl)] = U(*tern_tok)

    def tern2(g, eg):
        toks = []
        for pair in range(4):
            fc = g * 8 + pair * 2
            pf = wfp_slot()
            gr = io["W2R"][fc:fc + 2, :, eg * 512:(eg + 1) * 512].rearrange(
                "f p e -> p f e")
            td = P.dma("sp", wfp[pf].rearrange("p (f e) -> p f e", e=512),
                       gr, waits=wfp_busy[pf], sem="wfp%d" % pf)
            ws = 4 * (eg % 2) + pair
            sa, sb = 2 * (pair % 2), 2 * (pair % 2) + 1
            ts1 = P.op("act", (lambda E, o=sgn[sa], i=wfp[pf], a=inv2,
                               b=sm1(C, "half", 0):
                               E.activation(out=o, in_=i, func=AF.Sign,
                                            scale=a, bias=b)),
                       waits=U(td, C.t_thr, sgn_busy[sa]))
            ts2 = P.op("act", (lambda E, o=sgn[sb], i=wfp[pf], a=inv2,
                               b=sm1(C, "half", 1):
                               E.activation(out=o, in_=i, func=AF.Sign,
                                            scale=a, bias=b)),
                       waits=U(td, C.t_thr, sgn_busy[sb]))
            tsum = P.op("dve", (lambda E, o=f2w[ws], i=sgn[sa],
                                j=sgn[sb]:
                                E.tensor_add(out=o, in0=i, in1=j)),
                        waits=U(ts1, ts2, f2w_busy[ws]))
            sgn_busy[sa] = tsum
            sgn_busy[sb] = tsum
            wfp_busy[pf] = U(ts1, ts2)
            f2w_busy[ws] = None
            toks.append((ws, tsum))
        st_["t2"][(g, eg)] = toks

    def mm2(g, eg, t_h, ev_all):
        toks = st_["t2"].pop((g, eg))
        if g == 0:
            tseg = P.dma("act", b2seg, io["b2_r"][:, eg * 512:(eg + 1) * 512],
                         waits=U(C.t_thr, st_["t_b2seg_rd"], C.t_sc_q),
                         sem="b2s")
        for pair in range(4):
            ws, tsum = toks[pair]
            dst = f2w[ws]
            last = None
            for half in range(2):
                fcl = pair * 2 + half
                laststop = (fcl == 7 and g > 0)
                for tch in range(8):
                    stt = (fcl == 0)
                    w = U(C.bank_free[tch] if stt else None,
                          t_h if stt else None, tsum)
                    last = P.op("pe",
                        (lambda E, o=C.pb[tch][:, 0:512],
                         l=hslab[:, fcl * 1024 + tch * 128:
                                 fcl * 1024 + (tch + 1) * 128],
                         r=dst[:, half * 512:(half + 1) * 512],
                         s_=stt, sp=laststop:
                         E.matmul(out=o, lhsT=l, rhs=r, start=s_, stop=sp)),
                        waits=w, inc=(1 if tch == 7 else 0))
            f2w_busy[ws] = last
        seg_rd = []
        for tch in range(8):
            if g == 0:
                tokb = P.op("pe", (lambda E, o=C.pb[tch][:, 0:512],
                                   l=f32v(C, sm["invs_r"], 128, p=1),
                                   r=b2seg:
                                   E.matmul(out=o, lhsT=l, rhs=r, start=False,
                                            stop=True)),
                            waits=U(tseg, C.t_twos), inc=1)
                seg_rd.append(tokb)
            else:
                tokb = last
            C.bank_data[tch] = tokb
            xsl = f32v(C, X_OFF + tch * 2048 + eg * 512, 512)
            tev = P.op("dve", (lambda E, o=xsl, i=C.pb[tch][:, 0:512]:
                               E.scalar_tensor_tensor(out=o, in0=i, scalar=0.5,
                                                      in1=o, op0=ALU.mult,
                                                      op1=ALU.add)),
                       waits=U(tokb, C.t_ffn_x[tch]))
            C.bank_free[tch] = tev
            C.t_ffn_x[tch] = tev
            ev_all.append(tev)
        if g == 0:
            st_["t_b2seg_rd"] = U(*seg_rd)
    tern1(0, 0)
    tern1(0, 1)
    for g in range(8):
        h_toks = []
        for fcl in range(8):
            fc = g * 8 + fcl
            pw = fcl % 2
            ttern = st_["t1"].pop((g, fcl))
            for s in range(2):
                b = (2 * fcl + s) % 4
                mms = [(wbX[pw][:, dc * 128:(dc + 1) * 128],
                        xnf[:, dc * 1024 + s * 512: dc * 1024 + s * 512 + 512])
                       for dc in range(16)]
                tm = mm_group(C, b, mms, waits=U(ttern, C.t_xnf),
                              out_slice=C.pb[b][:, 0:512])
                wbX_busy[pw] = U(wbX_busy[pw], tm)
                hsl = hslab[:, fcl * 1024 + s * 512: fcl * 1024 + s * 512 + 512]
                b1c = C.slab[:, sm["b1t"] + fc: sm["b1t"] + fc + 1]
                tg = P.op("act", (lambda E, o=hsl, i=C.pb[b][:, 0:512], a=b1c:
                                  E.activation(out=o, in_=i, func=AF.Gelu, bias=a)),
                          waits=U(tm, C.t_consts, t_hfree))
                bank_read(C, b, tg)
                h_toks.append(tg)
            if fcl < 6:
                tern1(g, fcl + 2)
            if fcl == 5:
                tern2(g, 0)
        t_h = U(*h_toks)
        ev_all = []
        tern2(g, 1)
        mm2(g, 0, t_h, ev_all)
        tern2(g, 2)
        mm2(g, 1, t_h, ev_all)
        tern2(g, 3)
        mm2(g, 2, t_h, ev_all)
        if g < 7:
            tern1(g + 1, 0)
            tern1(g + 1, 1)
        mm2(g, 3, t_h, ev_all)
        t_hfree = U(*ev_all)
    C.wb_busy[0] = U(f2w_busy[0], f2w_busy[1])
    C.wb_busy[1] = U(f2w_busy[2], f2w_busy[3])
    C.t_sc_free = t_hfree
    C.t_slab2_free = t_hfree
    C.t_rows_free_ffn = t_hfree


def layer1(C):
    P, io, sm = C.P, C.io, C.sm
    sc = f32v(C, SC_OFF, 2048)
    sc2 = f32v(C, SC2_OFF, 2048)
    sq = f32v(C, SP2_OFF, 2048)
    wb, wb_busy = C.wb, C.wb_busy
    if C.part == "ALL":
        base_wait = C.t_xdone
        wkv = C.t_kv_read
        t_slab2 = C.t_xdone
        t_sc_free = C.t_xdone
        t_rows = C.t_rows_free
    else:
        base_wait = U(C.t_x, C.t_consts)
        wkv = None
        t_slab2 = None
        t_sc_free = None
        t_rows = C.t_consts
    C.t_kv_l1_free = wkv

    if C.part == "ALL":
        tst = ln_batch(C, C.t_consts, per_chunk=C.t_ffn_x)
    else:
        tst = ln_batch(C, U(base_wait, C.t_consts))
    lb = sm["lnb"]
    negmu8 = C.slab[:, lb + 16:lb + 24]
    rstd8 = C.slab[:, lb + 24:lb + 32]
    tmp8 = C.slab[:, lb + 32:lb + 40]
    rstd2_8 = C.slab[:, lb + 40:lb + 48]
    zscale8 = C.slab[:, sm["stat"]:sm["stat"] + 8]
    mA8 = C.slab[:, sm["stat"] + 8:sm["stat"] + 16]
    ta = P.op("dve", (lambda E, o=tmp8, i=rstd8:
                      E.tensor_mul(out=o, in0=i, in1=i)), waits=tst)
    tb = P.op("dve", (lambda E, o=rstd2_8, i=tmp8:
                      E.tensor_scalar(out=o, in0=i, scalar1=-EPS_LN,
                                      scalar2=1.0 + EPS_LN, op0=ALU.mult,
                                      op1=ALU.add)), waits=ta)
    tc0 = P.op("act", (lambda E, o=rstd2_8:
                       E.activation(out=o, in_=o, func=AF.Sqrt)), waits=tb)
    td = P.op("dve", (lambda E, o=rstd2_8: E.reciprocal(out=o, in_=o)),
              waits=tc0)
    te = P.op("dve", (lambda E, o=zscale8, i=rstd8, j=rstd2_8:
                      E.tensor_mul(out=o, in0=i, in1=j)), waits=td)
    tf = P.op("dve", (lambda E, o=mA8, i=rstd8:
                      E.tensor_scalar(out=o, in0=i, scalar1=2.0, scalar2=1.0,
                                      op0=ALU.mult, op1=ALU.add)), waits=tst)
    tg = P.op("dve", (lambda E, o=tmp8, i=negmu8, j=rstd8:
                      E.tensor_mul(out=o, in0=i, in1=j)), waits=U(tst, te))
    th = P.op("dve", (lambda E, o=tmp8:
                      E.tensor_scalar(out=o, in0=o, scalar1=2.0, scalar2=None,
                                      op0=ALU.mult)), waits=tg)
    tsc_all = U(te, tf, th)
    nb8 = C.slab[:, sm["lnb"] + 48:sm["lnb"] + 56]
    tnb = P.op("dve", (lambda E, o=nb8, i=negmu8, j=rstd8:
                       E.tensor_mul(out=o, in0=i, in1=j)), waits=tst)
    gb = [(C.slab[:, sm["gam"] + dc: sm["gam"] + dc + 1],
           C.slab[:, sm["bet"] + dc: sm["bet"] + dc + 1]) for dc in range(16)]
    xnf_toks, znf_toks = [], []
    tap_toks, tz_toks = [], []
    C.t_xup = [None] * 8
    for tc in range(TC):
        xc = f32v(C, X_OFF + tc * 2048, 2048)
        bx = sc if tc % 2 == 0 else sq
        bz = sc2 if tc % 2 == 0 else f32v(C, TAB_OFF, 2048)
        tap = P.op("act", (lambda E, o=bx, i=xc, a=rstd8[:, tc:tc + 1],
                           b=nb8[:, tc:tc + 1]:
                           E.activation(out=o, in_=i, func=AF.Identity,
                                        scale=a, bias=b)),
                   waits=U(tnb, t_sc_free, wkv,
                           xnf_toks[tc - 2] if tc >= 2 else None))
        tap_toks.append(tap)
        xnf_toks.append(transpose_chunk(C, bx, XNF_OFF, tc, tap))
        tz = P.op("dve", (lambda E, o=bz, i=xc, a=negmu8[:, tc:tc + 1],
                          b=zscale8[:, tc:tc + 1]:
                          E.tensor_scalar(out=o, in0=i, scalar1=a, scalar2=b,
                                          op0=ALU.add, op1=ALU.mult)),
                  waits=U(tsc_all, wkv,
                          znf_toks[tc - 2] if tc >= 2 else None))
        tz_toks.append(tz)
        znf_toks.append(transpose_chunk(C, bz, ZNF_OFF, tc, tz, evac=gb))
    C.t_xnf = U(*xnf_toks)
    def emit_xup():
        for tc in range(TC):
            xc = f32v(C, X_OFF + tc * 2048, 2048)
            tx = P.op("dve", (lambda E, o=xc, a=mA8[:, tc:tc + 1],
                              b=tmp8[:, tc:tc + 1]:
                              E.tensor_scalar(out=o, in0=o, scalar1=a, scalar2=b,
                                              op0=ALU.mult, op1=ALU.add)),
                      waits=U(tap_toks[tc], tz_toks[tc], tsc_all))
            C.t_xup[tc] = tx
    t_sq_guard = U(xnf_toks[1], xnf_toks[3], xnf_toks[5], xnf_toks[7])
    t_tab_guard = U(znf_toks[1], znf_toks[3], znf_toks[5], znf_toks[7])
    xnf = bf16v(C, XNF_OFF, 16384)
    znf = bf16v(C, ZNF_OFF, 16384)
    C.t_Xnf = U(*znf_toks)

    tab = [bf16v(C, TAB_OFF, 4096), bf16v(C, TAB_OFF, 4096)]
    tab_busy = [t_tab_guard, t_tab_guard]
    Pslab = bf16v(C, SLAB2_OFF, 8192)
    onesbf = C.slab[:, sm["onesb"]:sm["onesb"] + 1].bitcast(BF16)[:, 0:1]
    ident = f32v(C, sm["ident"], 128)

    def rope_ap(p, half):
        return tab[p][:, half * 2048: half * 2048 + 2048].rearrange(
            "p (cc t) -> p cc t", t=512)

    gi = 0
    Apart_toks = []
    C.prev_scrd = None
    wbL = [wb[0], wb[1],
           bf16v(C, HF_OFF, 2048), bf16v(C, WH_OFF, 2048)]
    wbL_busy = [wb_busy[0], wb_busy[1],
                U(wkv, base_wait), U(wkv, base_wait)]
    wbL_sem = ["wb0", "wb1", "wi2", "wi3"]
    for s in range(2):
        ptab = 0
        tdc = P.dma("sp", rope_ap(ptab, 0),
                    io["ropec"][:, :, s * 512:(s + 1) * 512],
                    waits=U(tab_busy[ptab], wkv), sem="tab")
        tds = P.dma("sp", rope_ap(ptab, 1),
                    io["ropes"][:, :, s * 512:(s + 1) * 512],
                    waits=U(tab_busy[ptab], wkv), sem="tab")
        t_tab = U(tdc, tds)
        nrm_open = {"k": False, "v": False}
        g0 = t_sq_guard if s == 0 else None
        nrm_last = {"k": g0, "v": g0}
        sq_last = {"k": None, "v": None}

        def proj(WX, bank, ec, pgi):
            tdw = P.dma("pool", wbL[pgi].rearrange("p (dc f) -> p dc f", f=128),
                        io[WX][ec], waits=wbL_busy[pgi], sem=wbL_sem[pgi])
            mms = [(wbL[pgi][:, dc * 128:(dc + 1) * 128],
                    znf[:, dc * 1024 + s * 512: dc * 1024 + s * 512 + 512])
                   for dc in range(16)]
            tk = mm_group(C, bank, mms, waits=U(tdw, C.t_Xnf),
                          out_slice=C.pb[bank][:, 0:512])
            return tk

        def evac(bank, dstsl, bias_name, ec, tk, extra=None):
            bc = C.slab[:, sm[bias_name] + ec: sm[bias_name] + ec + 1]
            te = P.op("act", (lambda E, o=dstsl, i=C.pb[bank][:, 0:512], a=bc:
                              E.activation(out=o, in_=i, func=AF.Identity,
                                           bias=a)),
                      waits=U(tk, C.t_consts, extra))
            bank_read(C, bank, te)
            return te

        def sqsum(src_sl, which, tprev, stop):
            bank = 4 if which == "k" else 5
            sq_off = 0 if which == "k" else 256
            sq_sl = sq[:, sq_off:sq_off + 256].bitcast(BF16)
            t1 = P.op("act", (lambda E, o=sq_sl, i=src_sl:
                              E.activation(out=o, in_=i, func=AF.Square)),
                      waits=U(tprev, nrm_last[which]))
            st = not nrm_open[which]
            nrm_open[which] = True
            w0 = U(t1, C.bank_free[bank]) if st else U(t1)
            t2 = P.op("pe", (lambda E, o=C.pb[bank][0:1, 0:512], l=onesbf,
                             r=sq_sl, s_=st, sp=stop:
                             E.matmul(out=o, lhsT=l, rhs=r, start=s_, stop=sp)),
                      waits=w0, inc=1)
            nrm_last[which] = t2
            if stop:
                C.bank_data[bank] = t2
                C.bank_free[bank] = None
            return t2

        P_toks = {}

        def do_chunk_post(ksl, vsl, tks, tvs, ecs):
            tqk = sqsum(ksl, "k", tks, stop=(ecs == 15))
            tqv = sqsum(vsl, "v", tvs, stop=(ecs == 15))
            psl = Pslab[:, ecs * 512:(ecs + 1) * 512]
            tp = P.op("dve", (lambda E, o=psl, i=ksl, j=vsl:
                              E.tensor_mul(out=o, in0=i, in1=j)),
                      waits=U(tks, tvs, t_slab2))
            P_toks[ecs] = U(tp, tqk, tqv)
            return U(tp, tqk), U(tp, tqv)

        for pr in range(4):
            ktoks, vtoks = {}, {}
            for half, ecs in ((0, pr), (1, pr + 4)):
                kb = (2, 6)[ecs % 2]
                vb = (3, 7)[ecs % 2]
                pgi = gi % 4; gi += 1
                tk = proj("WinKX", kb, ecs, pgi)
                ksl = sc[:, half * 512: half * 512 + 512]
                ktoks[half] = evac(kb, ksl, "binK", ecs, tk,
                                   extra=U(t_sc_free if s == 0 and pr == 0
                                           else None, C.prev_scrd))
                wbL_busy[pgi] = tk
                pgi = gi % 4; gi += 1
                tv = proj("WinVX", vb, ecs, pgi)
                vsl = sc2[:, half * 512: half * 512 + 512]
                vtoks[half] = evac(vb, vsl, "binV", ecs, tv, extra=C.prev_scrd)
                wbL_busy[pgi] = tv
            cosc = rope_ap(ptab, 0)[:, pr, :]
            sinc = rope_ap(ptab, 1)[:, pr, :]
            rots = []
            for scr, toks in ((sc, ktoks), (sc2, vtoks)):
                x1 = scr[:, 0:512]; x2 = scr[:, 512:1024]
                t1s = scr[:, 1024:1536]; t2s = scr[:, 1536:2048]
                ta = P.op("dve", (lambda E, o=t1s, i=x1, cc=cosc:
                                  E.tensor_mul(out=o, in0=i, in1=cc)),
                          waits=U(toks[0], toks[1], t_tab))
                tb = P.op("dve", (lambda E, o=t2s, i=x2, ss=sinc:
                                  E.tensor_mul(out=o, in0=i, in1=ss)))
                tcx = P.op("dve", (lambda E, o=t1s, i=t2s:
                                   E.tensor_sub(out=o, in0=o, in1=i)),
                           waits=U(ta, tb))
                td_ = P.op("dve", (lambda E, o=t2s, i=x2, cc=cosc:
                                   E.tensor_mul(out=o, in0=i, in1=cc)),
                           waits=U(tb, tcx))
                te_ = P.op("dve", (lambda E, o=x2, i=x1, ss=sinc:
                                   E.tensor_mul(out=o, in0=i, in1=ss)),
                           waits=td_)
                tf_ = P.op("dve", (lambda E, o=t2s, i=x2:
                                   E.tensor_add(out=o, in0=o, in1=i)),
                           waits=U(td_, te_))
                rots.append((t1s, tcx, t2s, tf_))
            (k1, tk1, k2, tk2) = rots[0]
            (v1, tv1, v2, tv2) = rots[1]
            do_chunk_post(k1, v1, tk1, tv1, pr)
            do_chunk_post(k2, v2, tk2, tv2, pr + 4)
            C.prev_scrd = U(P_toks[pr], P_toks[pr + 4], tk1, tk2, tv1, tv2)
        for ecs in range(8, 16):
            kb = (2, 6)[ecs % 2]
            vb = (3, 7)[ecs % 2]
            pgi = gi % 4; gi += 1
            tk = proj("WinKX", kb, ecs, pgi)
            ksl = sc[:, 0:512]
            tke = evac(kb, ksl, "binK", ecs, tk, extra=C.prev_scrd)
            wbL_busy[pgi] = tk
            pgi = gi % 4; gi += 1
            tv = proj("WinVX", vb, ecs, pgi)
            vsl = sc2[:, 0:512]
            tve = evac(vb, vsl, "binV", ecs, tv, extra=C.prev_scrd)
            wbL_busy[pgi] = tv
            do_chunk_post(ksl, vsl, tke, tve, ecs)
            C.prev_scrd = P_toks[ecs]
        nk_sb = sq[0:1, 1024:1536]
        t1 = P.op("dve", (lambda E, o=nk_sb, i=C.pb[4][0:1, 0:512]:
                          E.tensor_copy(out=o, in_=i)), waits=C.bank_data[4])
        C.bank_free[4] = t1
        prod = sq[0:1, 1536:2048]
        t2 = P.op("dve", (lambda E, o=prod, i=nk_sb, j=C.pb[5][0:1, 0:512]:
                          E.tensor_mul(out=o, in0=i, in1=j)),
                  waits=U(t1, C.bank_data[5]))
        C.bank_free[5] = t2
        t3 = P.op("act", (lambda E, o=prod:
                          E.activation(out=o, in_=o, func=AF.Sqrt)), waits=t2)
        t4 = P.op("dve", (lambda E, o=prod: E.reciprocal(out=o, in_=o)), waits=t3)
        w0 = U(C.bank_free[6], t4, C.t_consts)
        tbc = P.op("pe", (lambda E, o=C.pb[6][:, 0:512], l=ones_rr(C), r=prod:
                          E.matmul(out=o, lhsT=l, rhs=r, start=True, stop=True)),
                   waits=w0)
        C.bank_data[6] = tbc
        Gsb = sq[:, 512:768].bitcast(BF16)
        t5 = P.op("dve", (lambda E, o=Gsb, i=C.pb[6][:, 0:512]:
                          E.tensor_copy(out=o, in_=i)), waits=tbc)
        C.bank_free[6] = t5
        for ecs in range(16):
            psl = Pslab[:, ecs * 512:(ecs + 1) * 512]
            acol = C.slab[:, sm["Apart"] + s * 16 + ecs:
                          sm["Apart"] + s * 16 + ecs + 1]
            tA = P.op("dve", (lambda E, o=psl, i=psl, gg=Gsb, a=acol:
                              E.scalar_tensor_tensor(out=o, in0=i, scalar=1.0,
                                                     in1=gg, op0=ALU.mult,
                                                     op1=ALU.mult, accum_out=a)),
                      waits=U(P_toks[ecs], t5))
            Apart_toks.append(tA)
        tab_busy[ptab] = U(*[P_toks[e_] for e_ in range(8)])
        t_slab2 = U(*Apart_toks)
    t_Ap = U(*Apart_toks)
    wb_busy[0] = wbL_busy[0]
    wb_busy[1] = wbL_busy[1]
    C.t_wbL_free = U(wbL_busy[2], wbL_busy[3])
    Af = f32v(C, sm["Afin"], 16)
    tAs = P.op("dve", (lambda E, o=Af,
                       i=C.slab[:, sm["Apart"]:sm["Apart"] + 16],
                       j=C.slab[:, sm["Apart"] + 16:sm["Apart"] + 32]:
                       E.tensor_add(out=o, in0=i, in1=j)), waits=t_Ap)
    c2i, c2o = C.cc["c2i"], C.cc["c2o"]
    td = P.dma("pool", c2i.ap(), Af, waits=tAs, sem="cci")
    if SIM_NOCC:
        tl = []
        for r in range(8):
            tl.append(P.dma("pool", c2o.ap()[r * 128:(r + 1) * 128, :],
                            c2i.ap(), waits=td, sem="agl"))
        tcc = U(*tl)
    else:
        tcc = P.op("pool", (lambda E: E.collective_compute(
            "AllGather", ALU.bypass, replica_groups=RG,
            ins=[c2i.ap().opt()], outs=[c2o.ap().opt()])), waits=td)

    emit_xup()

    moe(C, t_Ap)

    s_ag = f32v(C, SC_OFF, 2048, p=8)
    tld = P.dma("pool", s_ag, c2o.ap().rearrange("(r p) c -> r (p c)", p=128),
                waits=U(tcc, t_Ap, getattr(C, "t_thr", None)), sem="agl")
    mask1 = C.slab[0:8, sm["mask"] + 1:sm["mask"] + 2]
    w0 = U(tld, C.bank_free[6], C.t_consts)
    tmo = None
    s_ag3 = s_ag.rearrange("r (p c) -> r p c", c=16)
    for cch in range(16):
        tmo = P.op("pe",
                   (lambda E, o=C.pb[6][:, cch:cch + 1],
                    l=s_ag3[:, :, cch], r=mask1:
                    E.matmul(out=o, lhsT=l, rhs=r, start=True, stop=True)),
                   waits=(w0 if cch == 0 else None), inc=(1 if cch == 15 else 0))
    C.bank_data[6] = tmo
    tof = P.op("dve", (lambda E, o=Af, i=C.pb[6][:, 0:16]:
                       E.tensor_copy(out=o, in_=i)), waits=U(tmo, tAs))
    C.bank_free[6] = tof
    tof2 = P.op("dve", (lambda E, o=Af:
                        E.tensor_scalar(out=o, in0=o, scalar1=0.5, scalar2=0.5,
                                        op0=ALU.mult, op1=ALU.add)), waits=tof)
    t_Afin = tof2

    YF = bf16v(C, SLAB2_OFF, 8192)
    bout_row = f32v(C, ROWS_OFF, 2048, p=1)
    t_borow = P.dma("act", bout_row, io["bout_r"], waits=t_rows, sem="brow")
    ring_busy = [C.t_moe_xnf_done] * 8
    ru = [0]

    def ring_take(units):
        u = ru[0]
        if u % units:
            u += units - (u % units)
        if u + units > 8:
            u = 0
        ru[0] = u + units
        off = XNF_OFF + u * 1024
        tok = U(*[ring_busy[i] for i in range(u, u + units)])
        def mark(t):
            for i in range(u, u + units):
                ring_busy[i] = t
        return off, tok, mark, "rg%d" % u

    x_final = []
    xdc = [[] for _ in range(8)]
    for s in range(2):
        y_toks = []
        for ecp in range(8):
            ec0 = ecp * 2
            off, wtok, mark, rsem = ring_take(2)
            gr = bf16v(C, off, 4096).rearrange("p (e n) -> p e n", n=2048)
            tdq = P.dma("sp", gr,
                        io["WinQX"][ec0:ec0 + 2].rearrange(
                            "e p d f -> p e (d f)"),
                        waits=wtok, sem=rsem)
            consumed = []
            for ei in range(2):
                ec = ec0 + ei
                bq_b = (2, 3)[ec % 2]
                mms = [(gr[:, ei, dc * 128:(dc + 1) * 128],
                        znf[:, dc * 1024 + s * 512: dc * 1024 + s * 512 + 512])
                       for dc in range(16)]
                tq = mm_group(C, bq_b, mms, waits=U(tdq, C.t_Xnf),
                              out_slice=C.pb[bq_b][:, 0:512])
                ysl = YF[:, ec * 512:(ec + 1) * 512]
                bqc = C.slab[:, sm["binQ"] + ec: sm["binQ"] + ec + 1]
                aac = C.slab[:, sm["Afin"] + ec: sm["Afin"] + ec + 1]
                ty = P.op("dve", (lambda E, o=ysl, i=C.pb[bq_b][:, 0:512],
                                  a=bqc, b=aac:
                                  E.tensor_scalar(out=o, in0=i, scalar1=a,
                                                  scalar2=b, op0=ALU.add,
                                                  op1=ALU.mult)),
                          waits=U(tq, t_Afin, t_Ap, C.t_moe_slab2))
                bank_read(C, bq_b, ty)
                y_toks.append(ty)
                consumed.append(tq)
            mark(U(*consumed))
        t_Y = U(*y_toks)
        for egp in range(2):
            for dcp in range(8):
                off, wtok, mark, rsem = ring_take(1)
                gr = bf16v(C, off, 2048).rearrange("p (d e) -> p d e", e=1024)
                tdw = P.dma("sp", gr,
                            io["WoutR"][dcp * 2:dcp * 2 + 2, :,
                                        egp * 1024:(egp + 1) * 1024]
                            .rearrange("d p e -> p d e"),
                            waits=wtok, sem=rsem)
                last = None
                for dcl in range(2):
                    dc = dcp * 2 + dcl
                    for bidx in range(8):
                        tch, eg = bidx // 2, bidx % 2
                        st = (dc == 0)
                        w = U(C.bank_free[bidx] if st else None,
                              t_Y if st else None, tdw)
                        last = P.op("pe",
                            (lambda E, o=C.pb[bidx][:, 0:512],
                             l=YF[:, dc * 512 + tch * 128:
                                   dc * 512 + (tch + 1) * 128],
                             r=gr[:, dcl, eg * 512:(eg + 1) * 512], s_=st:
                             E.matmul(out=o, lhsT=l, rhs=r, start=s_, stop=False)),
                            waits=w, inc=(1 if bidx == 7 else 0))
                mark(last)
            for bidx in range(8):
                tch, eg = bidx // 2, bidx % 2
                tokmm = P.op("pe", (lambda E, o=C.pb[bidx][:, 0:512], l=ones_rr(C),
                                    r=bout_row[:, egp * 1024 + eg * 512:
                                               egp * 1024 + eg * 512 + 512]:
                                    E.matmul(out=o, lhsT=l, rhs=r, start=False,
                                             stop=True)),
                             waits=U(t_borow, C.t_consts), inc=1)
                C.bank_data[bidx] = tokmm
                xsl = f32v(C, X_OFF + (s * 4 + tch) * 2048 + egp * 1024 + eg * 512,
                           512)
                tev = P.op("dve", (lambda E, o=xsl, i=C.pb[bidx][:, 0:512]:
                                   E.tensor_add(out=o, in0=o, in1=i)),
                           waits=U(tokmm, C.t_moe_x[s * 4 + tch],
                                   C.t_xup[s * 4 + tch]))
                C.bank_free[bidx] = tev
                x_final.append(tev)
                xdc[s * 4 + tch].append(tev)
    C.t_xdone = U(*x_final)
    C.t_xdone_c = [U(*xdc[c]) for c in range(8)]


def moe(C, t_sq_free):
    P, io, sm = C.P, C.io, C.sm
    xnf = bf16v(C, XNF_OFF, 16384)
    sq = f32v(C, SP2_OFF, 2048)
    hf = f32v(C, HF_OFF, 1024)
    whb = bf16v(C, WH_OFF, 1024)
    wb, wb_busy = C.wb, C.wb_busy
    ident = f32v(C, sm["ident"], 128)

    trt = P.dma("sp", wb[0][:, 0:128].rearrange("p (dc e) -> p dc e", e=8),
                io["RtX"], waits=wb_busy[0], sem="rt")
    rtv = wb[0][:, 0:128].rearrange("p (dc e) -> p dc e", e=8)
    tdd = P.dma("sp", wb[1].rearrange("p (dc e) -> p dc e", e=128),
                io["DnX"], waits=wb_busy[1], sem="dn")
    dnb = wb[1].rearrange("p (dc e) -> p dc e", e=128)

    lg_toks = []
    for s in range(2):
        w0 = U(C.bank_free[7], trt, C.t_xnf)
        tok = None
        for i in range(16):
            tok = P.op("pe", (lambda E, o=C.pb[7][0:8, 0:512],
                              l=rtv[:, i, :],
                              r=xnf[:, i * 1024 + s * 512: i * 1024 + s * 512 + 512],
                              st=(i == 0), sp=(i == 15):
                              E.matmul(out=o, lhsT=l, rhs=r, start=st, stop=sp)),
                       waits=(w0 if i == 0 else None), inc=(1 if i == 15 else 0))
        C.bank_data[7] = tok
        lsb = sq[0:8, s * 512: s * 512 + 512]
        te = P.op("dve", (lambda E, o=lsb, i=C.pb[7][0:8, 0:512]:
                          E.tensor_copy(out=o, in_=i)), waits=U(tok, t_sq_free))
        C.bank_free[7] = te
        lg_toks.append(te)
    wb_busy[0] = U(wb_busy[0], *lg_toks)
    t_lg = U(*lg_toks)

    ltok_toks = []
    for tcch in range(8):
        b = tcch % 2
        w0 = U(C.bank_free[b], t_lg, C.t_consts)
        tt_ = P.op("pe", (lambda E, o=C.pb[b][:, 0:8],
                          i=sq[0:8, tcch * 128:(tcch + 1) * 128],
                          idn=ident[0:8, 0:8]:
                          E.transpose(out=o, in_=i, identity=idn)), waits=w0)
        C.bank_data[b] = tt_
        lcol = C.slab[:, sm["ltok"] + tcch * 8: sm["ltok"] + tcch * 8 + 8]
        te = P.op("dve", (lambda E, o=lcol, i=C.pb[b][:, 0:8]:
                          E.tensor_copy(out=o, in_=i)), waits=tt_)
        C.bank_free[b] = te
        ltok_toks.append(te)

    wf_toks = []
    for tcch in range(8):
        lcol = C.slab[:, sm["ltok"] + tcch * 8: sm["ltok"] + tcch * 8 + 8]
        wcol = C.slab[:, sm["wful"] + tcch * 8: sm["wful"] + tcch * 8 + 8]
        e8 = C.slab[:, sm["e8"]:sm["e8"] + 8]
        m1 = sm1(C, "tiny", 8)
        m2 = sm1(C, "tiny", 9)
        t0 = ltok_toks[tcch]
        t1 = P.op("dve", (lambda E, o=m1, i=lcol:
                          E.reduce_max(out=o, in_=i, axis=AX.X)), waits=t0)
        t2 = P.op("dve", (lambda E, o=wcol, i=lcol, a=m1:
                          E.tensor_scalar(out=o, in0=i, scalar1=a, scalar2=None,
                                          op0=ALU.is_ge)), waits=t1)
        t3 = P.op("dve", (lambda E, o=wcol, i=lcol:
                          E.scalar_tensor_tensor(out=o, in0=o, scalar=-BIG,
                                                 in1=i, op0=ALU.mult,
                                                 op1=ALU.add)), waits=t2)
        t4 = P.op("dve", (lambda E, o=m2, i=wcol:
                          E.reduce_max(out=o, in_=i, axis=AX.X)), waits=t3)
        t5 = P.op("dve", (lambda E, o=wcol, i=lcol, a=m2:
                          E.tensor_scalar(out=o, in0=i, scalar1=a, scalar2=None,
                                          op0=ALU.is_ge)), waits=t4)
        negm1 = sm1(C, "tiny", 10)
        t6 = P.op("dve", (lambda E, o=negm1, i=m1:
                          E.tensor_scalar(out=o, in0=i, scalar1=-1.0,
                                          scalar2=None, op0=ALU.mult)), waits=t1)
        t7 = P.op("act", (lambda E, o=e8, i=lcol, a=negm1:
                          E.activation(out=o, in_=i, func=AF.Exp, bias=a)),
                  waits=U(t6, t5))
        t8 = P.op("dve", (lambda E, o=wcol, i=e8:
                          E.tensor_mul(out=o, in0=o, in1=i)), waits=t7)
        den = sm1(C, "tiny", 8)
        t9 = P.op("dve", (lambda E, o=den, i=wcol:
                          E.reduce_sum(out=o, in_=i, axis=AX.X)), waits=t8)
        rd = sm1(C, "tiny", 9)
        t10 = P.op("dve", (lambda E, o=rd, i=den: E.reciprocal(out=o, in_=i)),
                   waits=t9)
        t11 = P.op("dve", (lambda E, o=wcol, a=rd:
                           E.tensor_scalar(out=o, in0=o, scalar1=a, scalar2=None,
                                           op0=ALU.mult)), waits=t10)
        wf_toks.append(t11)

    wrow = sq[0:8, 1024:1536].bitcast(BF16)
    wT_toks = []
    for tcch in range(8):
        b = tcch % 2
        wcol = C.slab[:, sm["wful"] + tcch * 8: sm["wful"] + tcch * 8 + 8]
        w0 = U(C.bank_free[b], wf_toks[tcch], C.t_consts, t_sq_free)
        tt_ = P.op("pe", (lambda E, o=C.pb[b][0:8, 0:128], i=wcol, idn=ident:
                          E.transpose(out=o, in_=i, identity=idn)), waits=w0)
        C.bank_data[b] = tt_
        te = P.op("dve", (lambda E, o=wrow[:, tcch * 128:(tcch + 1) * 128],
                          i=C.pb[b][0:8, 0:128]:
                          E.tensor_copy(out=o, in_=i)), waits=tt_)
        C.bank_free[b] = te
        wT_toks.append(te)
    t_wT = U(*wT_toks)

    h_toks = []
    for s in range(2):
        mms = [(dnb[:, dc, :],
                xnf[:, dc * 1024 + s * 512: dc * 1024 + s * 512 + 512])
               for dc in range(16)]
        tm = mm_group(C, 7, mms, waits=U(tdd, C.t_xnf),
                      out_slice=C.pb[7][:, 0:512])
        te = P.op("dve", (lambda E, o=hf[:, s * 512:(s + 1) * 512],
                          i=C.pb[7][:, 0:512]:
                          E.tensor_copy(out=o, in_=i)),
                  waits=U(tm, C.t_kv_l1_free, C.t_wbL_free))
        bank_read(C, 7, te)
        h_toks.append(te)
    t_h = U(*h_toks)
    wb_busy[1] = t_h

    esl = C.slab[0:8, sm["esel"]:sm["esel"] + 64].bitcast(BF16)
    wrep_toks = []
    for s in range(2):
        w0 = U(C.bank_free[6], t_wT, C.t_consts)
        tm = P.op("pe", (lambda E, o=C.pb[6][:, 0:512], l=esl,
                         r=wrow[:, s * 512:(s + 1) * 512]:
                         E.matmul(out=o, lhsT=l, rhs=r, start=True, stop=True)),
                  waits=w0)
        C.bank_data[6] = tm
        twh = P.op("dve", (lambda E, o=whb[:, s * 512:(s + 1) * 512],
                           i=hf[:, s * 512:(s + 1) * 512], j=C.pb[6][:, 0:512]:
                           E.tensor_mul(out=o, in0=i, in1=j)),
                   waits=U(tm, t_h))
        C.bank_free[6] = twh
        wrep_toks.append(twh)
    t_wh = U(*wrep_toks)

    tdu = P.dma("sp", wb[0][:, 0:2048], io["UpA"],
                waits=U(wb_busy[0], t_lg), sem="up")
    upv = wb[0][:, 0:2048]
    C.t_moe_x = [None] * 8
    up_toks = []
    for eg in range(4):
        for tch in range(8):
            b = tch
            w0 = U(C.bank_free[b], t_wh, tdu)
            tm = P.op("pe", (lambda E, o=C.pb[b][:, 0:512],
                             l=whb[:, tch * 128:(tch + 1) * 128],
                             r=upv[:, eg * 512:(eg + 1) * 512]:
                             E.matmul(out=o, lhsT=l, rhs=r, start=True,
                                      stop=True)), waits=w0)
            C.bank_data[b] = tm
            xsl = f32v(C, X_OFF + tch * 2048 + eg * 512, 512)
            tev = P.op("dve", (lambda E, o=xsl, i=C.pb[b][:, 0:512]:
                               E.tensor_add(out=o, in0=o, in1=i)),
                       waits=U(tm, C.t_xup[tch]))
            C.bank_free[b] = tev
            C.t_moe_x[tch] = U(C.t_moe_x[tch], tev)
            up_toks.append(tev)
    wb_busy[0] = U(*up_toks)
    C.t_moe_slab2 = None
    C.t_moe_xnf_done = U(t_lg, t_h)


def kernel(**inputs):
    g = stage_host(inputs)
    cores = [stage_core(inputs, cid, g) for cid in range(NCORE)]
    if FUSED:
        nc = build_nc("ALL")
        in_maps = []
        for cid in range(NCORE):
            m = {k: (cores[cid][k] if k in cores[cid] else g[k])
                 for k in KEYS_ALL}
            in_maps.append(m)
        res = run_bass_kernel_spmd(nc, in_maps, core_ids=list(range(NCORE)))
        outs = [res.results[c]["out"] for c in range(NCORE)]
    else:
        nc0 = build_nc("L0")
        in_maps = []
        for cid in range(NCORE):
            m = {k: (cores[cid][k] if k in cores[cid] else g[k])
                 for k in KEYS_L0}
            in_maps.append(m)
        r0 = run_bass_kernel_spmd(nc0, in_maps, core_ids=list(range(NCORE)))
        nc1 = build_nc("L1")
        in_maps1 = []
        for cid in range(NCORE):
            m = {k: (cores[cid][k] if k in cores[cid] else g[k])
                 for k in KEYS_L1}
            m["xin"] = np.ascontiguousarray(np.asarray(r0.results[cid]["out"]))
            in_maps1.append(m)
        r1 = run_bass_kernel_spmd(nc1, in_maps1, core_ids=list(range(NCORE)))
        outs = [r1.results[c]["out"] for c in range(NCORE)]
    full = np.concatenate([np.asarray(o) for o in outs], axis=0)
    return full.reshape(1, 8192, 2048).astype(np.float32)


# version 16
# speedup vs baseline: 1.0805x; 1.0805x over previous
import numpy as np
import ml_dtypes
from contextlib import ExitStack

import concourse.bass as bass
import concourse.mybir as mybir
from concourse.bass_utils import run_bass_kernel_spmd

F32 = mybir.dt.float32
BF16 = mybir.dt.bfloat16
AF = mybir.ActivationFunctionType
ALU = mybir.AluOpType
AX = mybir.AxisListType
BF = ml_dtypes.bfloat16

NCORE = 8
TOK = 1024
TC = 8
D = 2048
DFF = 8192
SCALE = float(128.0 ** -0.5)
INVSCALE = float(128.0 ** 0.5)
EPS_LN = 1e-5
EPS_BIT = 1e-8
BIG = 30000.0
RG = [[0, 1, 2, 3, 4, 5, 6, 7]]
FUSED = True
SELF_WAIT = False
SIM_NOCC = False
DBG_STOP = None
RACE_DETECT = False

X_OFF = 0
XNF_OFF = 16384
KV_OFF = 24576
SLAB2_OFF = 40960
SC_OFF = 45056
WF32_OFF = 47104
WBF_OFF = 48128
ROWS_OFF = 50176
SMALL_OFF = 52224
SLAB_COLS = 53200

ZNF_OFF = KV_OFF
TAB_OFF = KV_OFF + 8192
HF_OFF = KV_OFF + 10240
WH_OFF = KV_OFF + 11264
SC2_OFF = KV_OFF + 12288
SP2_OFF = KV_OFF + 14336

ENGINES = ("pe", "dve", "act", "pool", "sp")
BLOCK_ATTR = {"pe": "tensor", "dve": "vector", "act": "scalar",
              "pool": "gpsimd", "sp": "sync"}
SEM_NAMES = ("pe", "dve", "act", "pool", "sp",
             "pool_dma", "sp_dma", "act_dma",
             "wfp0", "wfp1", "wfp2", "wfp3", "wfp4",
             "wb0", "wb1", "tab", "xio", "oio",
             "cst", "cci", "agl", "b2s", "brow", "rt", "dn", "up",
             "rg0", "rg1", "rg2", "rg3", "rg4", "rg5", "rg6", "rg7",
             "wk0", "wk1", "wk2", "wk3", "wk4", "wk5", "wi2", "wi3",
             "wq0", "wq1", "wq2", "wq3", "wq4", "wq5",
             "xo0", "xo1", "xo2", "xo3", "xo4", "xo5", "xo6", "xo7")


def U(*toks):
    out = {}
    for t in toks:
        if not t:
            continue
        for s, v in t.items():
            if out.get(s, -1) < v:
                out[s] = v
    return out


class Prog:
    def __init__(self, nc):
        self.nc = nc
        self.ops = {e: [] for e in ENGINES}
        self.sems = {}
        self.cnt = {s: 0 for s in SEM_NAMES}
        self.waited = {e: {} for e in ENGINES}

    def op(self, eng, fn, waits=None, inc=1, dma=False, sem=None):
        semname = (sem or eng + "_dma") if dma else eng
        need = []
        wt = self.waited[eng]
        if SELF_WAIT and not dma and eng in ("dve", "act", "pool"):
            prev = self.cnt[eng]
            if prev and wt.get(eng, 0) < prev:
                need.append((eng, prev))
                wt[eng] = prev
        if waits:
            for s, v in waits.items():
                if s.endswith("_dma"):
                    v = self.cnt[s]
                if wt.get(s, 0) >= v:
                    continue
                need.append((s, v))
                wt[s] = v
        step = (16 if dma else inc) if inc else 0
        if step:
            self.cnt[semname] += step
        target = self.cnt[semname]

        def emit(engine, _need=tuple(need), _fn=fn, _step=step, _sem=semname):
            for s, v in _need:
                engine.wait_ge(self.sems[s], v)
            instr = _fn(engine)
            if _step:
                instr.then_inc(self.sems[_sem], _step)
        self.ops[eng].append(emit)
        return {semname: target} if step else None

    def dma(self, eng, out, in_, waits=None, sem=None):
        return self.op(eng, lambda E, o=out, i=in_: E.dma_start(out=o, in_=i),
                       waits=waits, dma=True, sem=sem)

    def wait_only(self, eng, waits):
        need = []
        wt = self.waited[eng]
        for s, v in waits.items():
            if wt.get(s, 0) >= v:
                continue
            need.append((s, v))
            wt[s] = v

        def emit(engine, _need=tuple(need)):
            for s, v in _need:
                engine.wait_ge(self.sems[s], v)
        self.ops[eng].append(emit)

    def finalize(self, block):
        for e in ENGINES:
            fns = self.ops[e]
            if not fns:
                continue

            def body(engine, _fns=tuple(fns)):
                for f in _fns:
                    f(engine)
            getattr(block, BLOCK_ATTR[e])(body)


def _xform(W):
    return np.ascontiguousarray(W.reshape(16, 128, 16, 128).transpose(0, 3, 2, 1))


def _xform_de(Wde):
    ecn = Wde.shape[1] // 128
    return np.ascontiguousarray(
        Wde.reshape(16, 128, ecn, 128).transpose(2, 1, 0, 3))


def _rhsform(Wde):
    return np.ascontiguousarray(Wde.reshape(-1, 128, Wde.shape[1]))


def _pvec(v):
    return np.ascontiguousarray(np.asarray(v).reshape(-1, 128).T)


def stage_host(inputs):
    g = {}
    perm = np.concatenate([np.arange(0, 1024, 2), np.arange(1, 1024, 2),
                           np.arange(1024, 2048)])
    g["WqX"] = _xform(np.asarray(inputs["L0_attn_Wq"])).astype(BF)
    g["WkX"] = _xform(np.asarray(inputs["L0_attn_Wk"])).astype(BF)
    g["WvX"] = _xform(np.asarray(inputs["L0_attn_Wv"])).astype(BF)
    g["WoR"] = _rhsform(np.ascontiguousarray(
        np.asarray(inputs["L0_attn_Wo"]).T)).astype(BF)
    g["bq"] = _pvec(inputs["L0_attn_bq"])
    g["bk"] = _pvec(inputs["L0_attn_bk"])
    g["bv"] = _pvec(inputs["L0_attn_bv"])
    g["bo_r"] = np.ascontiguousarray(np.asarray(inputs["L0_attn_bo"])[None, :])
    w1 = np.asarray(inputs["L0_ffn_w1"])
    w2 = np.asarray(inputs["L0_ffn_w2"])
    g["W1X"] = np.ascontiguousarray(
        w1.reshape(64, 128, 16, 128).transpose(0, 3, 2, 1))
    g["W2R"] = np.ascontiguousarray(w2.T.reshape(64, 128, 2048))
    g["b1t"] = _pvec(inputs["L0_ffn_b1"])
    g["b2_r"] = np.ascontiguousarray(np.asarray(inputs["L0_ffn_b2"])[None, :])
    Win = np.asarray(inputs["L1_attn_W_in"])
    bin_ = np.asarray(inputs["L1_attn_b_in"])
    g["WinQX"] = _xform_de(np.ascontiguousarray(Win[:, 0:2048][:, perm])).astype(BF)
    g["WinKX"] = _xform_de(np.ascontiguousarray(Win[:, 2048:4096][:, perm])).astype(BF)
    g["WinVX"] = _xform_de(np.ascontiguousarray(Win[:, 4096:6144][:, perm])).astype(BF)
    g["binQ"] = _pvec(np.ascontiguousarray(bin_[0:2048][perm]))
    g["binK"] = _pvec(np.ascontiguousarray(bin_[2048:4096][perm]))
    g["binV"] = _pvec(np.ascontiguousarray(bin_[4096:6144][perm]))
    g["WoutR"] = _rhsform(np.ascontiguousarray(
        np.asarray(inputs["L1_attn_W_out"])[perm, :])).astype(BF)
    g["bout_r"] = np.ascontiguousarray(np.asarray(inputs["L1_attn_b_out"])[None, :])
    g["gam"] = _pvec(inputs["L1_attn_gamma"])
    g["bet"] = _pvec(inputs["L1_attn_beta"])
    rw = np.asarray(inputs["L1_ffn_router_w"])
    g["RtX"] = np.ascontiguousarray(
        rw.T.reshape(16, 128, 8).transpose(1, 0, 2)).astype(BF)
    dn = np.asarray(inputs["L1_ffn_experts_down"])
    dall = dn.transpose(1, 0, 2).reshape(2048, 128)
    g["DnX"] = np.ascontiguousarray(
        dall.reshape(16, 128, 128).transpose(1, 0, 2)).astype(BF)
    g["UpA"] = np.ascontiguousarray(
        np.asarray(inputs["L1_ffn_experts_up"]).reshape(128, 2048)).astype(BF)
    g["ident"] = np.eye(128, dtype=np.float32)
    esel = np.zeros((8, 128), dtype=np.float32)
    for ei in range(8):
        esel[ei, ei * 16:(ei + 1) * 16] = 1.0
    g["esel"] = esel.astype(BF)
    g["mask1"] = np.ones((8, 1), dtype=np.float32)
    return g


def stage_core(inputs, cid, g=None):
    c = {}
    x = np.asarray(inputs["L0_attn_x"]).reshape(8192, 2048)
    c["xin"] = np.ascontiguousarray(x[cid * TOK:(cid + 1) * TOK])
    if g is None:
        w1 = np.asarray(inputs["L0_ffn_w1"])
        w2 = np.asarray(inputs["L0_ffn_w2"])
        W1X = np.ascontiguousarray(
            w1.reshape(64, 128, 16, 128).transpose(0, 3, 2, 1))
        W2R = np.ascontiguousarray(w2.T.reshape(64, 128, 2048))
    else:
        W1X, W2R = g["W1X"], g["W2R"]
    gn = 16 * 131072
    c["W1M"] = np.ascontiguousarray(
        W1X.reshape(-1)[cid * gn:(cid + 1) * gn].reshape(16, 128, 1024))
    c["W2M"] = np.ascontiguousarray(
        W2R.reshape(-1)[cid * gn:(cid + 1) * gn].reshape(16, 128, 1024))
    inv = 1.0 / (10000.0 ** (np.arange(0, 1024, 2, dtype=np.float64) / 1024.0))
    pos = np.arange(cid * TOK, (cid + 1) * TOK, dtype=np.float64)
    ang = np.outer(inv, pos)
    c["ropec"] = np.ascontiguousarray(
        np.cos(ang).astype(np.float32).astype(BF)
        .reshape(4, 128, 1024).transpose(1, 0, 2))
    c["ropes"] = np.ascontiguousarray(
        np.sin(ang).astype(np.float32).astype(BF)
        .reshape(4, 128, 1024).transpose(1, 0, 2))
    mp = np.zeros((8, 1), dtype=np.float32)
    mp[:cid] = 1.0
    c["maskp"] = mp
    return c


KEYS_L0 = ["xin", "WqX", "WkX", "WvX", "WoR", "bq", "bk", "bv", "bo_r",
           "W1X", "W2R", "W1M", "W2M", "b1t", "b2_r", "maskp", "ident"]
KEYS_L1 = ["xin", "WinQX", "WinKX", "WinVX", "WoutR", "binQ", "binK", "binV",
           "bout_r", "gam", "bet", "RtX", "DnX", "UpA", "ropec", "ropes",
           "mask1", "ident", "esel"]
KEYS_ALL = sorted(set(KEYS_L0) | set(KEYS_L1))


class Ctx:
    pass


def small_alloc():
    m = {}
    off = SMALL_OFF

    def take(name, n):
        nonlocal off
        m[name] = off
        off += n
    take("stat", 16)
    take("bq", 16); take("bk", 16); take("bv", 16)
    take("binQ", 16); take("binK", 16); take("binV", 16)
    take("b1t", 64)
    take("gam", 16); take("bet", 16)
    take("carry", 18)
    take("offs", 18)
    take("Apart", 32)
    take("Afin", 16)
    take("thr", 4)
    take("ltok", 64)
    take("wful", 64)
    take("tiny", 16)
    take("e8", 8)
    take("ones_r", 128)
    take("invs_r", 128)
    take("ident", 128)
    take("esel", 64)
    take("mask", 4)
    take("onesb", 2)
    take("mean", 4)
    take("half", 2)
    take("lnb", 56)
    assert off <= SLAB_COLS, off
    return m


def f32v(C, off, n, p0=0, p=128):
    return C.slab[p0:p0 + p, off:off + n]


def bf16v(C, off, nbf):
    return C.slab[:, off:off + nbf // 2].bitcast(BF16)


def sm1(C, name, idx=0):
    o = C.sm[name] + idx
    return C.slab[:, o:o + 1]


def bank_read(C, b, tok):
    C.bank_free[b] = U(C.bank_free[b], tok)


def mm_group(C, b, mms, waits=None, out_slice=None):
    P = C.P
    n = len(mms)
    w0 = U(C.bank_free[b], waits)
    tok = None
    for i, (lh, rh) in enumerate(mms):
        tok = P.op("pe",
                   (lambda E, o=out_slice, l=lh, r=rh, st=(i == 0),
                    sp=(i == n - 1):
                    E.matmul(out=o, lhsT=l, rhs=r, start=st, stop=sp)),
                   waits=(w0 if i == 0 else None),
                   inc=(1 if i == n - 1 else 0))
    C.bank_data[b] = tok
    C.bank_free[b] = None
    return tok


def ones_rr(C):
    return f32v(C, C.sm["ones_r"], 128, p=1)


def build_nc(part):
    nc = bass.Bass("TRN2", target_bir_lowering=False, debug=False,
                   detect_race_conditions=RACE_DETECT)
    C = Ctx()
    C.part = part
    io = {}

    def din(name, shape, dt=F32):
        io[name] = nc.dram_tensor(name, list(shape), dt, kind="ExternalInput").ap()

    do_l0 = part in ("L0", "ALL")
    do_l1 = part in ("L1", "ALL")
    din("xin", [TOK, D])
    din("ident", [128, 128])
    if do_l0:
        din("WqX", [16, 128, 16, 128], BF16); din("WkX", [16, 128, 16, 128], BF16)
        din("WvX", [16, 128, 16, 128], BF16); din("WoR", [16, 128, 2048], BF16)
        din("bq", [128, 16]); din("bk", [128, 16]); din("bv", [128, 16])
        din("bo_r", [1, 2048])
        din("W1X", [64, 128, 16, 128]); din("W2R", [64, 128, 2048])
        din("W1M", [16, 128, 1024]); din("W2M", [16, 128, 1024])
        din("b1t", [128, 64]); din("b2_r", [1, 2048])
        din("maskp", [8, 1])
    if do_l1:
        din("WinQX", [16, 128, 16, 128], BF16); din("WinKX", [16, 128, 16, 128], BF16)
        din("WinVX", [16, 128, 16, 128], BF16); din("WoutR", [16, 128, 2048], BF16)
        din("binQ", [128, 16]); din("binK", [128, 16]); din("binV", [128, 16])
        din("bout_r", [1, 2048])
        din("gam", [128, 16]); din("bet", [128, 16])
        din("RtX", [128, 16, 8], BF16); din("DnX", [128, 16, 128], BF16)
        din("UpA", [128, 2048], BF16)
        din("ropec", [128, 4, 1024], BF16); din("ropes", [128, 4, 1024], BF16)
        din("mask1", [8, 1]); din("esel", [8, 128], BF16)
    io["out"] = nc.dram_tensor("out", [TOK, D], F32, kind="ExternalOutput").ap()

    cc = {}
    if do_l0:
        cc["c1i"] = nc.dram_tensor("c1i", [128, 18], F32, kind="Internal")
        cc["c1o"] = nc.dram_tensor("c1o", [1024, 18], F32, kind="Internal")
    if do_l1:
        cc["c2i"] = nc.dram_tensor("c2i", [128, 16], F32, kind="Internal")
        cc["c2o"] = nc.dram_tensor("c2o", [1024, 16], F32, kind="Internal")

    with ExitStack() as ctx:
        e = ctx.enter_context
        slab = e(nc.sbuf_tensor("slab", [128, SLAB_COLS], F32))
        pb = [e(nc.psum_tensor("pb%d" % i, [128, 512], F32)) for i in range(8)]
        P = Prog(nc)
        for s in SEM_NAMES:
            P.sems[s] = e(nc.semaphore("sem_" + s))
        block = e(nc.Block())
        C.nc, C.P, C.slab, C.pb, C.io, C.cc = nc, P, slab, pb, io, cc
        C.sm = small_alloc()
        C.bank_data = [None] * 8
        C.bank_free = [None] * 8
        C.wb = [bf16v(C, WBF_OFF, 2048), bf16v(C, WBF_OFF + 1024, 2048)]
        C.wb_busy = [None, None]
        build_body(C)
        P.finalize(block)
    nc.finalize()
    return nc


def build_body(C):
    P, io, sm = C.P, C.io, C.sm
    do_l0 = C.part in ("L0", "ALL")
    do_l1 = C.part in ("L1", "ALL")

    xin3 = io["xin"].rearrange("(tc p) e -> tc p e", p=128)
    t_xc = []
    for tc8 in range(TC):
        t_xc.append(P.dma("sp", f32v(C, X_OFF + tc8 * 2048, 2048),
                          xin3[tc8], sem="xo%d" % tc8))
    t_x = U(*t_xc)
    C.t_xc = t_xc
    consts = [P.dma("act", f32v(C, sm["ident"], 128), io["ident"], sem="cst")]
    if do_l0:
        for nm in ("bq", "bk", "bv"):
            consts.append(P.dma("act", f32v(C, sm[nm], 16), io[nm], sem="cst"))
        consts.append(P.dma("act", f32v(C, sm["b1t"], 64), io["b1t"], sem="cst"))
        consts.append(P.dma("act", f32v(C, sm["mask"], 1, p=8), io["maskp"], sem="cst"))
    if do_l1:
        for nm in ("binQ", "binK", "binV", "gam", "bet"):
            consts.append(P.dma("act", f32v(C, sm[nm], 16), io[nm], sem="cst"))
        consts.append(P.dma("act", C.slab[0:8, sm["esel"]:sm["esel"] + 64].bitcast(BF16), io["esel"], sem="cst"))
        consts.append(P.dma("act", f32v(C, sm["mask"] + 1, 1, p=8), io["mask1"], sem="cst"))
    consts.append(P.op("dve", lambda E: E.memset(f32v(C, sm["ones_r"], 128, p=1), 1.0)))
    consts.append(P.op("dve", lambda E: E.memset(f32v(C, sm["invs_r"], 128, p=1),
                                                 INVSCALE)))
    consts.append(P.op("dve", lambda E: E.memset(
        C.slab[:, sm["onesb"]:sm["onesb"] + 1].bitcast(BF16), 1.0)))
    consts.append(P.op("dve", lambda E: E.memset(
        C.slab[:, sm["onesb"] + 1:sm["onesb"] + 2], 1.0)))
    consts.append(P.op("dve", lambda E: E.memset(f32v(C, sm["carry"], 16), 0.0)))
    consts.append(P.op("dve", lambda E: E.memset(sm1(C, "half", 0), -0.5)))
    consts.append(P.op("dve", lambda E: E.memset(sm1(C, "half", 1), 0.5)))
    C.t_consts = U(*consts)
    C.t_x = t_x

    if do_l0:
        layer0(C)
    if do_l1:
        layer1(C)

    out3 = io["out"].rearrange("(tc p) e -> tc p e", p=128)
    xdc = getattr(C, "t_xdone_c", None)
    touts = []
    for tc8 in range(TC):
        touts.append(P.dma("sp", out3[tc8], f32v(C, X_OFF + tc8 * 2048, 2048),
                           waits=(xdc[tc8] if xdc else C.t_xdone),
                           sem="xo%d" % tc8))
    P.wait_only("sp", U(*touts))


def ln_stats(C, x_chunk, slot_negmu, slot_rstd, sq_scr, waits):
    P = C.P
    s1 = sm1(C, "tiny", 0)
    s2 = sm1(C, "tiny", 1)
    t1 = P.op("dve", (lambda E, o=s1, i=x_chunk:
                      E.reduce_sum(out=o, in_=i, axis=AX.X)), waits=waits)
    t2 = P.op("act", (lambda E, o=sq_scr, i=x_chunk, a=s2:
                      E.activation(out=o, in_=i, func=AF.Square, accum_out=a)),
              waits=waits)
    mu = sm1(C, "tiny", 2)
    t3 = P.op("dve", (lambda E, o=mu, i=s1:
                      E.tensor_scalar(out=o, in0=i, scalar1=1.0 / D, scalar2=None,
                                      op0=ALU.mult)), waits=t1)
    m2 = sm1(C, "tiny", 3)
    t4 = P.op("dve", (lambda E, o=m2, i=mu: E.tensor_mul(out=o, in0=i, in1=i)),
              waits=t3)
    var = sm1(C, "tiny", 4)
    t5 = P.op("dve", (lambda E, o=var, i=s2:
                      E.tensor_scalar(out=o, in0=i, scalar1=1.0 / D, scalar2=None,
                                      op0=ALU.mult)), waits=t2)
    t6 = P.op("dve", (lambda E, o=var, i=m2:
                      E.tensor_sub(out=o, in0=o, in1=i)), waits=U(t4, t5))
    t7 = P.op("dve", (lambda E, o=var:
                      E.tensor_scalar(out=o, in0=o, scalar1=EPS_LN, scalar2=None,
                                      op0=ALU.add)), waits=t6)
    sd = sm1(C, "tiny", 5)
    t8 = P.op("act", (lambda E, o=sd, i=var:
                      E.activation(out=o, in_=i, func=AF.Sqrt)), waits=t7)
    t9 = P.op("dve", (lambda E, o=slot_rstd, i=sd: E.reciprocal(out=o, in_=i)),
              waits=t8)
    t10 = P.op("dve", (lambda E, o=slot_negmu, i=mu:
                       E.tensor_scalar(out=o, in0=i, scalar1=-1.0, scalar2=None,
                                       op0=ALU.mult)), waits=t3)
    return U(t9, t10)



def ln_batch(C, waits, per_chunk=None):
    P, sm = C.P, C.sm
    lb = sm["lnb"]
    s1 = C.slab[:, lb:lb + 8]
    s2 = C.slab[:, lb + 8:lb + 16]
    negmu = C.slab[:, lb + 16:lb + 24]
    rstd = C.slab[:, lb + 24:lb + 32]
    tmp = C.slab[:, lb + 32:lb + 40]
    dump = f32v(C, SC_OFF, 2048)
    toks = []
    prev_sq = None
    for tc in range(TC):
        xc = f32v(C, X_OFF + tc * 2048, 2048)
        wc = U(waits, per_chunk[tc] if per_chunk else None)
        t1 = P.op("dve", (lambda E, o=s1[:, tc:tc + 1], i=xc:
                          E.reduce_sum(out=o, in_=i, axis=AX.X)), waits=wc)
        t2 = P.op("act", (lambda E, o=dump, i=xc, a=s2[:, tc:tc + 1]:
                          E.activation(out=o, in_=i, func=AF.Square,
                                       accum_out=a)),
                  waits=U(wc, prev_sq))
        prev_sq = t2
        toks.append(U(t1, t2))
    t0 = U(*toks)
    ta = P.op("dve", (lambda E, o=negmu, i=s1:
                      E.tensor_scalar(out=o, in0=i, scalar1=-1.0 / D,
                                      scalar2=None, op0=ALU.mult)), waits=t0)
    tb = P.op("dve", (lambda E, o=tmp, i=negmu:
                      E.tensor_mul(out=o, in0=i, in1=i)), waits=ta)
    tc_ = P.op("dve", (lambda E, o=rstd, i=s2:
                       E.tensor_scalar(out=o, in0=i, scalar1=1.0 / D,
                                       scalar2=None, op0=ALU.mult)), waits=t0)
    td = P.op("dve", (lambda E, o=rstd, i=tmp:
                      E.tensor_sub(out=o, in0=o, in1=i)), waits=U(tb, tc_))
    te = P.op("dve", (lambda E, o=rstd:
                      E.tensor_scalar(out=o, in0=o, scalar1=EPS_LN, scalar2=None,
                                      op0=ALU.add)), waits=td)
    tf = P.op("act", (lambda E, o=rstd:
                      E.activation(out=o, in_=o, func=AF.Sqrt)), waits=te)
    tg = P.op("dve", (lambda E, o=rstd: E.reciprocal(out=o, in_=o)), waits=tf)
    return U(tg, prev_sq)


def transpose_chunk(C, src_f32, dst_bf_off, tc, waits, evac=None):
    P = C.P
    toks = []
    ident = f32v(C, C.sm["ident"], 128)
    dst = bf16v(C, dst_bf_off, 16384).rearrange("p (dc t) -> p dc t", t=1024)
    for grp in range(4):
        b = grp % 2
        w0 = U(C.bank_free[b], waits, C.t_consts)
        tmm = None
        for j in range(4):
            dc = grp * 4 + j
            tmm = P.op("pe",
                       (lambda E, o=C.pb[b][:, j * 128:(j + 1) * 128],
                        i=src_f32[:, dc * 128:(dc + 1) * 128], idn=ident:
                        E.transpose(out=o, in_=i, identity=idn)),
                       waits=(w0 if j == 0 else None), inc=(1 if j == 3 else 0))
        C.bank_data[b] = tmm
        if evac is None:
            dst_ap = dst[:, grp * 4:grp * 4 + 4, tc * 128:(tc + 1) * 128]
            src_ap = C.pb[b][:, 0:512].rearrange("p (dc t) -> p dc t", t=128)
            tev = P.op("dve", (lambda E, o=dst_ap, i=src_ap:
                               E.tensor_copy(out=o, in_=i)), waits=tmm)
        else:
            tes = []
            for j in range(4):
                dc = grp * 4 + j
                s_c, b_c = evac[dc]
                tes.append(P.op("act",
                    (lambda E, o=dst[:, dc, tc * 128:(tc + 1) * 128],
                     i=C.pb[b][:, j * 128:(j + 1) * 128], s_=s_c, bb=b_c:
                     E.activation(out=o, in_=i, func=AF.Identity,
                                  scale=s_, bias=bb)),
                    waits=(U(tmm, C.t_consts) if j == 0 else None)))
            tev = U(*tes)
        C.bank_free[b] = tev
        toks.append(tev)
    return U(*toks)


def layer0(C):
    P, io, sm = C.P, C.io, C.sm
    sc = f32v(C, SC_OFF, 2048)
    xnf = bf16v(C, XNF_OFF, 16384)
    wb, wb_busy = C.wb, C.wb_busy
    tst = ln_batch(C, C.t_consts, per_chunk=C.t_xc)
    negmu8 = C.slab[:, sm["lnb"] + 16:sm["lnb"] + 24]
    rstd8 = C.slab[:, sm["lnb"] + 24:sm["lnb"] + 32]
    nb8 = C.slab[:, sm["lnb"] + 48:sm["lnb"] + 56]
    tnb = P.op("dve", (lambda E, o=nb8, i=negmu8, j=rstd8:
                       E.tensor_mul(out=o, in0=i, in1=j)), waits=tst)
    xnf_toks = []
    for tc in range(TC):
        xc = f32v(C, X_OFF + tc * 2048, 2048)
        buf = sc if tc % 2 == 0 else f32v(C, SP2_OFF, 2048)
        tap = P.op("act", (lambda E, o=buf, i=xc,
                           a=rstd8[:, tc:tc + 1], b=nb8[:, tc:tc + 1]:
                           E.activation(out=o, in_=i, func=AF.Identity,
                                        scale=a, bias=b)),
                   waits=U(tnb, xnf_toks[tc - 2] if tc >= 2 else None))
        xnf_toks.append(transpose_chunk(C, buf, XNF_OFF, tc, tap))
    C.t_xnf = U(*xnf_toks)

    wfp = [f32v(C, ROWS_OFF, 1024), f32v(C, ROWS_OFF + 1024, 1024),
           f32v(C, WF32_OFF, 1024)]
    wfp_busy = [None, None, None]
    C.wfp, C.wfp_busy = wfp, wfp_busy
    NGM = 16
    mean_toks = []
    absdump = f32v(C, SLAB2_OFF + 512, 1024)
    w1run = C.slab[:, sm["carry"] + 16:sm["carry"] + 17]
    w2run = C.slab[:, sm["carry"] + 17:sm["carry"] + 18]
    wtmp = [sm1(C, "tiny", 11), sm1(C, "tiny", 12)]
    t_wz = P.op("dve", lambda E: E.memset(
        C.slab[:, sm["carry"] + 16:sm["carry"] + 18], 0.0))
    mean_state = {"i": 0, "p": 0, "pa": None,
                  "prun": [t_wz, t_wz], "pend": []}

    def pump_load(n):
        for _ in range(n):
            i = mean_state["i"]
            if i >= 2 * NGM:
                return
            mean_state["i"] += 1
            p = mean_state["p"]
            mean_state["p"] = (p + 1) % 3
            mi, gidx = (0, i) if i < NGM else (1, i - NGM)
            src = io["W1M"] if mi == 0 else io["W2M"]
            td = P.dma("sp", wfp[p], src[gidx], waits=wfp_busy[p],
                       sem="wfp%d" % p)
            mean_state["pend"].append((p, td, mi, i))

    def pump_drain():
        for (p, td, mi, i) in mean_state["pend"]:
            tb = i % 2
            th = P.op("act", (lambda E, i_=wfp[p], o=absdump, a=wtmp[tb]:
                              E.activation(out=o, in_=i_, func=AF.Abs,
                                           accum_out=a)),
                      waits=U(td, mean_state["pa"], mean_state["prun"][tb]))
            mean_state["pa"] = th
            run = w1run if mi == 0 else w2run
            tr2 = P.op("dve", (lambda E, o=run, i_=wtmp[tb]:
                               E.tensor_add(out=o, in0=o, in1=i_)),
                       waits=U(th, *mean_state["prun"]))
            mean_state["prun"][tb] = tr2
            wfp_busy[p] = th
            mean_toks.append(tr2)
        mean_state["pend"] = []

    zeros = f32v(C, SLAB2_OFF, 512)
    tz = P.op("dve", lambda E: E.memset(zeros, 0.0))
    kvv = f32v(C, KV_OFF, 16384)
    rslots = [WBF_OFF, WBF_OFF + 512, WBF_OFF + 1024, WBF_OFF + 1536,
              SLAB2_OFF + 2816, SLAB2_OFF + 3328]
    rbusy = [None] * 6
    ridx = [0]

    def wslot():
        i = ridx[0] % 6
        ridx[0] += 1
        return i, bf16v(C, rslots[i], 1024), "wk%d" % i

    def wload(WX, ec):
        views = []
        for hh in range(2):
            i, vw, sname = wslot()
            td = P.dma("pool", vw,
                       io[WX][ec, :, hh * 8:(hh + 1) * 8, :].rearrange(
                           "p dc f -> p (dc f)"),
                       waits=rbusy[i], sem=sname)
            views.append((i, vw, td))
        return views

    def wtiles(views):
        mms = []
        tds = []
        for hh, (i, vw, td) in enumerate(views):
            tds.append(td)
            for dcl in range(8):
                mms.append(vw[:, dcl * 128:(dcl + 1) * 128])
        return mms, U(*tds)

    gi = 0
    scan_prev = {}
    t_scan_all = []
    sc_reads = []
    prev_exp = None
    prev_t4 = None
    prev_tu = [None, None]
    for ec in range(16):
        pump_load(2)
        kviews = wload("WkX", ec)
        vviews = wload("WvX", ec)
        ktiles, tdk = wtiles(kviews)
        vtiles, tdv = wtiles(vviews)
        phi = [None, None]
        k_reads = []
        k_mms = []
        exps = []
        for s in range(2):
            bk_k = 2 + 2 * s
            mms = [(ktiles[dc],
                    xnf[:, dc * 1024 + s * 512: dc * 1024 + s * 512 + 512])
                   for dc in range(16)]
            tk = mm_group(C, bk_k, mms, waits=U(tdk, C.t_xnf),
                          out_slice=C.pb[bk_k][:, 0:512])
            k_mms.append(tk)
            bkc = sm1(C, "bk", ec)
            mslot = sc[:, s * 2048 // 8: s * 256 + 256] if False else                 sc[:, 0:512] if s == 0 else sc[:, 512:1024]
            eslot = sc[:, 1536 + 0:1536 + 256].bitcast(BF16) if False else None
            eslot = (sc[:, 0:512], sc[:, 512:1024])[s]
            t1 = P.op("dve", (lambda E, o=mslot, i=C.pb[bk_k][:, 0:512], a=bkc:
                              E.tensor_scalar(out=o, in0=i, scalar1=a, scalar2=0.0,
                                              op0=ALU.add, op1=ALU.min)),
                      waits=U(tk, C.t_consts, prev_exp, prev_t4))
            t2 = P.op("act", (lambda E, o=eslot, i=mslot:
                              E.activation(out=o, in_=i, func=AF.Exp)),
                      waits=U(t1,))
            prev_exp = t2
            exps.append((tk, t2))
        for s in range(2):
            bk_k = 2 + 2 * s
            tk, t2 = exps[s]
            bkc = sm1(C, "bk", ec)
            eslot = (sc[:, 0:512], sc[:, 512:1024])[s]
            phis = sc[:, 1024 + s * 512: 1536 + s * 512]
            t3 = P.op("dve", (lambda E, o=phis, i=C.pb[bk_k][:, 0:512], a=bkc:
                              E.tensor_scalar(out=o, in0=i, scalar1=a, scalar2=0.0,
                                              op0=ALU.add, op1=ALU.max)),
                      waits=U(tk, prev_tu[s]))
            t4 = P.op("dve", (lambda E, o=phis, i=eslot:
                              E.tensor_add(out=o, in0=o, in1=i)), waits=U(t2, t3))
            prev_t4 = t4
            bank_read(C, bk_k, U(t3, t4))
            phi[s] = (phis, t4)
            k_reads.append(t4)
        for (i, vw, td) in kviews:
            rbusy[i] = U(*k_mms)
        v_reads = []
        for s in range(2):
            bk_v = 3 + 2 * s
            mms = [(vtiles[dc],
                    xnf[:, dc * 1024 + s * 512: dc * 1024 + s * 512 + 512])
                   for dc in range(16)]
            tv = mm_group(C, bk_v, mms, waits=U(tdv, C.t_xnf),
                          out_slice=C.pb[bk_v][:, 0:512])
            v_reads.append(tv)
            kvsl = kvv[:, ec * 1024 + s * 512: ec * 1024 + s * 512 + 512]
            bvc = sm1(C, "bv", ec)
            phis, tphi = phi[s]
            tu = P.op("dve", (lambda E, o=kvsl, i=C.pb[bk_v][:, 0:512], a=bvc,
                              h=phis:
                              E.scalar_tensor_tensor(out=o, in0=i, scalar=a, in1=h,
                                                     op0=ALU.add, op1=ALU.mult)),
                      waits=U(tv, tphi, C.t_consts))
            prev_tu[s] = tu
            bank_read(C, bk_v, tu)
            sc_reads.append(tu)
            csl = C.slab[:, sm["carry"] + ec: sm["carry"] + ec + 1]
            tsc = P.op("dve", (lambda E, o=kvsl, z=zeros, ini=csl:
                               E.tensor_tensor_scan(out=o, data0=o, data1=z,
                                                    initial=ini, op0=ALU.add,
                                                    op1=ALU.add)),
                       waits=U(tu, tz, C.t_consts, scan_prev.get(ec)))
            tcar = P.op("dve", (lambda E, o=csl,
                                i=kvv[:, ec * 1024 + s * 512 + 511:
                                      ec * 1024 + s * 512 + 512]:
                                E.tensor_copy(out=o, in_=i)), waits=tsc)
            scan_prev[ec] = tcar
            t_scan_all.append(tcar)
        for (i, vw, td) in vviews:
            rbusy[i] = U(*v_reads)
        pump_drain()
    C.t_scan = U(*t_scan_all)
    C.t_sc_attn = U(*sc_reads)
    t_meandone = U(*mean_toks)

    carry = f32v(C, sm["carry"], 18)
    c1i, c1o = C.cc["c1i"], C.cc["c1o"]
    td = P.dma("pool", c1i.ap(), carry, waits=U(C.t_scan, t_meandone),
               sem="cci")
    if SIM_NOCC:
        tl = []
        for r in range(8):
            tl.append(P.dma("pool", c1o.ap()[r * 128:(r + 1) * 128, :],
                            c1i.ap(), waits=td, sem="agl"))
        C.t_cc1 = U(*tl)
    else:
        C.t_cc1 = P.op("pool", (lambda E: E.collective_compute(
            "AllGather", ALU.bypass, replica_groups=RG,
            ins=[c1i.ap().opt()], outs=[c1o.ap().opt()])), waits=td)

    s_ag = f32v(C, SC_OFF, 2304, p=8)
    tld = P.dma("pool", s_ag, c1o.ap().rearrange("(r p) c -> r (p c)", p=128),
                waits=U(C.t_cc1, C.t_sc_attn), sem="agl")
    maskp = C.slab[0:8, sm["mask"]:sm["mask"] + 1]
    ones8 = C.slab[0:8, sm["onesb"] + 1:sm["onesb"] + 2]
    w0 = U(tld, C.bank_free[6], C.t_consts)
    tmo = None
    s_ag3 = s_ag.rearrange("r (p c) -> r p c", c=18)
    for cch in range(18):
        rr = maskp if cch < 16 else ones8
        tmo = P.op("pe",
                   (lambda E, o=C.pb[6][:, cch:cch + 1],
                    l=s_ag3[:, :, cch], r=rr:
                    E.matmul(out=o, lhsT=l, rhs=r, start=True, stop=True)),
                   waits=(w0 if cch == 0 else None), inc=(1 if cch == 17 else 0))
    C.bank_data[6] = tmo
    offs = f32v(C, sm["offs"], 18)
    tof = P.op("dve", (lambda E, o=offs, i=C.pb[6][:, 0:18]:
                       E.tensor_copy(out=o, in_=i)), waits=tmo)
    C.bank_free[6] = tof
    C.t_offs = tof
    C.t_agread = U(tof, tld)
    wfp_busy[2] = U(wfp_busy[2], C.t_agread)

    onesc = C.slab[:, sm["onesb"] + 1:sm["onesb"] + 2]
    w1vec = C.slab[:, sm["offs"] + 16:sm["offs"] + 17]
    w2vec = C.slab[:, sm["offs"] + 17:sm["offs"] + 18]
    w0 = U(C.bank_free[6], C.t_consts)
    tmm = P.op("pe", (lambda E, o=C.pb[6][0:1, 0:1], l=w1vec, r=onesc:
                      E.matmul(out=o, lhsT=l, rhs=r, start=True, stop=True)),
               waits=w0)
    C.bank_data[6] = tmm
    mean_s = C.slab[0:1, sm["mean"]:sm["mean"] + 1]
    t1m = P.op("dve", (lambda E, o=mean_s, i=C.pb[6][0:1, 0:1]:
                       E.tensor_scalar(out=o, in0=i, scalar1=1.0 / (DFF * D),
                                       scalar2=EPS_BIT, op0=ALU.mult,
                                       op1=ALU.max)), waits=tmm)
    C.bank_free[6] = t1m
    t2m = P.op("dve", (lambda E, o=mean_s:
                       E.tensor_scalar(out=o, in0=o, scalar1=0.5, scalar2=None,
                                       op0=ALU.mult)), waits=t1m)
    w0 = U(C.bank_free[7], t2m, C.t_consts)
    tbc = P.op("pe", (lambda E, o=C.pb[7][:, 0:1], l=ones_rr(C), r=mean_s:
                      E.matmul(out=o, lhsT=l, rhs=r, start=True, stop=True)),
               waits=w0)
    C.bank_data[7] = tbc
    thrp1 = C.slab[:, sm["thr"]:sm["thr"] + 1]
    thrn1 = C.slab[:, sm["thr"] + 1:sm["thr"] + 2]
    t3m = P.op("dve", (lambda E, o=thrp1, i=C.pb[7][:, 0:1]:
                       E.tensor_copy(out=o, in_=i)), waits=tbc)
    C.bank_free[7] = t3m
    t4m = P.op("dve", (lambda E, o=thrn1, i=thrp1:
                       E.tensor_scalar(out=o, in0=i, scalar1=-1.0, scalar2=None,
                                       op0=ALU.mult)), waits=t3m)
    C.t_thr1 = U(t3m, t4m)

    w0 = U(C.bank_free[6], C.t_consts)
    tmm2 = P.op("pe", (lambda E, o=C.pb[6][0:1, 0:1], l=w2vec, r=onesc:
                       E.matmul(out=o, lhsT=l, rhs=r, start=True, stop=True)),
                waits=w0)
    C.bank_data[6] = tmm2
    mean2_s = C.slab[0:1, sm["mean"] + 1: sm["mean"] + 2]
    t1i = P.op("dve", (lambda E, o=mean2_s, i=C.pb[6][0:1, 0:1]:
                       E.tensor_scalar(out=o, in0=i, scalar1=1.0 / (DFF * D),
                                       scalar2=EPS_BIT, op0=ALU.mult,
                                       op1=ALU.max)), waits=tmm2)
    C.bank_free[6] = t1i
    t2i = P.op("dve", (lambda E, o=mean2_s:
                       E.reciprocal(out=o, in_=o)), waits=t1i)
    w0 = U(C.bank_free[7], t2i, C.t_consts)
    tbc2 = P.op("pe", (lambda E, o=C.pb[7][:, 0:1], l=ones_rr(C), r=mean2_s:
                       E.matmul(out=o, lhsT=l, rhs=r, start=True, stop=True)),
                waits=w0)
    C.bank_data[7] = tbc2
    inv2s = C.slab[:, sm["thr"] + 2:sm["thr"] + 3]
    t3i = P.op("dve", (lambda E, o=inv2s, i=C.pb[7][:, 0:1]:
                       E.tensor_copy(out=o, in_=i)), waits=tbc2)
    C.bank_free[7] = t3i
    C.t_thr = U(C.t_thr1, t3i)

    rbusy = list(rbusy)
    ridx = [0]

    def ctx_tile(dc, tch):
        s, i0 = tch // 4, (tch % 4) * 64
        return bf16v(C, KV_OFF + dc * 1024 + s * 512 + i0, 128)
    prev_exp = None
    prev_t4q = None
    prev_t5 = None
    ctx_t = []
    for ec in range(16):
        qtiles = []
        tds = []
        for hh in range(2):
            i = ridx[0] % 6
            ridx[0] += 1
            vw = bf16v(C, rslots[i], 1024)
            tdg = P.dma("sp", vw,
                        io["WqX"][ec, :, hh * 8:(hh + 1) * 8, :].rearrange(
                            "p dc f -> p (dc f)"),
                        waits=rbusy[i], sem="wq%d" % i)
            tds.append((i, tdg))
            for dcl in range(8):
                qtiles.append(vw[:, dcl * 128:(dcl + 1) * 128])
        tdq = U(*[t for _, t in tds])
        q_mms = []
        for s in range(2):
            bq_b = (0, 1, 2, 3, 4, 5)[(2 * ec + s) % 6]
            mms = [(qtiles[dc],
                    xnf[:, dc * 1024 + s * 512: dc * 1024 + s * 512 + 512])
                   for dc in range(16)]
            tq = mm_group(C, bq_b, mms, waits=U(tdq, C.t_xnf),
                          out_slice=C.pb[bq_b][:, 0:512])
            q_mms.append(tq)
            bqc = sm1(C, "bq", ec)
            mslot = sc[:, 0:512]; eslot = sc[:, 512:1024]; phis = sc[:, 1024:1536]
            t1 = P.op("dve", (lambda E, o=mslot, i=C.pb[bq_b][:, 0:512], a=bqc:
                              E.tensor_scalar(out=o, in0=i, scalar1=a, scalar2=0.0,
                                              op0=ALU.add, op1=ALU.min)),
                      waits=U(tq, C.t_agread, prev_exp))
            t2 = P.op("act", (lambda E, o=eslot, i=mslot:
                              E.activation(out=o, in_=i, func=AF.Exp)),
                      waits=U(t1, prev_t4q))
            prev_exp = t2
            t3 = P.op("dve", (lambda E, o=phis, i=C.pb[bq_b][:, 0:512], a=bqc:
                              E.tensor_scalar(out=o, in0=i, scalar1=a, scalar2=0.0,
                                              op0=ALU.add, op1=ALU.max)),
                      waits=U(tq, prev_t5))
            t4 = P.op("dve", (lambda E, o=phis, i=eslot:
                              E.tensor_add(out=o, in0=o, in1=i)), waits=U(t2, t3))
            prev_t4q = t4
            bank_read(C, bq_b, U(t1, t3))
            kvsl = kvv[:, ec * 1024 + s * 512: ec * 1024 + s * 512 + 512]
            ctxsl = kvsl.bitcast(BF16)[:, 0:512]
            oc = C.slab[:, sm["offs"] + ec: sm["offs"] + ec + 1]
            t5 = P.op("dve", (lambda E, o=ctxsl, i=kvsl, a=oc, h=phis:
                              E.scalar_tensor_tensor(out=o, in0=i, scalar=a,
                                                     in1=h, op0=ALU.add,
                                                     op1=ALU.mult)),
                      waits=U(t4, C.t_offs))
            prev_t5 = t5
            ctx_t.append(t5)
        for i, _ in tds:
            rbusy[i] = U(*q_mms)
    t_ctx = U(*ctx_t)
    bo_seg = f32v(C, SLAB2_OFF, 512, p=1)
    invs = f32v(C, sm["invs_r"], 128, p=1)
    t_boseg = None
    t_boseg_rd = None
    wo_x = [None] * 8
    t_wo_last = []
    for eg in range(4):
        t_boseg = P.dma("act", bo_seg, io["bo_r"][:, eg * 512:(eg + 1) * 512],
                        waits=U(t_ctx, t_boseg_rd), sem="brow")
        for ecp in range(8):
            i = ridx[0] % 6
            ridx[0] += 1
            vw = bf16v(C, rslots[i], 1024).rearrange("p (d e) -> p d e", e=512)
            tdw = P.dma("sp", vw,
                        io["WoR"][ecp * 2:ecp * 2 + 2, :, eg * 512:(eg + 1) * 512]
                        .rearrange("d p e -> p d e"),
                        waits=rbusy[i], sem="wq%d" % i)
            last = None
            for dcl in range(2):
                dc = ecp * 2 + dcl
                for tch in range(8):
                    st = (dc == 0)
                    w = U(C.bank_free[tch] if st else None,
                          t_ctx if st else None, tdw)
                    last = P.op("pe",
                        (lambda E, o=C.pb[tch][:, 0:512],
                         l=ctx_tile(dc, tch),
                         r=vw[:, dcl, :], s_=st:
                         E.matmul(out=o, lhsT=l, rhs=r, start=s_, stop=False)),
                        waits=w, inc=(1 if tch == 7 else 0))
            rbusy[i] = last
        seg_rd = []
        for tch in range(8):
            tokmm = P.op("pe", (lambda E, o=C.pb[tch][:, 0:512], l=invs,
                                r=bo_seg:
                                E.matmul(out=o, lhsT=l, rhs=r, start=False,
                                         stop=True)),
                         waits=U(t_boseg, C.t_consts), inc=1)
            seg_rd.append(tokmm)
            C.bank_data[tch] = tokmm
            xsl = f32v(C, X_OFF + tch * 2048 + eg * 512, 512)
            tev = P.op("dve", (lambda E, o=xsl, i=C.pb[tch][:, 0:512]:
                               E.scalar_tensor_tensor(out=o, in0=i, scalar=SCALE,
                                                      in1=o, op0=ALU.mult,
                                                      op1=ALU.add)),
                       waits=U(tokmm, C.t_xnf, wo_x[tch]))
            C.bank_free[tch] = tev
            wo_x[tch] = tev
            t_wo_last.append(tev)
        t_boseg_rd = U(*seg_rd)
    t_wo_done = U(*t_wo_last)
    C.t_ffn_x = wo_x
    C.t_hfree0 = t_wo_done
    C.t_sc_q = U(t_ctx, t_wo_done)
    C.t_twos = P.op("dve", lambda E: E.memset(invs, 2.0), waits=t_wo_done)

    if DBG_STOP == "wo":
        C.t_xdone = t_wo_done
        C.t_rows_free = C.t_xdone
        C.t_kv_read = U(t_ctx, t_wo_done)
        return
    ffn(C)
    C.t_xdone = U(*C.t_ffn_x)
    if DBG_STOP == "ffn":
        pass
    C.t_rows_free = C.t_xdone
    C.t_kv_read = U(t_ctx, t_wo_done)


def ffn(C):
    P, io, sm = C.P, C.io, C.sm
    xnf = bf16v(C, XNF_OFF, 16384)
    wfp = list(C.wfp) + [f32v(C, KV_OFF + 3072, 1024),
                         f32v(C, KV_OFF + 4096, 1024)]
    wfp_busy = list(C.wfp_busy) + [C.t_sc_q, C.t_sc_q]
    wbX = [bf16v(C, KV_OFF + 5120, 2048), bf16v(C, KV_OFF + 6144, 2048)]
    wbX_busy = [C.t_sc_q, C.t_sc_q]
    f2w = [bf16v(C, WBF_OFF + i * 512, 1024) for i in range(4)] + \
          [bf16v(C, KV_OFF + 1024 + i * 512, 1024) for i in range(4)]
    f2w_busy = [U(C.wb_busy[0], C.t_sc_q), U(C.wb_busy[0], C.t_sc_q),
                U(C.wb_busy[1], C.t_sc_q), U(C.wb_busy[1], C.t_sc_q),
                C.t_sc_q, C.t_sc_q, C.t_sc_q, C.t_sc_q]
    thrp1 = C.slab[:, sm["thr"]:sm["thr"] + 1]
    thrn1 = C.slab[:, sm["thr"] + 1:sm["thr"] + 2]
    inv2 = C.slab[:, sm["thr"] + 2:sm["thr"] + 3]
    hslab = bf16v(C, SLAB2_OFF, 8192)
    bscrF = bf16v(C, KV_OFF + 512, 1024)
    sgn = [bf16v(C, SC_OFF + i * 512, 1024) for i in range(4)]
    sgn_busy = [C.t_sc_q] * 4
    b2seg = f32v(C, KV_OFF, 512, p=1)

    st_ = {"gfp": 0, "prev_tc": C.t_sc_q, "t_b2seg_rd": None,
           "t1": {}, "t2": {}}
    t_hfree = C.t_hfree0

    def wfp_slot():
        pf = st_["gfp"] % 5
        st_["gfp"] += 1
        return pf

    def tern1(g, fcl):
        fc = g * 8 + fcl
        pw = fcl % 2
        w_prev = wbX_busy[pw]
        tern_tok = []
        for hh in range(2):
            pf = wfp_slot()
            hsl_in = io["W1X"][fc, :, hh * 8:(hh + 1) * 8, :].rearrange(
                "p dc f -> p (dc f)")
            td = P.dma("sp", wfp[pf], hsl_in, waits=wfp_busy[pf],
                       sem="wfp%d" % pf)
            dst = wbX[pw][:, hh * 1024:(hh + 1) * 1024]
            ta = P.op("dve", (lambda E, o=dst, i=wfp[pf], a=thrp1:
                              E.tensor_scalar(out=o, in0=i, scalar1=a,
                                              scalar2=None, op0=ALU.is_gt)),
                      waits=U(td, C.t_thr, w_prev))
            tb = P.op("dve", (lambda E, o=bscrF, i=wfp[pf], a=thrn1:
                              E.tensor_scalar(out=o, in0=i, scalar1=a,
                                              scalar2=None, op0=ALU.is_lt)),
                      waits=U(td, C.t_thr, st_["prev_tc"]))
            tc_ = P.op("dve", (lambda E, o=dst, i=bscrF:
                               E.tensor_sub(out=o, in0=o, in1=i)),
                       waits=U(ta, tb))
            st_["prev_tc"] = tc_
            wfp_busy[pf] = U(ta, tb)
            tern_tok.append(tc_)
        wbX_busy[pw] = None
        st_["t1"][(g, fcl)] = U(*tern_tok)

    def tern2(g, eg):
        toks = []
        for pair in range(4):
            fc = g * 8 + pair * 2
            pf = wfp_slot()
            gr = io["W2R"][fc:fc + 2, :, eg * 512:(eg + 1) * 512].rearrange(
                "f p e -> p f e")
            td = P.dma("sp", wfp[pf].rearrange("p (f e) -> p f e", e=512),
                       gr, waits=wfp_busy[pf], sem="wfp%d" % pf)
            ws = 4 * (eg % 2) + pair
            sa, sb = 2 * (pair % 2), 2 * (pair % 2) + 1
            ts1 = P.op("act", (lambda E, o=sgn[sa], i=wfp[pf], a=inv2,
                               b=sm1(C, "half", 0):
                               E.activation(out=o, in_=i, func=AF.Sign,
                                            scale=a, bias=b)),
                       waits=U(td, C.t_thr, sgn_busy[sa]))
            ts2 = P.op("act", (lambda E, o=sgn[sb], i=wfp[pf], a=inv2,
                               b=sm1(C, "half", 1):
                               E.activation(out=o, in_=i, func=AF.Sign,
                                            scale=a, bias=b)),
                       waits=U(td, C.t_thr, sgn_busy[sb]))
            tsum = P.op("dve", (lambda E, o=f2w[ws], i=sgn[sa],
                                j=sgn[sb]:
                                E.tensor_add(out=o, in0=i, in1=j)),
                        waits=U(ts1, ts2, f2w_busy[ws]))
            sgn_busy[sa] = tsum
            sgn_busy[sb] = tsum
            wfp_busy[pf] = U(ts1, ts2)
            f2w_busy[ws] = None
            toks.append((ws, tsum))
        st_["t2"][(g, eg)] = toks

    def mm2(g, eg, t_h, ev_all):
        toks = st_["t2"].pop((g, eg))
        if g == 0:
            tseg = P.dma("act", b2seg, io["b2_r"][:, eg * 512:(eg + 1) * 512],
                         waits=U(C.t_thr, st_["t_b2seg_rd"], C.t_sc_q),
                         sem="b2s")
        for pair in range(4):
            ws, tsum = toks[pair]
            dst = f2w[ws]
            last = None
            for half in range(2):
                fcl = pair * 2 + half
                laststop = (fcl == 7 and g > 0)
                for tch in range(8):
                    stt = (fcl == 0)
                    w = U(C.bank_free[tch] if stt else None,
                          t_h if stt else None, tsum)
                    last = P.op("pe",
                        (lambda E, o=C.pb[tch][:, 0:512],
                         l=hslab[:, fcl * 1024 + tch * 128:
                                 fcl * 1024 + (tch + 1) * 128],
                         r=dst[:, half * 512:(half + 1) * 512],
                         s_=stt, sp=laststop:
                         E.matmul(out=o, lhsT=l, rhs=r, start=s_, stop=sp)),
                        waits=w, inc=(1 if tch == 7 else 0))
            f2w_busy[ws] = last
        seg_rd = []
        for tch in range(8):
            if g == 0:
                tokb = P.op("pe", (lambda E, o=C.pb[tch][:, 0:512],
                                   l=f32v(C, sm["invs_r"], 128, p=1),
                                   r=b2seg:
                                   E.matmul(out=o, lhsT=l, rhs=r, start=False,
                                            stop=True)),
                            waits=U(tseg, C.t_twos), inc=1)
                seg_rd.append(tokb)
            else:
                tokb = last
            C.bank_data[tch] = tokb
            xsl = f32v(C, X_OFF + tch * 2048 + eg * 512, 512)
            tev = P.op("dve", (lambda E, o=xsl, i=C.pb[tch][:, 0:512]:
                               E.scalar_tensor_tensor(out=o, in0=i, scalar=0.5,
                                                      in1=o, op0=ALU.mult,
                                                      op1=ALU.add)),
                       waits=U(tokb, C.t_ffn_x[tch]))
            C.bank_free[tch] = tev
            C.t_ffn_x[tch] = tev
            ev_all.append(tev)
        if g == 0:
            st_["t_b2seg_rd"] = U(*seg_rd)
    tern1(0, 0)
    tern1(0, 1)
    for g in range(8):
        h_toks = []
        for fcl in range(8):
            fc = g * 8 + fcl
            pw = fcl % 2
            ttern = st_["t1"].pop((g, fcl))
            for s in range(2):
                b = (2 * fcl + s) % 4
                mms = [(wbX[pw][:, dc * 128:(dc + 1) * 128],
                        xnf[:, dc * 1024 + s * 512: dc * 1024 + s * 512 + 512])
                       for dc in range(16)]
                tm = mm_group(C, b, mms, waits=U(ttern, C.t_xnf),
                              out_slice=C.pb[b][:, 0:512])
                wbX_busy[pw] = U(wbX_busy[pw], tm)
                hsl = hslab[:, fcl * 1024 + s * 512: fcl * 1024 + s * 512 + 512]
                b1c = C.slab[:, sm["b1t"] + fc: sm["b1t"] + fc + 1]
                tg = P.op("act", (lambda E, o=hsl, i=C.pb[b][:, 0:512], a=b1c:
                                  E.activation(out=o, in_=i, func=AF.Gelu, bias=a)),
                          waits=U(tm, C.t_consts, t_hfree))
                bank_read(C, b, tg)
                h_toks.append(tg)
            if fcl < 6:
                tern1(g, fcl + 2)
            if fcl == 5:
                tern2(g, 0)
        t_h = U(*h_toks)
        ev_all = []
        tern2(g, 1)
        mm2(g, 0, t_h, ev_all)
        tern2(g, 2)
        mm2(g, 1, t_h, ev_all)
        tern2(g, 3)
        mm2(g, 2, t_h, ev_all)
        if g < 7:
            tern1(g + 1, 0)
            tern1(g + 1, 1)
        mm2(g, 3, t_h, ev_all)
        t_hfree = U(*ev_all)
    C.wb_busy[0] = U(f2w_busy[0], f2w_busy[1])
    C.wb_busy[1] = U(f2w_busy[2], f2w_busy[3])
    C.t_sc_free = t_hfree
    C.t_slab2_free = t_hfree
    C.t_rows_free_ffn = t_hfree


def layer1(C):
    P, io, sm = C.P, C.io, C.sm
    sc = f32v(C, SC_OFF, 2048)
    sc2 = f32v(C, SC2_OFF, 2048)
    sq = f32v(C, SP2_OFF, 2048)
    wb, wb_busy = C.wb, C.wb_busy
    if C.part == "ALL":
        base_wait = C.t_xdone
        wkv = C.t_kv_read
        t_slab2 = C.t_xdone
        t_sc_free = C.t_xdone
        t_rows = C.t_rows_free
    else:
        base_wait = U(C.t_x, C.t_consts)
        wkv = None
        t_slab2 = None
        t_sc_free = None
        t_rows = C.t_consts
    C.t_kv_l1_free = wkv

    if C.part == "ALL":
        tst = ln_batch(C, C.t_consts, per_chunk=C.t_ffn_x)
    else:
        tst = ln_batch(C, U(base_wait, C.t_consts))
    lb = sm["lnb"]
    negmu8 = C.slab[:, lb + 16:lb + 24]
    rstd8 = C.slab[:, lb + 24:lb + 32]
    tmp8 = C.slab[:, lb + 32:lb + 40]
    rstd2_8 = C.slab[:, lb + 40:lb + 48]
    zscale8 = C.slab[:, sm["stat"]:sm["stat"] + 8]
    mA8 = C.slab[:, sm["stat"] + 8:sm["stat"] + 16]
    ta = P.op("dve", (lambda E, o=tmp8, i=rstd8:
                      E.tensor_mul(out=o, in0=i, in1=i)), waits=tst)
    tb = P.op("dve", (lambda E, o=rstd2_8, i=tmp8:
                      E.tensor_scalar(out=o, in0=i, scalar1=-EPS_LN,
                                      scalar2=1.0 + EPS_LN, op0=ALU.mult,
                                      op1=ALU.add)), waits=ta)
    tc0 = P.op("act", (lambda E, o=rstd2_8:
                       E.activation(out=o, in_=o, func=AF.Sqrt)), waits=tb)
    td = P.op("dve", (lambda E, o=rstd2_8: E.reciprocal(out=o, in_=o)),
              waits=tc0)
    te = P.op("dve", (lambda E, o=zscale8, i=rstd8, j=rstd2_8:
                      E.tensor_mul(out=o, in0=i, in1=j)), waits=td)
    tf = P.op("dve", (lambda E, o=mA8, i=rstd8:
                      E.tensor_scalar(out=o, in0=i, scalar1=2.0, scalar2=1.0,
                                      op0=ALU.mult, op1=ALU.add)), waits=tst)
    tg = P.op("dve", (lambda E, o=tmp8, i=negmu8, j=rstd8:
                      E.tensor_mul(out=o, in0=i, in1=j)), waits=U(tst, te))
    th = P.op("dve", (lambda E, o=tmp8:
                      E.tensor_scalar(out=o, in0=o, scalar1=2.0, scalar2=None,
                                      op0=ALU.mult)), waits=tg)
    tsc_all = U(te, tf, th)
    nb8 = C.slab[:, sm["lnb"] + 48:sm["lnb"] + 56]
    tnb = P.op("dve", (lambda E, o=nb8, i=negmu8, j=rstd8:
                       E.tensor_mul(out=o, in0=i, in1=j)), waits=tst)
    gb = [(C.slab[:, sm["gam"] + dc: sm["gam"] + dc + 1],
           C.slab[:, sm["bet"] + dc: sm["bet"] + dc + 1]) for dc in range(16)]
    xnf_toks, znf_toks = [], []
    tap_toks, tz_toks = [], []
    C.t_xup = [None] * 8
    for tc in range(TC):
        xc = f32v(C, X_OFF + tc * 2048, 2048)
        bx = sc if tc % 2 == 0 else sq
        bz = sc2 if tc % 2 == 0 else f32v(C, TAB_OFF, 2048)
        tap = P.op("act", (lambda E, o=bx, i=xc, a=rstd8[:, tc:tc + 1],
                           b=nb8[:, tc:tc + 1]:
                           E.activation(out=o, in_=i, func=AF.Identity,
                                        scale=a, bias=b)),
                   waits=U(tnb, t_sc_free, wkv,
                           xnf_toks[tc - 2] if tc >= 2 else None))
        tap_toks.append(tap)
        xnf_toks.append(transpose_chunk(C, bx, XNF_OFF, tc, tap))
        tz = P.op("dve", (lambda E, o=bz, i=xc, a=negmu8[:, tc:tc + 1],
                          b=zscale8[:, tc:tc + 1]:
                          E.tensor_scalar(out=o, in0=i, scalar1=a, scalar2=b,
                                          op0=ALU.add, op1=ALU.mult)),
                  waits=U(tsc_all, wkv,
                          znf_toks[tc - 2] if tc >= 2 else None))
        tz_toks.append(tz)
        znf_toks.append(transpose_chunk(C, bz, ZNF_OFF, tc, tz, evac=gb))
    C.t_xnf = U(*xnf_toks)
    def emit_xup():
        for tc in range(TC):
            xc = f32v(C, X_OFF + tc * 2048, 2048)
            tx = P.op("dve", (lambda E, o=xc, a=mA8[:, tc:tc + 1],
                              b=tmp8[:, tc:tc + 1]:
                              E.tensor_scalar(out=o, in0=o, scalar1=a, scalar2=b,
                                              op0=ALU.mult, op1=ALU.add)),
                      waits=U(tap_toks[tc], tz_toks[tc], tsc_all))
            C.t_xup[tc] = tx
    t_sq_guard = U(xnf_toks[1], xnf_toks[3], xnf_toks[5], xnf_toks[7])
    t_tab_guard = U(znf_toks[1], znf_toks[3], znf_toks[5], znf_toks[7])
    xnf = bf16v(C, XNF_OFF, 16384)
    znf = bf16v(C, ZNF_OFF, 16384)
    C.t_Xnf = U(*znf_toks)

    tab = [bf16v(C, TAB_OFF, 4096), bf16v(C, TAB_OFF, 4096)]
    tab_busy = [t_tab_guard, t_tab_guard]
    Pslab = bf16v(C, SLAB2_OFF, 8192)
    onesbf = C.slab[:, sm["onesb"]:sm["onesb"] + 1].bitcast(BF16)[:, 0:1]
    ident = f32v(C, sm["ident"], 128)

    def rope_ap(p, half):
        return tab[p][:, half * 2048: half * 2048 + 2048].rearrange(
            "p (cc t) -> p cc t", t=512)

    gi = 0
    Apart_toks = []
    C.prev_scrd = None
    C.prev_rot = None
    wbL = [wb[0], wb[1],
           bf16v(C, HF_OFF, 2048), bf16v(C, WH_OFF, 2048)]
    wbL_busy = [wb_busy[0], wb_busy[1],
                U(wkv, base_wait), U(wkv, base_wait)]
    wbL_sem = ["wb0", "wb1", "wi2", "wi3"]
    for s in range(2):
        ptab = 0
        tdc = P.dma("sp", rope_ap(ptab, 0),
                    io["ropec"][:, :, s * 512:(s + 1) * 512],
                    waits=U(tab_busy[ptab], wkv), sem="tab")
        tds = P.dma("sp", rope_ap(ptab, 1),
                    io["ropes"][:, :, s * 512:(s + 1) * 512],
                    waits=U(tab_busy[ptab], wkv), sem="tab")
        t_tab = U(tdc, tds)
        nrm_open = {"k": False, "v": False}
        g0 = t_sq_guard if s == 0 else None
        nrm_last = {"k": g0, "v": g0}
        sq_last = {"k": None, "v": None}

        def proj(WX, bank, ec, pgi):
            tdw = P.dma("pool", wbL[pgi].rearrange("p (dc f) -> p dc f", f=128),
                        io[WX][ec], waits=wbL_busy[pgi], sem=wbL_sem[pgi])
            mms = [(wbL[pgi][:, dc * 128:(dc + 1) * 128],
                    znf[:, dc * 1024 + s * 512: dc * 1024 + s * 512 + 512])
                   for dc in range(16)]
            tk = mm_group(C, bank, mms, waits=U(tdw, C.t_Xnf),
                          out_slice=C.pb[bank][:, 0:512])
            return tk

        def evac(bank, dstsl, bias_name, ec, tk, extra=None):
            bc = C.slab[:, sm[bias_name] + ec: sm[bias_name] + ec + 1]
            te = P.op("act", (lambda E, o=dstsl, i=C.pb[bank][:, 0:512], a=bc:
                              E.activation(out=o, in_=i, func=AF.Identity,
                                           bias=a)),
                      waits=U(tk, C.t_consts, extra))
            bank_read(C, bank, te)
            return te

        def sqsum(src_sl, which, tprev, stop):
            bank = 4 if which == "k" else 5
            sq_off = 0 if which == "k" else 256
            sq_sl = sq[:, sq_off:sq_off + 256].bitcast(BF16)
            t1 = P.op("act", (lambda E, o=sq_sl, i=src_sl:
                              E.activation(out=o, in_=i, func=AF.Square)),
                      waits=U(tprev, nrm_last[which]))
            st = not nrm_open[which]
            nrm_open[which] = True
            w0 = U(t1, C.bank_free[bank]) if st else U(t1)
            t2 = P.op("pe", (lambda E, o=C.pb[bank][0:1, 0:512], l=onesbf,
                             r=sq_sl, s_=st, sp=stop:
                             E.matmul(out=o, lhsT=l, rhs=r, start=s_, stop=sp)),
                      waits=w0, inc=1)
            nrm_last[which] = t2
            if stop:
                C.bank_data[bank] = t2
                C.bank_free[bank] = None
            return t2

        P_toks = {}

        def do_chunk_post(ksl, vsl, tks, tvs, ecs):
            tqk = sqsum(ksl, "k", tks, stop=(ecs == 15))
            tqv = sqsum(vsl, "v", tvs, stop=(ecs == 15))
            psl = Pslab[:, ecs * 512:(ecs + 1) * 512]
            tp = P.op("dve", (lambda E, o=psl, i=ksl, j=vsl:
                              E.tensor_mul(out=o, in0=i, in1=j)),
                      waits=U(tks, tvs, t_slab2))
            P_toks[ecs] = U(tp, tqk, tqv)
            return U(tp, tqk, tqv)

        pend_post = []

        def flush_post():
            toks = [fn() for fn in pend_post]
            pend_post.clear()
            return U(*toks) if toks else None

        for pr in range(4):
            ktoks, vtoks = {}, {}
            for half, ecs in ((0, pr), (1, pr + 4)):
                kb = (2, 6)[ecs % 2]
                vb = (3, 7)[ecs % 2]
                pgi = gi % 4; gi += 1
                tk = proj("WinKX", kb, ecs, pgi)
                ksl = sc[:, half * 512: half * 512 + 512]
                ktoks[half] = evac(kb, ksl, "binK", ecs, tk,
                                   extra=U(t_sc_free if s == 0 and pr == 0
                                           else None, C.prev_scrd,
                                           C.prev_rot))
                wbL_busy[pgi] = tk
                pgi = gi % 4; gi += 1
                tv = proj("WinVX", vb, ecs, pgi)
                vsl = sc2[:, half * 512: half * 512 + 512]
                vtoks[half] = evac(vb, vsl, "binV", ecs, tv,
                                   extra=U(C.prev_scrd, C.prev_rot))
                wbL_busy[pgi] = tv
            t_post = flush_post()
            if t_post:
                C.prev_scrd = t_post
            cosc = rope_ap(ptab, 0)[:, pr, :]
            sinc = rope_ap(ptab, 1)[:, pr, :]
            rots = []
            for scr, toks in ((sc, ktoks), (sc2, vtoks)):
                x1 = scr[:, 0:512]; x2 = scr[:, 512:1024]
                t1s = scr[:, 1024:1536]; t2s = scr[:, 1536:2048]
                ta = P.op("dve", (lambda E, o=t1s, i=x1, cc=cosc:
                                  E.tensor_mul(out=o, in0=i, in1=cc)),
                          waits=U(toks[0], toks[1], t_tab, t_post))
                tb = P.op("dve", (lambda E, o=t2s, i=x2, ss=sinc:
                                  E.tensor_mul(out=o, in0=i, in1=ss)),
                          waits=t_post)
                tcx = P.op("dve", (lambda E, o=t1s, i=t2s:
                                   E.tensor_sub(out=o, in0=o, in1=i)),
                           waits=U(ta, tb))
                td_ = P.op("dve", (lambda E, o=t2s, i=x2, cc=cosc:
                                   E.tensor_mul(out=o, in0=i, in1=cc)),
                           waits=U(tb, tcx))
                te_ = P.op("dve", (lambda E, o=x2, i=x1, ss=sinc:
                                   E.tensor_mul(out=o, in0=i, in1=ss)),
                           waits=td_)
                tf_ = P.op("dve", (lambda E, o=t2s, i=x2:
                                   E.tensor_add(out=o, in0=o, in1=i)),
                           waits=U(td_, te_))
                rots.append((t1s, tcx, t2s, tf_))
            (k1, tk1, k2, tk2) = rots[0]
            (v1, tv1, v2, tv2) = rots[1]
            C.prev_rot = U(tk1, tk2, tv1, tv2)
            pend_post.append(lambda a=k1, b=v1, c=U(tk1, tv1), e=pr:
                             do_chunk_post(a, b, c, c, e))
            pend_post.append(lambda a=k2, b=v2, c=U(tk2, tv2), e=pr + 4:
                             do_chunk_post(a, b, c, c, e))
        for ecs in range(8, 16):
            kb = (2, 6)[ecs % 2]
            vb = (3, 7)[ecs % 2]
            half = ecs % 2
            pgi = gi % 4; gi += 1
            tk = proj("WinKX", kb, ecs, pgi)
            ksl = sc[:, half * 512: half * 512 + 512]
            tke = evac(kb, ksl, "binK", ecs, tk,
                       extra=U(C.prev_scrd, C.prev_rot))
            wbL_busy[pgi] = tk
            pgi = gi % 4; gi += 1
            tv = proj("WinVX", vb, ecs, pgi)
            vsl = sc2[:, half * 512: half * 512 + 512]
            tve = evac(vb, vsl, "binV", ecs, tv,
                       extra=U(C.prev_scrd, C.prev_rot))
            wbL_busy[pgi] = tv
            t_post = flush_post()
            if t_post:
                C.prev_scrd = t_post
            pend_post.append(lambda a=ksl, b=vsl, c=U(tke, tve), e=ecs:
                             do_chunk_post(a, b, c, c, e))
        C.prev_scrd = flush_post()
        nk_sb = sq[0:1, 1024:1536]
        t1 = P.op("dve", (lambda E, o=nk_sb, i=C.pb[4][0:1, 0:512]:
                          E.tensor_copy(out=o, in_=i)), waits=C.bank_data[4])
        C.bank_free[4] = t1
        prod = sq[0:1, 1536:2048]
        t2 = P.op("dve", (lambda E, o=prod, i=nk_sb, j=C.pb[5][0:1, 0:512]:
                          E.tensor_mul(out=o, in0=i, in1=j)),
                  waits=U(t1, C.bank_data[5]))
        C.bank_free[5] = t2
        t3 = P.op("act", (lambda E, o=prod:
                          E.activation(out=o, in_=o, func=AF.Sqrt)), waits=t2)
        t4 = P.op("dve", (lambda E, o=prod: E.reciprocal(out=o, in_=o)), waits=t3)
        w0 = U(C.bank_free[6], t4, C.t_consts)
        tbc = P.op("pe", (lambda E, o=C.pb[6][:, 0:512], l=ones_rr(C), r=prod:
                          E.matmul(out=o, lhsT=l, rhs=r, start=True, stop=True)),
                   waits=w0)
        C.bank_data[6] = tbc
        Gsb = sq[:, 512:768].bitcast(BF16)
        t5 = P.op("dve", (lambda E, o=Gsb, i=C.pb[6][:, 0:512]:
                          E.tensor_copy(out=o, in_=i)), waits=tbc)
        C.bank_free[6] = t5
        for ecs in range(16):
            psl = Pslab[:, ecs * 512:(ecs + 1) * 512]
            acol = C.slab[:, sm["Apart"] + s * 16 + ecs:
                          sm["Apart"] + s * 16 + ecs + 1]
            tA = P.op("dve", (lambda E, o=psl, i=psl, gg=Gsb, a=acol:
                              E.scalar_tensor_tensor(out=o, in0=i, scalar=1.0,
                                                     in1=gg, op0=ALU.mult,
                                                     op1=ALU.mult, accum_out=a)),
                      waits=U(P_toks[ecs], t5))
            Apart_toks.append(tA)
        tab_busy[ptab] = U(*[P_toks[e_] for e_ in range(8)])
        t_slab2 = U(*Apart_toks)
    t_Ap = U(*Apart_toks)
    wb_busy[0] = wbL_busy[0]
    wb_busy[1] = wbL_busy[1]
    C.t_wbL_free = U(wbL_busy[2], wbL_busy[3])
    Af = f32v(C, sm["Afin"], 16)
    tAs = P.op("dve", (lambda E, o=Af,
                       i=C.slab[:, sm["Apart"]:sm["Apart"] + 16],
                       j=C.slab[:, sm["Apart"] + 16:sm["Apart"] + 32]:
                       E.tensor_add(out=o, in0=i, in1=j)), waits=t_Ap)
    c2i, c2o = C.cc["c2i"], C.cc["c2o"]
    td = P.dma("pool", c2i.ap(), Af, waits=tAs, sem="cci")
    if SIM_NOCC:
        tl = []
        for r in range(8):
            tl.append(P.dma("pool", c2o.ap()[r * 128:(r + 1) * 128, :],
                            c2i.ap(), waits=td, sem="agl"))
        tcc = U(*tl)
    else:
        tcc = P.op("pool", (lambda E: E.collective_compute(
            "AllGather", ALU.bypass, replica_groups=RG,
            ins=[c2i.ap().opt()], outs=[c2o.ap().opt()])), waits=td)

    emit_xup()

    moe(C, t_Ap)

    s_ag = f32v(C, SC_OFF, 2048, p=8)
    tld = P.dma("pool", s_ag, c2o.ap().rearrange("(r p) c -> r (p c)", p=128),
                waits=U(tcc, t_Ap, getattr(C, "t_thr", None)), sem="agl")
    mask1 = C.slab[0:8, sm["mask"] + 1:sm["mask"] + 2]
    w0 = U(tld, C.bank_free[6], C.t_consts)
    tmo = None
    s_ag3 = s_ag.rearrange("r (p c) -> r p c", c=16)
    for cch in range(16):
        tmo = P.op("pe",
                   (lambda E, o=C.pb[6][:, cch:cch + 1],
                    l=s_ag3[:, :, cch], r=mask1:
                    E.matmul(out=o, lhsT=l, rhs=r, start=True, stop=True)),
                   waits=(w0 if cch == 0 else None), inc=(1 if cch == 15 else 0))
    C.bank_data[6] = tmo
    tof = P.op("dve", (lambda E, o=Af, i=C.pb[6][:, 0:16]:
                       E.tensor_copy(out=o, in_=i)), waits=U(tmo, tAs))
    C.bank_free[6] = tof
    tof2 = P.op("dve", (lambda E, o=Af:
                        E.tensor_scalar(out=o, in0=o, scalar1=0.5, scalar2=0.5,
                                        op0=ALU.mult, op1=ALU.add)), waits=tof)
    t_Afin = tof2

    YF = bf16v(C, SLAB2_OFF, 8192)
    bout_row = f32v(C, ROWS_OFF, 2048, p=1)
    t_borow = P.dma("act", bout_row, io["bout_r"], waits=t_rows, sem="brow")
    ring_busy = [C.t_moe_xnf_done] * 8
    ru = [0]

    def ring_take(units):
        u = ru[0]
        if u % units:
            u += units - (u % units)
        if u + units > 8:
            u = 0
        ru[0] = u + units
        off = XNF_OFF + u * 1024
        tok = U(*[ring_busy[i] for i in range(u, u + units)])
        def mark(t):
            for i in range(u, u + units):
                ring_busy[i] = t
        return off, tok, mark, "rg%d" % u

    x_final = []
    xdc = [[] for _ in range(8)]
    for s in range(2):
        y_toks = []
        for ecp in range(8):
            ec0 = ecp * 2
            off, wtok, mark, rsem = ring_take(2)
            gr = bf16v(C, off, 4096).rearrange("p (e n) -> p e n", n=2048)
            tdq = P.dma("sp", gr,
                        io["WinQX"][ec0:ec0 + 2].rearrange(
                            "e p d f -> p e (d f)"),
                        waits=wtok, sem=rsem)
            consumed = []
            for ei in range(2):
                ec = ec0 + ei
                bq_b = (2, 3)[ec % 2]
                mms = [(gr[:, ei, dc * 128:(dc + 1) * 128],
                        znf[:, dc * 1024 + s * 512: dc * 1024 + s * 512 + 512])
                       for dc in range(16)]
                tq = mm_group(C, bq_b, mms, waits=U(tdq, C.t_Xnf),
                              out_slice=C.pb[bq_b][:, 0:512])
                ysl = YF[:, ec * 512:(ec + 1) * 512]
                bqc = C.slab[:, sm["binQ"] + ec: sm["binQ"] + ec + 1]
                aac = C.slab[:, sm["Afin"] + ec: sm["Afin"] + ec + 1]
                ty = P.op("dve", (lambda E, o=ysl, i=C.pb[bq_b][:, 0:512],
                                  a=bqc, b=aac:
                                  E.tensor_scalar(out=o, in0=i, scalar1=a,
                                                  scalar2=b, op0=ALU.add,
                                                  op1=ALU.mult)),
                          waits=U(tq, t_Afin, t_Ap, C.t_moe_slab2))
                bank_read(C, bq_b, ty)
                y_toks.append(ty)
                consumed.append(tq)
            mark(U(*consumed))
        t_Y = U(*y_toks)
        for egp in range(2):
            for dcp in range(8):
                off, wtok, mark, rsem = ring_take(1)
                gr = bf16v(C, off, 2048).rearrange("p (d e) -> p d e", e=1024)
                tdw = P.dma("sp", gr,
                            io["WoutR"][dcp * 2:dcp * 2 + 2, :,
                                        egp * 1024:(egp + 1) * 1024]
                            .rearrange("d p e -> p d e"),
                            waits=wtok, sem=rsem)
                last = None
                for dcl in range(2):
                    dc = dcp * 2 + dcl
                    for bidx in range(8):
                        tch, eg = bidx // 2, bidx % 2
                        st = (dc == 0)
                        w = U(C.bank_free[bidx] if st else None,
                              t_Y if st else None, tdw)
                        last = P.op("pe",
                            (lambda E, o=C.pb[bidx][:, 0:512],
                             l=YF[:, dc * 512 + tch * 128:
                                   dc * 512 + (tch + 1) * 128],
                             r=gr[:, dcl, eg * 512:(eg + 1) * 512], s_=st:
                             E.matmul(out=o, lhsT=l, rhs=r, start=s_, stop=False)),
                            waits=w, inc=(1 if bidx == 7 else 0))
                mark(last)
            for bidx in range(8):
                tch, eg = bidx // 2, bidx % 2
                tokmm = P.op("pe", (lambda E, o=C.pb[bidx][:, 0:512], l=ones_rr(C),
                                    r=bout_row[:, egp * 1024 + eg * 512:
                                               egp * 1024 + eg * 512 + 512]:
                                    E.matmul(out=o, lhsT=l, rhs=r, start=False,
                                             stop=True)),
                             waits=U(t_borow, C.t_consts), inc=1)
                C.bank_data[bidx] = tokmm
                xsl = f32v(C, X_OFF + (s * 4 + tch) * 2048 + egp * 1024 + eg * 512,
                           512)
                tev = P.op("dve", (lambda E, o=xsl, i=C.pb[bidx][:, 0:512]:
                                   E.tensor_add(out=o, in0=o, in1=i)),
                           waits=U(tokmm, C.t_moe_x[s * 4 + tch],
                                   C.t_xup[s * 4 + tch]))
                C.bank_free[bidx] = tev
                x_final.append(tev)
                xdc[s * 4 + tch].append(tev)
    C.t_xdone = U(*x_final)
    C.t_xdone_c = [U(*xdc[c]) for c in range(8)]


def moe(C, t_sq_free):
    P, io, sm = C.P, C.io, C.sm
    xnf = bf16v(C, XNF_OFF, 16384)
    sq = f32v(C, SP2_OFF, 2048)
    hf = f32v(C, HF_OFF, 1024)
    whb = bf16v(C, WH_OFF, 1024)
    wb, wb_busy = C.wb, C.wb_busy
    ident = f32v(C, sm["ident"], 128)

    trt = P.dma("sp", wb[0][:, 0:128].rearrange("p (dc e) -> p dc e", e=8),
                io["RtX"], waits=wb_busy[0], sem="rt")
    rtv = wb[0][:, 0:128].rearrange("p (dc e) -> p dc e", e=8)
    tdd = P.dma("sp", wb[1].rearrange("p (dc e) -> p dc e", e=128),
                io["DnX"], waits=wb_busy[1], sem="dn")
    dnb = wb[1].rearrange("p (dc e) -> p dc e", e=128)

    lg_toks = []
    for s in range(2):
        w0 = U(C.bank_free[7], trt, C.t_xnf)
        tok = None
        for i in range(16):
            tok = P.op("pe", (lambda E, o=C.pb[7][0:8, 0:512],
                              l=rtv[:, i, :],
                              r=xnf[:, i * 1024 + s * 512: i * 1024 + s * 512 + 512],
                              st=(i == 0), sp=(i == 15):
                              E.matmul(out=o, lhsT=l, rhs=r, start=st, stop=sp)),
                       waits=(w0 if i == 0 else None), inc=(1 if i == 15 else 0))
        C.bank_data[7] = tok
        lsb = sq[0:8, s * 512: s * 512 + 512]
        te = P.op("dve", (lambda E, o=lsb, i=C.pb[7][0:8, 0:512]:
                          E.tensor_copy(out=o, in_=i)), waits=U(tok, t_sq_free))
        C.bank_free[7] = te
        lg_toks.append(te)
    wb_busy[0] = U(wb_busy[0], *lg_toks)
    t_lg = U(*lg_toks)

    ltok_toks = []
    for tcch in range(8):
        b = tcch % 2
        w0 = U(C.bank_free[b], t_lg, C.t_consts)
        tt_ = P.op("pe", (lambda E, o=C.pb[b][:, 0:8],
                          i=sq[0:8, tcch * 128:(tcch + 1) * 128],
                          idn=ident[0:8, 0:8]:
                          E.transpose(out=o, in_=i, identity=idn)), waits=w0)
        C.bank_data[b] = tt_
        lcol = C.slab[:, sm["ltok"] + tcch * 8: sm["ltok"] + tcch * 8 + 8]
        te = P.op("dve", (lambda E, o=lcol, i=C.pb[b][:, 0:8]:
                          E.tensor_copy(out=o, in_=i)), waits=tt_)
        C.bank_free[b] = te
        ltok_toks.append(te)

    wf_toks = []
    for tcch in range(8):
        lcol = C.slab[:, sm["ltok"] + tcch * 8: sm["ltok"] + tcch * 8 + 8]
        wcol = C.slab[:, sm["wful"] + tcch * 8: sm["wful"] + tcch * 8 + 8]
        e8 = C.slab[:, sm["e8"]:sm["e8"] + 8]
        m1 = sm1(C, "tiny", 8)
        m2 = sm1(C, "tiny", 9)
        t0 = ltok_toks[tcch]
        t1 = P.op("dve", (lambda E, o=m1, i=lcol:
                          E.reduce_max(out=o, in_=i, axis=AX.X)), waits=t0)
        t2 = P.op("dve", (lambda E, o=wcol, i=lcol, a=m1:
                          E.tensor_scalar(out=o, in0=i, scalar1=a, scalar2=None,
                                          op0=ALU.is_ge)), waits=t1)
        t3 = P.op("dve", (lambda E, o=wcol, i=lcol:
                          E.scalar_tensor_tensor(out=o, in0=o, scalar=-BIG,
                                                 in1=i, op0=ALU.mult,
                                                 op1=ALU.add)), waits=t2)
        t4 = P.op("dve", (lambda E, o=m2, i=wcol:
                          E.reduce_max(out=o, in_=i, axis=AX.X)), waits=t3)
        t5 = P.op("dve", (lambda E, o=wcol, i=lcol, a=m2:
                          E.tensor_scalar(out=o, in0=i, scalar1=a, scalar2=None,
                                          op0=ALU.is_ge)), waits=t4)
        negm1 = sm1(C, "tiny", 10)
        t6 = P.op("dve", (lambda E, o=negm1, i=m1:
                          E.tensor_scalar(out=o, in0=i, scalar1=-1.0,
                                          scalar2=None, op0=ALU.mult)), waits=t1)
        t7 = P.op("act", (lambda E, o=e8, i=lcol, a=negm1:
                          E.activation(out=o, in_=i, func=AF.Exp, bias=a)),
                  waits=U(t6, t5))
        t8 = P.op("dve", (lambda E, o=wcol, i=e8:
                          E.tensor_mul(out=o, in0=o, in1=i)), waits=t7)
        den = sm1(C, "tiny", 8)
        t9 = P.op("dve", (lambda E, o=den, i=wcol:
                          E.reduce_sum(out=o, in_=i, axis=AX.X)), waits=t8)
        rd = sm1(C, "tiny", 9)
        t10 = P.op("dve", (lambda E, o=rd, i=den: E.reciprocal(out=o, in_=i)),
                   waits=t9)
        t11 = P.op("dve", (lambda E, o=wcol, a=rd:
                           E.tensor_scalar(out=o, in0=o, scalar1=a, scalar2=None,
                                           op0=ALU.mult)), waits=t10)
        wf_toks.append(t11)

    wrow = sq[0:8, 1024:1536].bitcast(BF16)
    wT_toks = []
    for tcch in range(8):
        b = tcch % 2
        wcol = C.slab[:, sm["wful"] + tcch * 8: sm["wful"] + tcch * 8 + 8]
        w0 = U(C.bank_free[b], wf_toks[tcch], C.t_consts, t_sq_free)
        tt_ = P.op("pe", (lambda E, o=C.pb[b][0:8, 0:128], i=wcol, idn=ident:
                          E.transpose(out=o, in_=i, identity=idn)), waits=w0)
        C.bank_data[b] = tt_
        te = P.op("dve", (lambda E, o=wrow[:, tcch * 128:(tcch + 1) * 128],
                          i=C.pb[b][0:8, 0:128]:
                          E.tensor_copy(out=o, in_=i)), waits=tt_)
        C.bank_free[b] = te
        wT_toks.append(te)
    t_wT = U(*wT_toks)

    h_toks = []
    for s in range(2):
        mms = [(dnb[:, dc, :],
                xnf[:, dc * 1024 + s * 512: dc * 1024 + s * 512 + 512])
               for dc in range(16)]
        tm = mm_group(C, 7, mms, waits=U(tdd, C.t_xnf),
                      out_slice=C.pb[7][:, 0:512])
        te = P.op("dve", (lambda E, o=hf[:, s * 512:(s + 1) * 512],
                          i=C.pb[7][:, 0:512]:
                          E.tensor_copy(out=o, in_=i)),
                  waits=U(tm, C.t_kv_l1_free, C.t_wbL_free))
        bank_read(C, 7, te)
        h_toks.append(te)
    t_h = U(*h_toks)
    wb_busy[1] = t_h

    esl = C.slab[0:8, sm["esel"]:sm["esel"] + 64].bitcast(BF16)
    wrep_toks = []
    for s in range(2):
        w0 = U(C.bank_free[6], t_wT, C.t_consts)
        tm = P.op("pe", (lambda E, o=C.pb[6][:, 0:512], l=esl,
                         r=wrow[:, s * 512:(s + 1) * 512]:
                         E.matmul(out=o, lhsT=l, rhs=r, start=True, stop=True)),
                  waits=w0)
        C.bank_data[6] = tm
        twh = P.op("dve", (lambda E, o=whb[:, s * 512:(s + 1) * 512],
                           i=hf[:, s * 512:(s + 1) * 512], j=C.pb[6][:, 0:512]:
                           E.tensor_mul(out=o, in0=i, in1=j)),
                   waits=U(tm, t_h))
        C.bank_free[6] = twh
        wrep_toks.append(twh)
    t_wh = U(*wrep_toks)

    tdu = P.dma("sp", wb[0][:, 0:2048], io["UpA"],
                waits=U(wb_busy[0], t_lg), sem="up")
    upv = wb[0][:, 0:2048]
    C.t_moe_x = [None] * 8
    up_toks = []
    for eg in range(4):
        for tch in range(8):
            b = tch
            w0 = U(C.bank_free[b], t_wh, tdu)
            tm = P.op("pe", (lambda E, o=C.pb[b][:, 0:512],
                             l=whb[:, tch * 128:(tch + 1) * 128],
                             r=upv[:, eg * 512:(eg + 1) * 512]:
                             E.matmul(out=o, lhsT=l, rhs=r, start=True,
                                      stop=True)), waits=w0)
            C.bank_data[b] = tm
            xsl = f32v(C, X_OFF + tch * 2048 + eg * 512, 512)
            tev = P.op("dve", (lambda E, o=xsl, i=C.pb[b][:, 0:512]:
                               E.tensor_add(out=o, in0=o, in1=i)),
                       waits=U(tm, C.t_xup[tch]))
            C.bank_free[b] = tev
            C.t_moe_x[tch] = U(C.t_moe_x[tch], tev)
            up_toks.append(tev)
    wb_busy[0] = U(*up_toks)
    C.t_moe_slab2 = None
    C.t_moe_xnf_done = U(t_lg, t_h)


def kernel(**inputs):
    g = stage_host(inputs)
    cores = [stage_core(inputs, cid, g) for cid in range(NCORE)]
    if FUSED:
        nc = build_nc("ALL")
        in_maps = []
        for cid in range(NCORE):
            m = {k: (cores[cid][k] if k in cores[cid] else g[k])
                 for k in KEYS_ALL}
            in_maps.append(m)
        res = run_bass_kernel_spmd(nc, in_maps, core_ids=list(range(NCORE)))
        outs = [res.results[c]["out"] for c in range(NCORE)]
    else:
        nc0 = build_nc("L0")
        in_maps = []
        for cid in range(NCORE):
            m = {k: (cores[cid][k] if k in cores[cid] else g[k])
                 for k in KEYS_L0}
            in_maps.append(m)
        r0 = run_bass_kernel_spmd(nc0, in_maps, core_ids=list(range(NCORE)))
        nc1 = build_nc("L1")
        in_maps1 = []
        for cid in range(NCORE):
            m = {k: (cores[cid][k] if k in cores[cid] else g[k])
                 for k in KEYS_L1}
            m["xin"] = np.ascontiguousarray(np.asarray(r0.results[cid]["out"]))
            in_maps1.append(m)
        r1 = run_bass_kernel_spmd(nc1, in_maps1, core_ids=list(range(NCORE)))
        outs = [r1.results[c]["out"] for c in range(NCORE)]
    full = np.concatenate([np.asarray(o) for o in outs], axis=0)
    return full.reshape(1, 8192, 2048).astype(np.float32)


# version 18
# speedup vs baseline: 1.1336x; 1.0491x over previous
import numpy as np
import ml_dtypes
from contextlib import ExitStack

import concourse.bass as bass
import concourse.mybir as mybir
from concourse.bass_utils import run_bass_kernel_spmd

F32 = mybir.dt.float32
BF16 = mybir.dt.bfloat16
AF = mybir.ActivationFunctionType
ALU = mybir.AluOpType
AX = mybir.AxisListType
BF = ml_dtypes.bfloat16

NCORE = 8
TOK = 1024
TC = 8
D = 2048
DFF = 8192
SCALE = float(128.0 ** -0.5)
INVSCALE = float(128.0 ** 0.5)
EPS_LN = 1e-5
EPS_BIT = 1e-8
BIG = 30000.0
RG = [[0, 1, 2, 3, 4, 5, 6, 7]]
FUSED = True
SELF_WAIT = False
SIM_NOCC = False
DBG_STOP = None
RACE_DETECT = False

X_OFF = 0
XNF_OFF = 16384
KV_OFF = 24576
SLAB2_OFF = 40960
SC_OFF = 45056
WF32_OFF = 47104
WBF_OFF = 48128
ROWS_OFF = 50176
SMALL_OFF = 52224
SLAB_COLS = 53200

ZNF_OFF = KV_OFF
TAB_OFF = KV_OFF + 8192
HF_OFF = KV_OFF + 10240
WH_OFF = KV_OFF + 11264
SC2_OFF = KV_OFF + 12288
SP2_OFF = KV_OFF + 14336

ENGINES = ("pe", "dve", "act", "pool", "sp")
BLOCK_ATTR = {"pe": "tensor", "dve": "vector", "act": "scalar",
              "pool": "gpsimd", "sp": "sync"}
SEM_NAMES = ("pe", "dve", "act", "pool", "sp",
             "pool_dma", "sp_dma", "act_dma",
             "wfp0", "wfp1", "wfp2", "wfp3", "wfp4",
             "wb0", "wb1", "tab", "xio", "oio",
             "cst", "cci", "agl", "b2s", "brow", "rt", "dn", "up",
             "rg0", "rg1", "rg2", "rg3", "rg4", "rg5", "rg6", "rg7",
             "wk0", "wk1", "wk2", "wk3", "wk4", "wk5", "wi2", "wi3",
             "wq0", "wq1", "wq2", "wq3", "wq4", "wq5",
             "xo0", "xo1", "xo2", "xo3", "xo4", "xo5", "xo6", "xo7")


def U(*toks):
    out = {}
    for t in toks:
        if not t:
            continue
        for s, v in t.items():
            if out.get(s, -1) < v:
                out[s] = v
    return out


class Prog:
    def __init__(self, nc):
        self.nc = nc
        self.ops = {e: [] for e in ENGINES}
        self.sems = {}
        self.cnt = {s: 0 for s in SEM_NAMES}
        self.waited = {e: {} for e in ENGINES}

    def op(self, eng, fn, waits=None, inc=1, dma=False, sem=None):
        semname = (sem or eng + "_dma") if dma else eng
        need = []
        wt = self.waited[eng]
        if SELF_WAIT and not dma and eng in ("dve", "act", "pool"):
            prev = self.cnt[eng]
            if prev and wt.get(eng, 0) < prev:
                need.append((eng, prev))
                wt[eng] = prev
        if waits:
            for s, v in waits.items():
                if s.endswith("_dma"):
                    v = self.cnt[s]
                if wt.get(s, 0) >= v:
                    continue
                need.append((s, v))
                wt[s] = v
        step = (16 if dma else inc) if inc else 0
        if step:
            self.cnt[semname] += step
        target = self.cnt[semname]

        def emit(engine, _need=tuple(need), _fn=fn, _step=step, _sem=semname):
            for s, v in _need:
                engine.wait_ge(self.sems[s], v)
            instr = _fn(engine)
            if _step:
                instr.then_inc(self.sems[_sem], _step)
        self.ops[eng].append(emit)
        return {semname: target} if step else None

    def dma(self, eng, out, in_, waits=None, sem=None):
        return self.op(eng, lambda E, o=out, i=in_: E.dma_start(out=o, in_=i),
                       waits=waits, dma=True, sem=sem)

    def wait_only(self, eng, waits):
        need = []
        wt = self.waited[eng]
        for s, v in waits.items():
            if wt.get(s, 0) >= v:
                continue
            need.append((s, v))
            wt[s] = v

        def emit(engine, _need=tuple(need)):
            for s, v in _need:
                engine.wait_ge(self.sems[s], v)
        self.ops[eng].append(emit)

    def finalize(self, block):
        for e in ENGINES:
            fns = self.ops[e]
            if not fns:
                continue

            def body(engine, _fns=tuple(fns)):
                for f in _fns:
                    f(engine)
            getattr(block, BLOCK_ATTR[e])(body)


def _xform(W):
    return np.ascontiguousarray(W.reshape(16, 128, 16, 128).transpose(0, 3, 2, 1))


def _xform_de(Wde):
    ecn = Wde.shape[1] // 128
    return np.ascontiguousarray(
        Wde.reshape(16, 128, ecn, 128).transpose(2, 1, 0, 3))


def _rhsform(Wde):
    return np.ascontiguousarray(Wde.reshape(-1, 128, Wde.shape[1]))


def _pvec(v):
    return np.ascontiguousarray(np.asarray(v).reshape(-1, 128).T)


def stage_host(inputs):
    g = {}
    perm = np.concatenate([np.arange(0, 1024, 2), np.arange(1, 1024, 2),
                           np.arange(1024, 2048)])
    g["WqX"] = _xform(np.asarray(inputs["L0_attn_Wq"])).astype(BF)
    g["WkX"] = _xform(np.asarray(inputs["L0_attn_Wk"])).astype(BF)
    g["WvX"] = _xform(np.asarray(inputs["L0_attn_Wv"])).astype(BF)
    g["WoR"] = _rhsform(np.ascontiguousarray(
        np.asarray(inputs["L0_attn_Wo"]).T)).astype(BF)
    g["bq"] = _pvec(inputs["L0_attn_bq"])
    g["bk"] = _pvec(inputs["L0_attn_bk"])
    g["bv"] = _pvec(inputs["L0_attn_bv"])
    g["bo_r"] = np.ascontiguousarray(
        np.asarray(inputs["L0_attn_bo"])[None, :]).astype(BF)
    w1 = np.asarray(inputs["L0_ffn_w1"])
    w2 = np.asarray(inputs["L0_ffn_w2"])
    g["W1X"] = np.ascontiguousarray(
        w1.reshape(64, 128, 16, 128).transpose(0, 3, 2, 1))
    g["W2R"] = np.ascontiguousarray(w2.T.reshape(64, 128, 2048))
    g["b1t"] = _pvec(inputs["L0_ffn_b1"])
    g["b2_r"] = np.ascontiguousarray(
        np.asarray(inputs["L0_ffn_b2"])[None, :]).astype(BF)
    Win = np.asarray(inputs["L1_attn_W_in"])
    bin_ = np.asarray(inputs["L1_attn_b_in"])
    g["WinQX"] = _xform_de(np.ascontiguousarray(Win[:, 0:2048][:, perm])).astype(BF)
    g["WinKX"] = _xform_de(np.ascontiguousarray(Win[:, 2048:4096][:, perm])).astype(BF)
    g["WinVX"] = _xform_de(np.ascontiguousarray(Win[:, 4096:6144][:, perm])).astype(BF)
    g["binQ"] = _pvec(np.ascontiguousarray(bin_[0:2048][perm]))
    g["binK"] = _pvec(np.ascontiguousarray(bin_[2048:4096][perm]))
    g["binV"] = _pvec(np.ascontiguousarray(bin_[4096:6144][perm]))
    g["WoutR"] = _rhsform(np.ascontiguousarray(
        np.asarray(inputs["L1_attn_W_out"])[perm, :])).astype(BF)
    g["bout_r"] = np.ascontiguousarray(
        np.asarray(inputs["L1_attn_b_out"])[None, :]).astype(BF)
    g["gam"] = _pvec(inputs["L1_attn_gamma"])
    g["bet"] = _pvec(inputs["L1_attn_beta"])
    rw = np.asarray(inputs["L1_ffn_router_w"])
    g["RtX"] = np.ascontiguousarray(
        rw.T.reshape(16, 128, 8).transpose(1, 0, 2)).astype(BF)
    dn = np.asarray(inputs["L1_ffn_experts_down"])
    dall = dn.transpose(1, 0, 2).reshape(2048, 128)
    g["DnX"] = np.ascontiguousarray(
        dall.reshape(16, 128, 128).transpose(1, 0, 2)).astype(BF)
    g["UpA"] = np.ascontiguousarray(
        np.asarray(inputs["L1_ffn_experts_up"]).reshape(128, 2048)).astype(BF)
    g["ident"] = np.eye(128, dtype=np.float32)
    esel = np.zeros((8, 128), dtype=np.float32)
    for ei in range(8):
        esel[ei, ei * 16:(ei + 1) * 16] = 1.0
    g["esel"] = esel.astype(BF)
    g["mask1"] = np.ones((8, 1), dtype=np.float32)
    return g


def stage_core(inputs, cid, g=None):
    c = {}
    x = np.asarray(inputs["L0_attn_x"]).reshape(8192, 2048)
    c["xin"] = np.ascontiguousarray(x[cid * TOK:(cid + 1) * TOK])
    if g is None:
        w1 = np.asarray(inputs["L0_ffn_w1"])
        w2 = np.asarray(inputs["L0_ffn_w2"])
        W1X = np.ascontiguousarray(
            w1.reshape(64, 128, 16, 128).transpose(0, 3, 2, 1))
        W2R = np.ascontiguousarray(w2.T.reshape(64, 128, 2048))
    else:
        W1X, W2R = g["W1X"], g["W2R"]
    gn = 16 * 131072
    c["W1M"] = np.ascontiguousarray(
        W1X.reshape(-1)[cid * gn:(cid + 1) * gn].reshape(16, 128, 1024))
    c["W2M"] = np.ascontiguousarray(
        W2R.reshape(-1)[cid * gn:(cid + 1) * gn].reshape(16, 128, 1024))
    inv = 1.0 / (10000.0 ** (np.arange(0, 1024, 2, dtype=np.float64) / 1024.0))
    pos = np.arange(cid * TOK, (cid + 1) * TOK, dtype=np.float64)
    ang = np.outer(inv, pos)
    c["ropec"] = np.ascontiguousarray(
        np.cos(ang).astype(np.float32).astype(BF)
        .reshape(4, 128, 1024).transpose(1, 0, 2))
    c["ropes"] = np.ascontiguousarray(
        np.sin(ang).astype(np.float32).astype(BF)
        .reshape(4, 128, 1024).transpose(1, 0, 2))
    mp = np.zeros((8, 1), dtype=np.float32)
    mp[:cid] = 1.0
    c["maskp"] = mp
    return c


KEYS_L0 = ["xin", "WqX", "WkX", "WvX", "WoR", "bq", "bk", "bv", "bo_r",
           "W1X", "W2R", "W1M", "W2M", "b1t", "b2_r", "maskp", "ident"]
KEYS_L1 = ["xin", "WinQX", "WinKX", "WinVX", "WoutR", "binQ", "binK", "binV",
           "bout_r", "gam", "bet", "RtX", "DnX", "UpA", "ropec", "ropes",
           "mask1", "ident", "esel"]
KEYS_ALL = sorted(set(KEYS_L0) | set(KEYS_L1))


class Ctx:
    pass


def small_alloc():
    m = {}
    off = SMALL_OFF

    def take(name, n):
        nonlocal off
        m[name] = off
        off += n
    take("stat", 16)
    take("bq", 16); take("bk", 16); take("bv", 16)
    take("binQ", 16); take("binK", 16); take("binV", 16)
    take("b1t", 64)
    take("gam", 16); take("bet", 16)
    take("carry", 18)
    take("offs", 18)
    take("Apart", 32)
    take("Afin", 16)
    take("thr", 4)
    take("ltok", 64)
    take("wful", 64)
    take("tiny", 16)
    take("e8", 8)
    take("ones_r", 128)
    take("invs_r", 128)
    take("ident", 128)
    take("esel", 64)
    take("mask", 4)
    take("onesb", 2)
    take("mean", 4)
    take("half", 2)
    take("lnb", 56)
    assert off <= SLAB_COLS, off
    return m


def f32v(C, off, n, p0=0, p=128):
    return C.slab[p0:p0 + p, off:off + n]


def bf16v(C, off, nbf):
    return C.slab[:, off:off + nbf // 2].bitcast(BF16)


def sm1(C, name, idx=0):
    o = C.sm[name] + idx
    return C.slab[:, o:o + 1]


def bank_read(C, b, tok):
    C.bank_free[b] = U(C.bank_free[b], tok)


def mm_group(C, b, mms, waits=None, out_slice=None):
    P = C.P
    n = len(mms)
    w0 = U(C.bank_free[b], waits)
    tok = None
    for i, (lh, rh) in enumerate(mms):
        tok = P.op("pe",
                   (lambda E, o=out_slice, l=lh, r=rh, st=(i == 0),
                    sp=(i == n - 1):
                    E.matmul(out=o, lhsT=l, rhs=r, start=st, stop=sp)),
                   waits=(w0 if i == 0 else None),
                   inc=(1 if i == n - 1 else 0))
    C.bank_data[b] = tok
    C.bank_free[b] = None
    return tok


def ones_rr(C):
    return f32v(C, C.sm["ones_r"], 128, p=1)


def invs_bf(C):
    return C.slab[0:1, C.sm["invs_r"]:C.sm["invs_r"] + 64].bitcast(BF16)


def ones_bf_r(C):
    return C.slab[0:1, C.sm["invs_r"] + 64:C.sm["invs_r"] + 128].bitcast(BF16)


def build_nc(part):
    nc = bass.Bass("TRN2", target_bir_lowering=False, debug=False,
                   detect_race_conditions=RACE_DETECT)
    C = Ctx()
    C.part = part
    io = {}

    def din(name, shape, dt=F32):
        io[name] = nc.dram_tensor(name, list(shape), dt, kind="ExternalInput").ap()

    do_l0 = part in ("L0", "ALL")
    do_l1 = part in ("L1", "ALL")
    din("xin", [TOK, D])
    din("ident", [128, 128])
    if do_l0:
        din("WqX", [16, 128, 16, 128], BF16); din("WkX", [16, 128, 16, 128], BF16)
        din("WvX", [16, 128, 16, 128], BF16); din("WoR", [16, 128, 2048], BF16)
        din("bq", [128, 16]); din("bk", [128, 16]); din("bv", [128, 16])
        din("bo_r", [1, 2048], BF16)
        din("W1X", [64, 128, 16, 128]); din("W2R", [64, 128, 2048])
        din("W1M", [16, 128, 1024]); din("W2M", [16, 128, 1024])
        din("b1t", [128, 64]); din("b2_r", [1, 2048], BF16)
        din("maskp", [8, 1])
    if do_l1:
        din("WinQX", [16, 128, 16, 128], BF16); din("WinKX", [16, 128, 16, 128], BF16)
        din("WinVX", [16, 128, 16, 128], BF16); din("WoutR", [16, 128, 2048], BF16)
        din("binQ", [128, 16]); din("binK", [128, 16]); din("binV", [128, 16])
        din("bout_r", [1, 2048], BF16)
        din("gam", [128, 16]); din("bet", [128, 16])
        din("RtX", [128, 16, 8], BF16); din("DnX", [128, 16, 128], BF16)
        din("UpA", [128, 2048], BF16)
        din("ropec", [128, 4, 1024], BF16); din("ropes", [128, 4, 1024], BF16)
        din("mask1", [8, 1]); din("esel", [8, 128], BF16)
    io["out"] = nc.dram_tensor("out", [TOK, D], F32, kind="ExternalOutput").ap()

    cc = {}
    if do_l0:
        cc["c1i"] = nc.dram_tensor("c1i", [128, 18], F32, kind="Internal")
        cc["c1o"] = nc.dram_tensor("c1o", [1024, 18], F32, kind="Internal")
    if do_l1:
        cc["c2i"] = nc.dram_tensor("c2i", [128, 16], F32, kind="Internal")
        cc["c2o"] = nc.dram_tensor("c2o", [1024, 16], F32, kind="Internal")

    with ExitStack() as ctx:
        e = ctx.enter_context
        slab = e(nc.sbuf_tensor("slab", [128, SLAB_COLS], F32))
        pb = [e(nc.psum_tensor("pb%d" % i, [128, 512], F32)) for i in range(8)]
        P = Prog(nc)
        for s in SEM_NAMES:
            P.sems[s] = e(nc.semaphore("sem_" + s))
        block = e(nc.Block())
        C.nc, C.P, C.slab, C.pb, C.io, C.cc = nc, P, slab, pb, io, cc
        C.sm = small_alloc()
        C.bank_data = [None] * 8
        C.bank_free = [None] * 8
        C.wb = [bf16v(C, WBF_OFF, 2048), bf16v(C, WBF_OFF + 1024, 2048)]
        C.wb_busy = [None, None]
        build_body(C)
        P.finalize(block)
    nc.finalize()
    return nc


def build_body(C):
    P, io, sm = C.P, C.io, C.sm
    do_l0 = C.part in ("L0", "ALL")
    do_l1 = C.part in ("L1", "ALL")

    consts = [P.dma("act", f32v(C, sm["ident"], 128), io["ident"], sem="cst")]
    xin3 = io["xin"].rearrange("(tc p) e -> tc p e", p=128)
    t_xc = []
    for tc8 in range(TC):
        t_xc.append(P.dma("sp" if tc8 % 2 == 0 else "act",
                          f32v(C, X_OFF + tc8 * 2048, 2048),
                          xin3[tc8], sem="xo%d" % tc8))
    t_x = U(*t_xc)
    C.t_xc = t_xc
    if do_l0:
        for nm in ("bq", "bk", "bv"):
            consts.append(P.dma("act", f32v(C, sm[nm], 16), io[nm], sem="cst"))
        consts.append(P.dma("act", f32v(C, sm["b1t"], 64), io["b1t"], sem="cst"))
        consts.append(P.dma("act", f32v(C, sm["mask"], 1, p=8), io["maskp"], sem="cst"))
    if do_l1:
        for nm in ("binQ", "binK", "binV", "gam", "bet"):
            consts.append(P.dma("act", f32v(C, sm[nm], 16), io[nm], sem="cst"))
        consts.append(P.dma("act", C.slab[0:8, sm["esel"]:sm["esel"] + 64].bitcast(BF16), io["esel"], sem="cst"))
        consts.append(P.dma("act", f32v(C, sm["mask"] + 1, 1, p=8), io["mask1"], sem="cst"))
    consts.append(P.op("dve", lambda E: E.memset(f32v(C, sm["ones_r"], 128, p=1), 1.0)))
    consts.append(P.op("dve", lambda E: E.memset(invs_bf(C), INVSCALE)))
    consts.append(P.op("dve", lambda E: E.memset(ones_bf_r(C), 1.0)))
    consts.append(P.op("dve", lambda E: E.memset(
        C.slab[:, sm["onesb"]:sm["onesb"] + 1].bitcast(BF16), 1.0)))
    consts.append(P.op("dve", lambda E: E.memset(
        C.slab[:, sm["onesb"] + 1:sm["onesb"] + 2], 1.0)))
    consts.append(P.op("dve", lambda E: E.memset(f32v(C, sm["carry"], 16), 0.0)))
    consts.append(P.op("dve", lambda E: E.memset(sm1(C, "half", 0), -0.5)))
    consts.append(P.op("dve", lambda E: E.memset(sm1(C, "half", 1), 0.5)))
    C.t_consts = U(*consts)
    C.t_x = t_x

    if do_l0:
        layer0(C)
    if do_l1:
        layer1(C)

    out3 = io["out"].rearrange("(tc p) e -> tc p e", p=128)
    xdc = getattr(C, "t_xdone_c", None)
    touts = []
    for tc8 in range(TC):
        touts.append(P.dma("sp", out3[tc8], f32v(C, X_OFF + tc8 * 2048, 2048),
                           waits=(xdc[tc8] if xdc else C.t_xdone),
                           sem="xo%d" % tc8))
    P.wait_only("sp", U(*touts))


def ln_stats(C, x_chunk, slot_negmu, slot_rstd, sq_scr, waits):
    P = C.P
    s1 = sm1(C, "tiny", 0)
    s2 = sm1(C, "tiny", 1)
    t1 = P.op("dve", (lambda E, o=s1, i=x_chunk:
                      E.reduce_sum(out=o, in_=i, axis=AX.X)), waits=waits)
    t2 = P.op("act", (lambda E, o=sq_scr, i=x_chunk, a=s2:
                      E.activation(out=o, in_=i, func=AF.Square, accum_out=a)),
              waits=waits)
    mu = sm1(C, "tiny", 2)
    t3 = P.op("dve", (lambda E, o=mu, i=s1:
                      E.tensor_scalar(out=o, in0=i, scalar1=1.0 / D, scalar2=None,
                                      op0=ALU.mult)), waits=t1)
    m2 = sm1(C, "tiny", 3)
    t4 = P.op("dve", (lambda E, o=m2, i=mu: E.tensor_mul(out=o, in0=i, in1=i)),
              waits=t3)
    var = sm1(C, "tiny", 4)
    t5 = P.op("dve", (lambda E, o=var, i=s2:
                      E.tensor_scalar(out=o, in0=i, scalar1=1.0 / D, scalar2=None,
                                      op0=ALU.mult)), waits=t2)
    t6 = P.op("dve", (lambda E, o=var, i=m2:
                      E.tensor_sub(out=o, in0=o, in1=i)), waits=U(t4, t5))
    t7 = P.op("dve", (lambda E, o=var:
                      E.tensor_scalar(out=o, in0=o, scalar1=EPS_LN, scalar2=None,
                                      op0=ALU.add)), waits=t6)
    sd = sm1(C, "tiny", 5)
    t8 = P.op("act", (lambda E, o=sd, i=var:
                      E.activation(out=o, in_=i, func=AF.Sqrt)), waits=t7)
    t9 = P.op("dve", (lambda E, o=slot_rstd, i=sd: E.reciprocal(out=o, in_=i)),
              waits=t8)
    t10 = P.op("dve", (lambda E, o=slot_negmu, i=mu:
                       E.tensor_scalar(out=o, in0=i, scalar1=-1.0, scalar2=None,
                                       op0=ALU.mult)), waits=t3)
    return U(t9, t10)



def ln_batch(C, waits, per_chunk=None):
    P, sm = C.P, C.sm
    lb = sm["lnb"]
    s1 = C.slab[:, lb:lb + 8]
    s2 = C.slab[:, lb + 8:lb + 16]
    negmu = C.slab[:, lb + 16:lb + 24]
    rstd = C.slab[:, lb + 24:lb + 32]
    tmp = C.slab[:, lb + 32:lb + 40]
    dump = f32v(C, SC_OFF, 2048)
    half_toks = []
    prev_sq = None
    for h in range(2):
        toks = []
        for tc in range(h * 4, h * 4 + 4):
            xc = f32v(C, X_OFF + tc * 2048, 2048)
            wc = U(waits, per_chunk[tc] if per_chunk else None)
            t1 = P.op("dve", (lambda E, o=s1[:, tc:tc + 1], i=xc:
                              E.reduce_sum(out=o, in_=i, axis=AX.X)), waits=wc)
            t2 = P.op("act", (lambda E, o=dump, i=xc, a=s2[:, tc:tc + 1]:
                              E.activation(out=o, in_=i, func=AF.Square,
                                           accum_out=a)),
                      waits=U(wc, prev_sq))
            prev_sq = t2
            toks.append(U(t1, t2))
        t0 = U(*toks)
        c0, c1 = h * 4, h * 4 + 4
        ta = P.op("dve", (lambda E, o=negmu[:, c0:c1], i=s1[:, c0:c1]:
                          E.tensor_scalar(out=o, in0=i, scalar1=-1.0 / D,
                                          scalar2=None, op0=ALU.mult)), waits=t0)
        tb = P.op("dve", (lambda E, o=tmp[:, c0:c1], i=negmu[:, c0:c1]:
                          E.tensor_mul(out=o, in0=i, in1=i)), waits=ta)
        tc_ = P.op("dve", (lambda E, o=rstd[:, c0:c1], i=s2[:, c0:c1]:
                           E.tensor_scalar(out=o, in0=i, scalar1=1.0 / D,
                                           scalar2=None, op0=ALU.mult)), waits=t0)
        td = P.op("dve", (lambda E, o=rstd[:, c0:c1], i=tmp[:, c0:c1]:
                          E.tensor_sub(out=o, in0=o, in1=i)), waits=U(tb, tc_))
        te = P.op("dve", (lambda E, o=rstd[:, c0:c1]:
                          E.tensor_scalar(out=o, in0=o, scalar1=EPS_LN,
                                          scalar2=None, op0=ALU.add)), waits=td)
        tf = P.op("act", (lambda E, o=rstd[:, c0:c1]:
                          E.activation(out=o, in_=o, func=AF.Sqrt)), waits=te)
        tg = P.op("dve", (lambda E, o=rstd[:, c0:c1]:
                          E.reciprocal(out=o, in_=o)), waits=tf)
        half_toks.append(U(tg, prev_sq))
    return half_toks


def transpose_chunk(C, src_f32, dst_bf_off, tc, waits, evac=None):
    P = C.P
    toks = []
    ident = f32v(C, C.sm["ident"], 128)
    dst = bf16v(C, dst_bf_off, 16384).rearrange("p (dc t) -> p dc t", t=1024)
    for grp in range(4):
        b = grp % 2
        w0 = U(C.bank_free[b], waits, C.t_consts)
        tmm = None
        for j in range(4):
            dc = grp * 4 + j
            tmm = P.op("pe",
                       (lambda E, o=C.pb[b][:, j * 128:(j + 1) * 128],
                        i=src_f32[:, dc * 128:(dc + 1) * 128], idn=ident:
                        E.transpose(out=o, in_=i, identity=idn)),
                       waits=(w0 if j == 0 else None), inc=(1 if j == 3 else 0))
        C.bank_data[b] = tmm
        if evac is None:
            dst_ap = dst[:, grp * 4:grp * 4 + 4, tc * 128:(tc + 1) * 128]
            src_ap = C.pb[b][:, 0:512].rearrange("p (dc t) -> p dc t", t=128)
            tev = P.op("dve", (lambda E, o=dst_ap, i=src_ap:
                               E.tensor_copy(out=o, in_=i)), waits=tmm)
        else:
            tes = []
            for j in range(4):
                dc = grp * 4 + j
                s_c, b_c = evac[dc]
                tes.append(P.op("act",
                    (lambda E, o=dst[:, dc, tc * 128:(tc + 1) * 128],
                     i=C.pb[b][:, j * 128:(j + 1) * 128], s_=s_c, bb=b_c:
                     E.activation(out=o, in_=i, func=AF.Identity,
                                  scale=s_, bias=bb)),
                    waits=(U(tmm, C.t_consts) if j == 0 else None)))
            tev = U(*tes)
        C.bank_free[b] = tev
        toks.append(tev)
    return U(*toks)


def layer0(C):
    P, io, sm = C.P, C.io, C.sm
    sc = f32v(C, SC_OFF, 2048)
    xnf = bf16v(C, XNF_OFF, 16384)
    wb, wb_busy = C.wb, C.wb_busy
    tsth = ln_batch(C, C.t_consts, per_chunk=C.t_xc)
    negmu8 = C.slab[:, sm["lnb"] + 16:sm["lnb"] + 24]
    rstd8 = C.slab[:, sm["lnb"] + 24:sm["lnb"] + 32]
    nb8 = C.slab[:, sm["lnb"] + 48:sm["lnb"] + 56]
    tnb_h = []
    for h in range(2):
        c0, c1 = h * 4, h * 4 + 4
        tnb_h.append(P.op("dve", (lambda E, o=nb8[:, c0:c1],
                                  i=negmu8[:, c0:c1], j=rstd8[:, c0:c1]:
                                  E.tensor_mul(out=o, in0=i, in1=j)),
                          waits=tsth[h]))
    xnf_toks = []
    for tc in range(TC):
        xc = f32v(C, X_OFF + tc * 2048, 2048)
        buf = sc if tc % 2 == 0 else f32v(C, SP2_OFF, 2048)
        tap = P.op("act", (lambda E, o=buf, i=xc,
                           a=rstd8[:, tc:tc + 1], b=nb8[:, tc:tc + 1]:
                           E.activation(out=o, in_=i, func=AF.Identity,
                                        scale=a, bias=b)),
                   waits=U(tnb_h[tc // 4],
                           xnf_toks[tc - 2] if tc >= 2 else None))
        xnf_toks.append(transpose_chunk(C, buf, XNF_OFF, tc, tap))
    C.t_xnf = U(*xnf_toks)
    C.t_xnf_h = [U(*xnf_toks[0:4]), U(*xnf_toks[4:8])]

    wfp = [f32v(C, ROWS_OFF, 1024), f32v(C, ROWS_OFF + 1024, 1024),
           f32v(C, WF32_OFF, 1024)]
    wfp_busy = [None, None, None]
    C.wfp, C.wfp_busy = wfp, wfp_busy
    NGM = 16
    mean_toks = []
    absdump = f32v(C, SLAB2_OFF + 512, 1024)
    w1run = C.slab[:, sm["carry"] + 16:sm["carry"] + 17]
    w2run = C.slab[:, sm["carry"] + 17:sm["carry"] + 18]
    wtmp = [sm1(C, "tiny", 11), sm1(C, "tiny", 12)]
    t_wz = P.op("dve", lambda E: E.memset(
        C.slab[:, sm["carry"] + 16:sm["carry"] + 18], 0.0))
    mean_state = {"i": 0, "p": 0, "pa": None,
                  "prun": [t_wz, t_wz], "pend": []}

    def pump_load(n):
        for _ in range(n):
            i = mean_state["i"]
            if i >= 2 * NGM:
                return
            mean_state["i"] += 1
            p = mean_state["p"]
            mean_state["p"] = (p + 1) % 3
            mi, gidx = (0, i) if i < NGM else (1, i - NGM)
            src = io["W1M"] if mi == 0 else io["W2M"]
            td = P.dma("sp", wfp[p], src[gidx], waits=wfp_busy[p],
                       sem="wfp%d" % p)
            mean_state["pend"].append((p, td, mi, i))

    def pump_drain():
        for (p, td, mi, i) in mean_state["pend"]:
            tb = i % 2
            th = P.op("act", (lambda E, i_=wfp[p], o=absdump, a=wtmp[tb]:
                              E.activation(out=o, in_=i_, func=AF.Abs,
                                           accum_out=a)),
                      waits=U(td, mean_state["pa"], mean_state["prun"][tb]))
            mean_state["pa"] = th
            run = w1run if mi == 0 else w2run
            tr2 = P.op("dve", (lambda E, o=run, i_=wtmp[tb]:
                               E.tensor_add(out=o, in0=o, in1=i_)),
                       waits=U(th, *mean_state["prun"]))
            mean_state["prun"][tb] = tr2
            wfp_busy[p] = th
            mean_toks.append(tr2)
        mean_state["pend"] = []

    zeros = f32v(C, SLAB2_OFF, 512)
    tz = P.op("dve", lambda E: E.memset(zeros, 0.0))
    kvv = f32v(C, KV_OFF, 16384)
    rslots = [WBF_OFF, WBF_OFF + 512, WBF_OFF + 1024, WBF_OFF + 1536,
              SLAB2_OFF + 2816, SLAB2_OFF + 3328]
    rbusy = [None] * 6
    ridx = [0]

    def wslot():
        i = ridx[0] % 6
        ridx[0] += 1
        return i, bf16v(C, rslots[i], 1024), "wk%d" % i

    def wload(WX, ec):
        views = []
        for hh in range(2):
            i, vw, sname = wslot()
            td = P.dma("pool", vw,
                       io[WX][ec, :, hh * 8:(hh + 1) * 8, :].rearrange(
                           "p dc f -> p (dc f)"),
                       waits=rbusy[i], sem=sname)
            views.append((i, vw, td))
        return views

    def wtiles(views):
        mms = []
        tds = []
        for hh, (i, vw, td) in enumerate(views):
            tds.append(td)
            for dcl in range(8):
                mms.append(vw[:, dcl * 128:(dcl + 1) * 128])
        return mms, U(*tds)

    gi = 0
    scan_prev = {}
    t_scan_all = []
    sc_reads = []
    prev_exp = None
    prev_t4 = None
    prev_tu = [None, None]
    for ec in range(16):
        pump_load(2)
        kviews = wload("WkX", ec)
        vviews = wload("WvX", ec)
        ktiles, tdk = wtiles(kviews)
        vtiles, tdv = wtiles(vviews)
        phi = [None, None]
        k_reads = []
        k_mms = []
        exps = []
        for s in range(2):
            bk_k = 2 + 2 * s
            mms = [(ktiles[dc],
                    xnf[:, dc * 1024 + s * 512: dc * 1024 + s * 512 + 512])
                   for dc in range(16)]
            tk = mm_group(C, bk_k, mms, waits=U(tdk, C.t_xnf_h[s]),
                          out_slice=C.pb[bk_k][:, 0:512])
            k_mms.append(tk)
            bkc = sm1(C, "bk", ec)
            mslot = sc[:, s * 2048 // 8: s * 256 + 256] if False else                 sc[:, 0:512] if s == 0 else sc[:, 512:1024]
            eslot = sc[:, 1536 + 0:1536 + 256].bitcast(BF16) if False else None
            eslot = (sc[:, 0:512], sc[:, 512:1024])[s]
            t1 = P.op("dve", (lambda E, o=mslot, i=C.pb[bk_k][:, 0:512], a=bkc:
                              E.tensor_scalar(out=o, in0=i, scalar1=a, scalar2=0.0,
                                              op0=ALU.add, op1=ALU.min)),
                      waits=U(tk, C.t_consts, prev_exp, prev_t4))
            t2 = P.op("act", (lambda E, o=eslot, i=mslot:
                              E.activation(out=o, in_=i, func=AF.Exp)),
                      waits=U(t1,))
            prev_exp = t2
            exps.append((tk, t2))
        for s in range(2):
            bk_k = 2 + 2 * s
            tk, t2 = exps[s]
            bkc = sm1(C, "bk", ec)
            eslot = (sc[:, 0:512], sc[:, 512:1024])[s]
            phis = sc[:, 1024 + s * 512: 1536 + s * 512]
            t3 = P.op("dve", (lambda E, o=phis, i=C.pb[bk_k][:, 0:512], a=bkc:
                              E.tensor_scalar(out=o, in0=i, scalar1=a, scalar2=0.0,
                                              op0=ALU.add, op1=ALU.max)),
                      waits=U(tk, prev_tu[s]))
            t4 = P.op("dve", (lambda E, o=phis, i=eslot:
                              E.tensor_add(out=o, in0=o, in1=i)), waits=U(t2, t3))
            prev_t4 = t4
            bank_read(C, bk_k, U(t3, t4))
            phi[s] = (phis, t4)
            k_reads.append(t4)
        for (i, vw, td) in kviews:
            rbusy[i] = U(*k_mms)
        v_reads = []
        for s in range(2):
            bk_v = 3 + 2 * s
            mms = [(vtiles[dc],
                    xnf[:, dc * 1024 + s * 512: dc * 1024 + s * 512 + 512])
                   for dc in range(16)]
            tv = mm_group(C, bk_v, mms, waits=U(tdv, C.t_xnf_h[s]),
                          out_slice=C.pb[bk_v][:, 0:512])
            v_reads.append(tv)
            kvsl = kvv[:, ec * 1024 + s * 512: ec * 1024 + s * 512 + 512]
            bvc = sm1(C, "bv", ec)
            phis, tphi = phi[s]
            tu = P.op("dve", (lambda E, o=kvsl, i=C.pb[bk_v][:, 0:512], a=bvc,
                              h=phis:
                              E.scalar_tensor_tensor(out=o, in0=i, scalar=a, in1=h,
                                                     op0=ALU.add, op1=ALU.mult)),
                      waits=U(tv, tphi, C.t_consts))
            prev_tu[s] = tu
            bank_read(C, bk_v, tu)
            sc_reads.append(tu)
            csl = C.slab[:, sm["carry"] + ec: sm["carry"] + ec + 1]
            tsc = P.op("dve", (lambda E, o=kvsl, z=zeros, ini=csl:
                               E.tensor_tensor_scan(out=o, data0=o, data1=z,
                                                    initial=ini, op0=ALU.add,
                                                    op1=ALU.add)),
                       waits=U(tu, tz, C.t_consts, scan_prev.get(ec)))
            tcar = P.op("dve", (lambda E, o=csl,
                                i=kvv[:, ec * 1024 + s * 512 + 511:
                                      ec * 1024 + s * 512 + 512]:
                                E.tensor_copy(out=o, in_=i)), waits=tsc)
            scan_prev[ec] = tcar
            t_scan_all.append(tcar)
        for (i, vw, td) in vviews:
            rbusy[i] = U(*v_reads)
        pump_drain()
    C.t_scan = U(*t_scan_all)
    C.t_sc_attn = U(*sc_reads)
    t_meandone = U(*mean_toks)

    carry = f32v(C, sm["carry"], 18)
    c1i, c1o = C.cc["c1i"], C.cc["c1o"]
    td = P.dma("pool", c1i.ap(), carry, waits=U(C.t_scan, t_meandone),
               sem="cci")
    if SIM_NOCC:
        tl = []
        for r in range(8):
            tl.append(P.dma("pool", c1o.ap()[r * 128:(r + 1) * 128, :],
                            c1i.ap(), waits=td, sem="agl"))
        C.t_cc1 = U(*tl)
    else:
        C.t_cc1 = P.op("pool", (lambda E: E.collective_compute(
            "AllGather", ALU.bypass, replica_groups=RG,
            ins=[c1i.ap().opt()], outs=[c1o.ap().opt()])), waits=td)

    rbusy = list(rbusy)
    ridx = [0]

    def q_mm(ec):
        qtiles = []
        tds = []
        for hh in range(2):
            i = ridx[0] % 6
            ridx[0] += 1
            vw = bf16v(C, rslots[i], 1024)
            tdg = P.dma("sp", vw,
                        io["WqX"][ec, :, hh * 8:(hh + 1) * 8, :].rearrange(
                            "p dc f -> p (dc f)"),
                        waits=rbusy[i], sem="wq%d" % i)
            tds.append((i, tdg))
            for dcl in range(8):
                qtiles.append(vw[:, dcl * 128:(dcl + 1) * 128])
        tdq = U(*[t for _, t in tds])
        q_mms = []
        banks = []
        for s in range(2):
            bq_b = (0, 1, 2, 3, 4, 5)[(2 * ec + s) % 6]
            banks.append(bq_b)
            mms = [(qtiles[dc],
                    xnf[:, dc * 1024 + s * 512: dc * 1024 + s * 512 + 512])
                   for dc in range(16)]
            tq = mm_group(C, bq_b, mms, waits=U(tdq, C.t_xnf_h[s]),
                          out_slice=C.pb[bq_b][:, 0:512])
            q_mms.append(tq)
        for i, _ in tds:
            rbusy[i] = U(*q_mms)
        return q_mms, banks

    q_pre = {}
    for ec in range(3):
        q_pre[ec] = q_mm(ec)

    s_ag = f32v(C, SC_OFF, 2304, p=8)
    tld = P.dma("pool", s_ag, c1o.ap().rearrange("(r p) c -> r (p c)", p=128),
                waits=U(C.t_cc1, C.t_sc_attn), sem="agl")
    maskp = C.slab[0:8, sm["mask"]:sm["mask"] + 1]
    ones8 = C.slab[0:8, sm["onesb"] + 1:sm["onesb"] + 2]
    w0 = U(tld, C.bank_free[6], C.t_consts)
    tmo = None
    s_ag3 = s_ag.rearrange("r (p c) -> r p c", c=18)
    for cch in range(18):
        rr = maskp if cch < 16 else ones8
        tmo = P.op("pe",
                   (lambda E, o=C.pb[6][:, cch:cch + 1],
                    l=s_ag3[:, :, cch], r=rr:
                    E.matmul(out=o, lhsT=l, rhs=r, start=True, stop=True)),
                   waits=(w0 if cch == 0 else None), inc=(1 if cch == 17 else 0))
    C.bank_data[6] = tmo
    offs = f32v(C, sm["offs"], 18)
    tof = P.op("dve", (lambda E, o=offs, i=C.pb[6][:, 0:18]:
                       E.tensor_copy(out=o, in_=i)), waits=tmo)
    C.bank_free[6] = tof
    C.t_offs = tof
    C.t_agread = U(tof, tld)
    wfp_busy[2] = U(wfp_busy[2], C.t_agread)

    onesc = C.slab[:, sm["onesb"] + 1:sm["onesb"] + 2]
    w1vec = C.slab[:, sm["offs"] + 16:sm["offs"] + 17]
    w2vec = C.slab[:, sm["offs"] + 17:sm["offs"] + 18]
    w0 = U(C.bank_free[6], C.t_consts)
    tmm = P.op("pe", (lambda E, o=C.pb[6][0:1, 0:1], l=w1vec, r=onesc:
                      E.matmul(out=o, lhsT=l, rhs=r, start=True, stop=True)),
               waits=w0)
    C.bank_data[6] = tmm
    mean_s = C.slab[0:1, sm["mean"]:sm["mean"] + 1]
    t1m = P.op("dve", (lambda E, o=mean_s, i=C.pb[6][0:1, 0:1]:
                       E.tensor_scalar(out=o, in0=i, scalar1=1.0 / (DFF * D),
                                       scalar2=EPS_BIT, op0=ALU.mult,
                                       op1=ALU.max)), waits=tmm)
    C.bank_free[6] = t1m
    t2m = P.op("dve", (lambda E, o=mean_s:
                       E.tensor_scalar(out=o, in0=o, scalar1=0.5, scalar2=None,
                                       op0=ALU.mult)), waits=t1m)
    w0 = U(C.bank_free[7], t2m, C.t_consts)
    tbc = P.op("pe", (lambda E, o=C.pb[7][:, 0:1], l=ones_rr(C), r=mean_s:
                      E.matmul(out=o, lhsT=l, rhs=r, start=True, stop=True)),
               waits=w0)
    C.bank_data[7] = tbc
    thrp1 = C.slab[:, sm["thr"]:sm["thr"] + 1]
    thrn1 = C.slab[:, sm["thr"] + 1:sm["thr"] + 2]
    t3m = P.op("dve", (lambda E, o=thrp1, i=C.pb[7][:, 0:1]:
                       E.tensor_copy(out=o, in_=i)), waits=tbc)
    C.bank_free[7] = t3m
    t4m = P.op("dve", (lambda E, o=thrn1, i=thrp1:
                       E.tensor_scalar(out=o, in0=i, scalar1=-1.0, scalar2=None,
                                       op0=ALU.mult)), waits=t3m)
    C.t_thr1 = U(t3m, t4m)

    w0 = U(C.bank_free[6], C.t_consts)
    tmm2 = P.op("pe", (lambda E, o=C.pb[6][0:1, 0:1], l=w2vec, r=onesc:
                       E.matmul(out=o, lhsT=l, rhs=r, start=True, stop=True)),
                waits=w0)
    C.bank_data[6] = tmm2
    mean2_s = C.slab[0:1, sm["mean"] + 1: sm["mean"] + 2]
    t1i = P.op("dve", (lambda E, o=mean2_s, i=C.pb[6][0:1, 0:1]:
                       E.tensor_scalar(out=o, in0=i, scalar1=1.0 / (DFF * D),
                                       scalar2=EPS_BIT, op0=ALU.mult,
                                       op1=ALU.max)), waits=tmm2)
    C.bank_free[6] = t1i
    t2i = P.op("dve", (lambda E, o=mean2_s:
                       E.reciprocal(out=o, in_=o)), waits=t1i)
    w0 = U(C.bank_free[7], t2i, C.t_consts)
    tbc2 = P.op("pe", (lambda E, o=C.pb[7][:, 0:1], l=ones_rr(C), r=mean2_s:
                       E.matmul(out=o, lhsT=l, rhs=r, start=True, stop=True)),
                waits=w0)
    C.bank_data[7] = tbc2
    inv2s = C.slab[:, sm["thr"] + 2:sm["thr"] + 3]
    t3i = P.op("dve", (lambda E, o=inv2s, i=C.pb[7][:, 0:1]:
                       E.tensor_copy(out=o, in_=i)), waits=tbc2)
    C.bank_free[7] = t3i
    C.t_thr = U(C.t_thr1, t3i)

    def ctx_tile(dc, tch):
        s, i0 = tch // 4, (tch % 4) * 64
        return bf16v(C, KV_OFF + dc * 1024 + s * 512 + i0, 128)
    prev_exp = None
    prev_t4q = None
    prev_t5 = None
    ctx_t = []
    for ec in range(16):
        q_mms, q_banks = q_pre.pop(ec) if ec in q_pre else q_mm(ec)
        for s in range(2):
            bq_b = q_banks[s]
            tq = q_mms[s]
            bqc = sm1(C, "bq", ec)
            mslot = sc[:, 0:512]; eslot = sc[:, 512:1024]; phis = sc[:, 1024:1536]
            t1 = P.op("dve", (lambda E, o=mslot, i=C.pb[bq_b][:, 0:512], a=bqc:
                              E.tensor_scalar(out=o, in0=i, scalar1=a, scalar2=0.0,
                                              op0=ALU.add, op1=ALU.min)),
                      waits=U(tq, C.t_agread, prev_exp))
            t2 = P.op("act", (lambda E, o=eslot, i=mslot:
                              E.activation(out=o, in_=i, func=AF.Exp)),
                      waits=U(t1, prev_t4q))
            prev_exp = t2
            t3 = P.op("dve", (lambda E, o=phis, i=C.pb[bq_b][:, 0:512], a=bqc:
                              E.tensor_scalar(out=o, in0=i, scalar1=a, scalar2=0.0,
                                              op0=ALU.add, op1=ALU.max)),
                      waits=U(tq, prev_t5))
            t4 = P.op("dve", (lambda E, o=phis, i=eslot:
                              E.tensor_add(out=o, in0=o, in1=i)), waits=U(t2, t3))
            prev_t4q = t4
            bank_read(C, bq_b, U(t1, t3))
            kvsl = kvv[:, ec * 1024 + s * 512: ec * 1024 + s * 512 + 512]
            ctxsl = kvsl.bitcast(BF16)[:, 0:512]
            oc = C.slab[:, sm["offs"] + ec: sm["offs"] + ec + 1]
            t5 = P.op("dve", (lambda E, o=ctxsl, i=kvsl, a=oc, h=phis:
                              E.scalar_tensor_tensor(out=o, in0=i, scalar=a,
                                                     in1=h, op0=ALU.add,
                                                     op1=ALU.mult)),
                      waits=U(t4, C.t_offs))
            prev_t5 = t5
            ctx_t.append(t5)
    t_ctx = U(*ctx_t)
    bo_seg = C.slab[0:1, SLAB2_OFF:SLAB2_OFF + 256].bitcast(BF16)
    invs = invs_bf(C)
    t_boseg = None
    t_boseg_rd = None
    wo_x = [None] * 8
    t_wo_last = []
    for eg in range(4):
        t_boseg = P.dma("act", bo_seg, io["bo_r"][:, eg * 512:(eg + 1) * 512],
                        waits=U(t_ctx, t_boseg_rd), sem="brow")
        for ecp in range(8):
            i = ridx[0] % 6
            ridx[0] += 1
            vw = bf16v(C, rslots[i], 1024).rearrange("p (d e) -> p d e", e=512)
            tdw = P.dma("sp", vw,
                        io["WoR"][ecp * 2:ecp * 2 + 2, :, eg * 512:(eg + 1) * 512]
                        .rearrange("d p e -> p d e"),
                        waits=rbusy[i], sem="wq%d" % i)
            last = None
            for dcl in range(2):
                dc = ecp * 2 + dcl
                for tch in range(8):
                    st = (dc == 0)
                    w = U(C.bank_free[tch] if st else None,
                          t_ctx if st else None, tdw)
                    last = P.op("pe",
                        (lambda E, o=C.pb[tch][:, 0:512],
                         l=ctx_tile(dc, tch),
                         r=vw[:, dcl, :], s_=st:
                         E.matmul(out=o, lhsT=l, rhs=r, start=s_, stop=False)),
                        waits=w, inc=(1 if tch == 7 else 0))
            rbusy[i] = last
        seg_rd = []
        for tch in range(8):
            tokmm = P.op("pe", (lambda E, o=C.pb[tch][:, 0:512], l=invs,
                                r=bo_seg:
                                E.matmul(out=o, lhsT=l, rhs=r, start=False,
                                         stop=True)),
                         waits=U(t_boseg, C.t_consts), inc=1)
            seg_rd.append(tokmm)
            C.bank_data[tch] = tokmm
            xsl = f32v(C, X_OFF + tch * 2048 + eg * 512, 512)
            tev = P.op("dve", (lambda E, o=xsl, i=C.pb[tch][:, 0:512]:
                               E.scalar_tensor_tensor(out=o, in0=i, scalar=SCALE,
                                                      in1=o, op0=ALU.mult,
                                                      op1=ALU.add)),
                       waits=U(tokmm, C.t_xnf, wo_x[tch]))
            C.bank_free[tch] = tev
            wo_x[tch] = tev
            t_wo_last.append(tev)
        t_boseg_rd = U(*seg_rd)
    t_wo_done = U(*t_wo_last)
    C.t_ffn_x = wo_x
    C.t_hfree0 = t_wo_done
    C.t_sc_q = U(t_ctx, t_wo_done)
    C.t_twos = P.op("dve", lambda E: E.memset(invs_bf(C), 2.0),
                    waits=t_wo_done)

    if DBG_STOP == "wo":
        C.t_xdone = t_wo_done
        C.t_rows_free = C.t_xdone
        C.t_kv_read = U(t_ctx, t_wo_done)
        return
    ffn(C)
    C.t_xdone = U(*C.t_ffn_x)
    if DBG_STOP == "ffn":
        pass
    C.t_rows_free = C.t_xdone
    C.t_kv_read = U(t_ctx, t_wo_done)


def ffn(C):
    P, io, sm = C.P, C.io, C.sm
    xnf = bf16v(C, XNF_OFF, 16384)
    wfp = list(C.wfp) + [f32v(C, KV_OFF + 3072, 1024),
                         f32v(C, KV_OFF + 4096, 1024)]
    wfp_busy = list(C.wfp_busy) + [C.t_sc_q, C.t_sc_q]
    wbX = [bf16v(C, KV_OFF + 5120, 2048), bf16v(C, KV_OFF + 6144, 2048)]
    wbX_busy = [C.t_sc_q, C.t_sc_q]
    f2w = [bf16v(C, WBF_OFF + i * 512, 1024) for i in range(4)] + \
          [bf16v(C, KV_OFF + 1024 + i * 512, 1024) for i in range(4)]
    f2w_busy = [U(C.wb_busy[0], C.t_sc_q), U(C.wb_busy[0], C.t_sc_q),
                U(C.wb_busy[1], C.t_sc_q), U(C.wb_busy[1], C.t_sc_q),
                C.t_sc_q, C.t_sc_q, C.t_sc_q, C.t_sc_q]
    thrp1 = C.slab[:, sm["thr"]:sm["thr"] + 1]
    thrn1 = C.slab[:, sm["thr"] + 1:sm["thr"] + 2]
    inv2 = C.slab[:, sm["thr"] + 2:sm["thr"] + 3]
    hslab = bf16v(C, SLAB2_OFF, 8192)
    bscrF = bf16v(C, KV_OFF + 512, 1024)
    sgn = [bf16v(C, SC_OFF + i * 512, 1024) for i in range(4)]
    sgn_busy = [C.t_sc_q] * 4
    b2seg = C.slab[0:1, KV_OFF:KV_OFF + 256].bitcast(BF16)

    st_ = {"gfp": 0, "prev_tc": C.t_sc_q, "t_b2seg_rd": None,
           "t1": {}, "t2": {}}
    t_hfree = C.t_hfree0

    def wfp_slot():
        pf = st_["gfp"] % 5
        st_["gfp"] += 1
        return pf

    def tern1(g, fcl):
        fc = g * 8 + fcl
        pw = fcl % 2
        w_prev = wbX_busy[pw]
        tern_tok = []
        for hh in range(2):
            pf = wfp_slot()
            hsl_in = io["W1X"][fc, :, hh * 8:(hh + 1) * 8, :].rearrange(
                "p dc f -> p (dc f)")
            td = P.dma("sp", wfp[pf], hsl_in, waits=wfp_busy[pf],
                       sem="wfp%d" % pf)
            dst = wbX[pw][:, hh * 1024:(hh + 1) * 1024]
            ta = P.op("dve", (lambda E, o=dst, i=wfp[pf], a=thrp1:
                              E.tensor_scalar(out=o, in0=i, scalar1=a,
                                              scalar2=None, op0=ALU.is_gt)),
                      waits=U(td, C.t_thr, w_prev))
            tb = P.op("dve", (lambda E, o=bscrF, i=wfp[pf], a=thrn1:
                              E.tensor_scalar(out=o, in0=i, scalar1=a,
                                              scalar2=None, op0=ALU.is_lt)),
                      waits=U(td, C.t_thr, st_["prev_tc"]))
            tc_ = P.op("dve", (lambda E, o=dst, i=bscrF:
                               E.tensor_sub(out=o, in0=o, in1=i)),
                       waits=U(ta, tb))
            st_["prev_tc"] = tc_
            wfp_busy[pf] = U(ta, tb)
            tern_tok.append(tc_)
        wbX_busy[pw] = None
        st_["t1"][(g, fcl)] = U(*tern_tok)

    def tern2(g, eg):
        toks = []
        for pair in range(4):
            fc = g * 8 + pair * 2
            pf = wfp_slot()
            gr = io["W2R"][fc:fc + 2, :, eg * 512:(eg + 1) * 512].rearrange(
                "f p e -> p f e")
            td = P.dma("sp", wfp[pf].rearrange("p (f e) -> p f e", e=512),
                       gr, waits=wfp_busy[pf], sem="wfp%d" % pf)
            ws = 4 * (eg % 2) + pair
            sa, sb = 2 * (pair % 2), 2 * (pair % 2) + 1
            ts1 = P.op("act", (lambda E, o=sgn[sa], i=wfp[pf], a=inv2,
                               b=sm1(C, "half", 0):
                               E.activation(out=o, in_=i, func=AF.Sign,
                                            scale=a, bias=b)),
                       waits=U(td, C.t_thr, sgn_busy[sa]))
            ts2 = P.op("act", (lambda E, o=sgn[sb], i=wfp[pf], a=inv2,
                               b=sm1(C, "half", 1):
                               E.activation(out=o, in_=i, func=AF.Sign,
                                            scale=a, bias=b)),
                       waits=U(td, C.t_thr, sgn_busy[sb]))
            tsum = P.op("dve", (lambda E, o=f2w[ws], i=sgn[sa],
                                j=sgn[sb]:
                                E.tensor_add(out=o, in0=i, in1=j)),
                        waits=U(ts1, ts2, f2w_busy[ws]))
            sgn_busy[sa] = tsum
            sgn_busy[sb] = tsum
            wfp_busy[pf] = U(ts1, ts2)
            f2w_busy[ws] = None
            toks.append((ws, tsum))
        st_["t2"][(g, eg)] = toks

    def mm2(g, eg, t_h, ev_all):
        toks = st_["t2"].pop((g, eg))
        if g == 0:
            tseg = P.dma("act", b2seg, io["b2_r"][:, eg * 512:(eg + 1) * 512],
                         waits=U(C.t_thr, st_["t_b2seg_rd"], C.t_sc_q),
                         sem="b2s")
        for pair in range(4):
            ws, tsum = toks[pair]
            dst = f2w[ws]
            last = None
            for half in range(2):
                fcl = pair * 2 + half
                laststop = (fcl == 7 and g > 0)
                for tch in range(8):
                    stt = (fcl == 0)
                    w = U(C.bank_free[tch] if stt else None,
                          t_h if stt else None, tsum)
                    last = P.op("pe",
                        (lambda E, o=C.pb[tch][:, 0:512],
                         l=hslab[:, fcl * 1024 + tch * 128:
                                 fcl * 1024 + (tch + 1) * 128],
                         r=dst[:, half * 512:(half + 1) * 512],
                         s_=stt, sp=laststop:
                         E.matmul(out=o, lhsT=l, rhs=r, start=s_, stop=sp)),
                        waits=w, inc=(1 if tch == 7 else 0))
            f2w_busy[ws] = last
        seg_rd = []
        for tch in range(8):
            if g == 0:
                tokb = P.op("pe", (lambda E, o=C.pb[tch][:, 0:512],
                                   l=invs_bf(C),
                                   r=b2seg:
                                   E.matmul(out=o, lhsT=l, rhs=r, start=False,
                                            stop=True)),
                            waits=U(tseg, C.t_twos), inc=1)
                seg_rd.append(tokb)
            else:
                tokb = last
            C.bank_data[tch] = tokb
            xsl = f32v(C, X_OFF + tch * 2048 + eg * 512, 512)
            tev = P.op("dve", (lambda E, o=xsl, i=C.pb[tch][:, 0:512]:
                               E.scalar_tensor_tensor(out=o, in0=i, scalar=0.5,
                                                      in1=o, op0=ALU.mult,
                                                      op1=ALU.add)),
                       waits=U(tokb, C.t_ffn_x[tch]))
            C.bank_free[tch] = tev
            C.t_ffn_x[tch] = tev
            ev_all.append(tev)
        if g == 0:
            st_["t_b2seg_rd"] = U(*seg_rd)
    tern1(0, 0)
    tern1(0, 1)
    for g in range(8):
        h_toks = []
        for fcl in range(8):
            fc = g * 8 + fcl
            pw = fcl % 2
            ttern = st_["t1"].pop((g, fcl))
            for s in range(2):
                b = (2 * fcl + s) % 4
                mms = [(wbX[pw][:, dc * 128:(dc + 1) * 128],
                        xnf[:, dc * 1024 + s * 512: dc * 1024 + s * 512 + 512])
                       for dc in range(16)]
                tm = mm_group(C, b, mms, waits=U(ttern, C.t_xnf),
                              out_slice=C.pb[b][:, 0:512])
                wbX_busy[pw] = U(wbX_busy[pw], tm)
                hsl = hslab[:, fcl * 1024 + s * 512: fcl * 1024 + s * 512 + 512]
                b1c = C.slab[:, sm["b1t"] + fc: sm["b1t"] + fc + 1]
                tg = P.op("act", (lambda E, o=hsl, i=C.pb[b][:, 0:512], a=b1c:
                                  E.activation(out=o, in_=i, func=AF.Gelu, bias=a)),
                          waits=U(tm, C.t_consts, t_hfree))
                bank_read(C, b, tg)
                h_toks.append(tg)
            if fcl < 6:
                tern1(g, fcl + 2)
            if fcl == 5:
                tern2(g, 0)
        t_h = U(*h_toks)
        ev_all = []
        tern2(g, 1)
        mm2(g, 0, t_h, ev_all)
        tern2(g, 2)
        mm2(g, 1, t_h, ev_all)
        tern2(g, 3)
        mm2(g, 2, t_h, ev_all)
        if g < 7:
            tern1(g + 1, 0)
            tern1(g + 1, 1)
        mm2(g, 3, t_h, ev_all)
        t_hfree = U(*ev_all)
    C.wb_busy[0] = U(f2w_busy[0], f2w_busy[1])
    C.wb_busy[1] = U(f2w_busy[2], f2w_busy[3])
    C.t_sc_free = t_hfree
    C.t_slab2_free = t_hfree
    C.t_rows_free_ffn = t_hfree


def layer1(C):
    P, io, sm = C.P, C.io, C.sm
    sc = f32v(C, SC_OFF, 2048)
    sc2 = f32v(C, SC2_OFF, 2048)
    sq = f32v(C, SP2_OFF, 2048)
    wb, wb_busy = C.wb, C.wb_busy
    if C.part == "ALL":
        base_wait = C.t_xdone
        wkv = C.t_kv_read
        t_slab2 = C.t_xdone
        t_sc_free = C.t_xdone
        t_rows = C.t_rows_free
    else:
        base_wait = U(C.t_x, C.t_consts)
        wkv = None
        t_slab2 = None
        t_sc_free = None
        t_rows = C.t_consts
    C.t_kv_l1_free = wkv

    if C.part == "ALL":
        tst = U(*ln_batch(C, C.t_consts, per_chunk=C.t_ffn_x))
    else:
        tst = U(*ln_batch(C, U(base_wait, C.t_consts)))
    lb = sm["lnb"]
    negmu8 = C.slab[:, lb + 16:lb + 24]
    rstd8 = C.slab[:, lb + 24:lb + 32]
    tmp8 = C.slab[:, lb + 32:lb + 40]
    rstd2_8 = C.slab[:, lb + 40:lb + 48]
    zscale8 = C.slab[:, sm["stat"]:sm["stat"] + 8]
    mA8 = C.slab[:, sm["stat"] + 8:sm["stat"] + 16]
    ta = P.op("dve", (lambda E, o=tmp8, i=rstd8:
                      E.tensor_mul(out=o, in0=i, in1=i)), waits=tst)
    tb = P.op("dve", (lambda E, o=rstd2_8, i=tmp8:
                      E.tensor_scalar(out=o, in0=i, scalar1=-EPS_LN,
                                      scalar2=1.0 + EPS_LN, op0=ALU.mult,
                                      op1=ALU.add)), waits=ta)
    tc0 = P.op("act", (lambda E, o=rstd2_8:
                       E.activation(out=o, in_=o, func=AF.Sqrt)), waits=tb)
    td = P.op("dve", (lambda E, o=rstd2_8: E.reciprocal(out=o, in_=o)),
              waits=tc0)
    te = P.op("dve", (lambda E, o=zscale8, i=rstd8, j=rstd2_8:
                      E.tensor_mul(out=o, in0=i, in1=j)), waits=td)
    tf = P.op("dve", (lambda E, o=mA8, i=rstd8:
                      E.tensor_scalar(out=o, in0=i, scalar1=2.0, scalar2=1.0,
                                      op0=ALU.mult, op1=ALU.add)), waits=tst)
    tg = P.op("dve", (lambda E, o=tmp8, i=negmu8, j=rstd8:
                      E.tensor_mul(out=o, in0=i, in1=j)), waits=U(tst, te))
    th = P.op("dve", (lambda E, o=tmp8:
                      E.tensor_scalar(out=o, in0=o, scalar1=2.0, scalar2=None,
                                      op0=ALU.mult)), waits=tg)
    tsc_all = U(te, tf, th)
    nb8 = C.slab[:, sm["lnb"] + 48:sm["lnb"] + 56]
    tnb = P.op("dve", (lambda E, o=nb8, i=negmu8, j=rstd8:
                       E.tensor_mul(out=o, in0=i, in1=j)), waits=tst)
    gb = [(C.slab[:, sm["gam"] + dc: sm["gam"] + dc + 1],
           C.slab[:, sm["bet"] + dc: sm["bet"] + dc + 1]) for dc in range(16)]
    xnf_toks, znf_toks = [], []
    tap_toks, tz_toks = [], []
    C.t_xup = [None] * 8

    def emit_tap(tc):
        xc = f32v(C, X_OFF + tc * 2048, 2048)
        bx = sc if tc % 2 == 0 else sq
        tap = P.op("act", (lambda E, o=bx, i=xc, a=rstd8[:, tc:tc + 1],
                           b=nb8[:, tc:tc + 1]:
                           E.activation(out=o, in_=i, func=AF.Identity,
                                        scale=a, bias=b)),
                   waits=U(tnb, t_sc_free, wkv,
                           xnf_toks[tc - 2] if tc >= 2 else None))
        tap_toks.append(tap)

    def emit_tz(tc):
        xc = f32v(C, X_OFF + tc * 2048, 2048)
        bz = sc2 if tc % 2 == 0 else f32v(C, TAB_OFF, 2048)
        tz = P.op("dve", (lambda E, o=bz, i=xc, a=negmu8[:, tc:tc + 1],
                          b=zscale8[:, tc:tc + 1]:
                          E.tensor_scalar(out=o, in0=i, scalar1=a, scalar2=b,
                                          op0=ALU.add, op1=ALU.mult)),
                  waits=U(tsc_all, wkv,
                          znf_toks[tc - 2] if tc >= 2 else None))
        tz_toks.append(tz)

    emit_tap(0)
    emit_tz(0)
    for tc in range(TC):
        bx = sc if tc % 2 == 0 else sq
        bz = sc2 if tc % 2 == 0 else f32v(C, TAB_OFF, 2048)
        if tc + 1 < TC:
            emit_tap(tc + 1)
        xnf_toks.append(transpose_chunk(C, bx, XNF_OFF, tc, tap_toks[tc]))
        if tc + 1 < TC:
            emit_tz(tc + 1)
        znf_toks.append(transpose_chunk(C, bz, ZNF_OFF, tc, tz_toks[tc],
                                        evac=gb))
    C.t_xnf = U(*xnf_toks)
    def emit_xup():
        for tc in range(TC):
            xc = f32v(C, X_OFF + tc * 2048, 2048)
            tx = P.op("dve", (lambda E, o=xc, a=mA8[:, tc:tc + 1],
                              b=tmp8[:, tc:tc + 1]:
                              E.tensor_scalar(out=o, in0=o, scalar1=a, scalar2=b,
                                              op0=ALU.mult, op1=ALU.add)),
                      waits=U(tap_toks[tc], tz_toks[tc], tsc_all))
            C.t_xup[tc] = tx
    t_sq_guard = U(xnf_toks[1], xnf_toks[3], xnf_toks[5], xnf_toks[7])
    t_tab_guard = U(znf_toks[1], znf_toks[3], znf_toks[5], znf_toks[7])
    xnf = bf16v(C, XNF_OFF, 16384)
    znf = bf16v(C, ZNF_OFF, 16384)
    C.t_Xnf = U(*znf_toks)

    tab = [bf16v(C, TAB_OFF, 4096), bf16v(C, TAB_OFF, 4096)]
    tab_busy = [t_tab_guard, t_tab_guard]
    Pslab = bf16v(C, SLAB2_OFF, 8192)
    onesbf = C.slab[:, sm["onesb"]:sm["onesb"] + 1].bitcast(BF16)[:, 0:1]
    ident = f32v(C, sm["ident"], 128)

    def rope_ap(p, half):
        return tab[p][:, half * 2048: half * 2048 + 2048].rearrange(
            "p (cc t) -> p cc t", t=512)

    gi = 0
    Apart_toks = []
    C.prev_scrd = None
    C.prev_rot = None
    wbL = [wb[0], wb[1],
           bf16v(C, HF_OFF, 2048), bf16v(C, WH_OFF, 2048)]
    wbL_busy = [wb_busy[0], wb_busy[1],
                U(wkv, base_wait), U(wkv, base_wait)]
    wbL_sem = ["wb0", "wb1", "wi2", "wi3"]
    for s in range(2):
        ptab = 0
        tdc = P.dma("sp", rope_ap(ptab, 0),
                    io["ropec"][:, :, s * 512:(s + 1) * 512],
                    waits=U(tab_busy[ptab], wkv), sem="tab")
        tds = P.dma("sp", rope_ap(ptab, 1),
                    io["ropes"][:, :, s * 512:(s + 1) * 512],
                    waits=U(tab_busy[ptab], wkv), sem="tab")
        t_tab = U(tdc, tds)
        nrm_open = {"k": False, "v": False}
        g0 = t_sq_guard if s == 0 else None
        nrm_last = {"k": g0, "v": g0}
        sq_last = {"k": None, "v": None}

        def proj(WX, bank, ec, pgi):
            tdw = P.dma("pool", wbL[pgi].rearrange("p (dc f) -> p dc f", f=128),
                        io[WX][ec], waits=wbL_busy[pgi], sem=wbL_sem[pgi])
            mms = [(wbL[pgi][:, dc * 128:(dc + 1) * 128],
                    znf[:, dc * 1024 + s * 512: dc * 1024 + s * 512 + 512])
                   for dc in range(16)]
            tk = mm_group(C, bank, mms, waits=U(tdw, C.t_Xnf),
                          out_slice=C.pb[bank][:, 0:512])
            return tk

        def evac(bank, dstsl, bias_name, ec, tk, extra=None):
            bc = C.slab[:, sm[bias_name] + ec: sm[bias_name] + ec + 1]
            te = P.op("act", (lambda E, o=dstsl, i=C.pb[bank][:, 0:512], a=bc:
                              E.activation(out=o, in_=i, func=AF.Identity,
                                           bias=a)),
                      waits=U(tk, C.t_consts, extra))
            bank_read(C, bank, te)
            return te

        def sqsum(src_sl, which, tprev, stop):
            bank = 4 if which == "k" else 5
            sq_off = 0 if which == "k" else 256
            sq_sl = sq[:, sq_off:sq_off + 256].bitcast(BF16)
            t1 = P.op("act", (lambda E, o=sq_sl, i=src_sl:
                              E.activation(out=o, in_=i, func=AF.Square)),
                      waits=U(tprev, nrm_last[which]))
            st = not nrm_open[which]
            nrm_open[which] = True
            w0 = U(t1, C.bank_free[bank]) if st else U(t1)
            t2 = P.op("pe", (lambda E, o=C.pb[bank][0:1, 0:512], l=onesbf,
                             r=sq_sl, s_=st, sp=stop:
                             E.matmul(out=o, lhsT=l, rhs=r, start=s_, stop=sp)),
                      waits=w0, inc=1)
            nrm_last[which] = t2
            if stop:
                C.bank_data[bank] = t2
                C.bank_free[bank] = None
            return t2

        P_toks = {}

        def do_chunk_post(ksl, vsl, tks, tvs, ecs):
            tqk = sqsum(ksl, "k", tks, stop=(ecs == 15))
            tqv = sqsum(vsl, "v", tvs, stop=(ecs == 15))
            psl = Pslab[:, ecs * 512:(ecs + 1) * 512]
            tp = P.op("dve", (lambda E, o=psl, i=ksl, j=vsl:
                              E.tensor_mul(out=o, in0=i, in1=j)),
                      waits=U(tks, tvs, t_slab2))
            P_toks[ecs] = U(tp, tqk, tqv)
            return U(tp, tqk, tqv)

        pend_post = []

        def flush_post():
            toks = [fn() for fn in pend_post]
            pend_post.clear()
            return U(*toks) if toks else None

        for pr in range(4):
            ktoks, vtoks = {}, {}
            for half, ecs in ((0, pr), (1, pr + 4)):
                kb = (2, 6)[ecs % 2]
                vb = (3, 7)[ecs % 2]
                pgi = gi % 4; gi += 1
                tk = proj("WinKX", kb, ecs, pgi)
                ksl = sc[:, half * 512: half * 512 + 512]
                ktoks[half] = evac(kb, ksl, "binK", ecs, tk,
                                   extra=U(t_sc_free if s == 0 and pr == 0
                                           else None, C.prev_scrd,
                                           C.prev_rot))
                wbL_busy[pgi] = tk
                pgi = gi % 4; gi += 1
                tv = proj("WinVX", vb, ecs, pgi)
                vsl = sc2[:, half * 512: half * 512 + 512]
                vtoks[half] = evac(vb, vsl, "binV", ecs, tv,
                                   extra=U(C.prev_scrd, C.prev_rot))
                wbL_busy[pgi] = tv
            t_post = flush_post()
            if t_post:
                C.prev_scrd = t_post
            cosc = rope_ap(ptab, 0)[:, pr, :]
            sinc = rope_ap(ptab, 1)[:, pr, :]
            rots = []
            for scr, toks in ((sc, ktoks), (sc2, vtoks)):
                x1 = scr[:, 0:512]; x2 = scr[:, 512:1024]
                t1s = scr[:, 1024:1536]; t2s = scr[:, 1536:2048]
                ta = P.op("dve", (lambda E, o=t1s, i=x1, cc=cosc:
                                  E.tensor_mul(out=o, in0=i, in1=cc)),
                          waits=U(toks[0], toks[1], t_tab, t_post))
                tb = P.op("dve", (lambda E, o=t2s, i=x2, ss=sinc:
                                  E.tensor_mul(out=o, in0=i, in1=ss)),
                          waits=t_post)
                tcx = P.op("dve", (lambda E, o=t1s, i=t2s:
                                   E.tensor_sub(out=o, in0=o, in1=i)),
                           waits=U(ta, tb))
                td_ = P.op("dve", (lambda E, o=t2s, i=x2, cc=cosc:
                                   E.tensor_mul(out=o, in0=i, in1=cc)),
                           waits=U(tb, tcx))
                te_ = P.op("dve", (lambda E, o=x2, i=x1, ss=sinc:
                                   E.tensor_mul(out=o, in0=i, in1=ss)),
                           waits=td_)
                tf_ = P.op("dve", (lambda E, o=t2s, i=x2:
                                   E.tensor_add(out=o, in0=o, in1=i)),
                           waits=U(td_, te_))
                rots.append((t1s, tcx, t2s, tf_))
            (k1, tk1, k2, tk2) = rots[0]
            (v1, tv1, v2, tv2) = rots[1]
            C.prev_rot = U(tk1, tk2, tv1, tv2)
            pend_post.append(lambda a=k1, b=v1, c=U(tk1, tv1), e=pr:
                             do_chunk_post(a, b, c, c, e))
            pend_post.append(lambda a=k2, b=v2, c=U(tk2, tv2), e=pr + 4:
                             do_chunk_post(a, b, c, c, e))
        for ecs in range(8, 16):
            kb = (2, 6)[ecs % 2]
            vb = (3, 7)[ecs % 2]
            half = ecs % 2
            pgi = gi % 4; gi += 1
            tk = proj("WinKX", kb, ecs, pgi)
            ksl = sc[:, half * 512: half * 512 + 512]
            tke = evac(kb, ksl, "binK", ecs, tk,
                       extra=U(C.prev_scrd, C.prev_rot))
            wbL_busy[pgi] = tk
            pgi = gi % 4; gi += 1
            tv = proj("WinVX", vb, ecs, pgi)
            vsl = sc2[:, half * 512: half * 512 + 512]
            tve = evac(vb, vsl, "binV", ecs, tv,
                       extra=U(C.prev_scrd, C.prev_rot))
            wbL_busy[pgi] = tv
            t_post = flush_post()
            if t_post:
                C.prev_scrd = t_post
            pend_post.append(lambda a=ksl, b=vsl, c=U(tke, tve), e=ecs:
                             do_chunk_post(a, b, c, c, e))
        C.prev_scrd = flush_post()
        nk_sb = sq[0:1, 1024:1536]
        t1 = P.op("dve", (lambda E, o=nk_sb, i=C.pb[4][0:1, 0:512]:
                          E.tensor_copy(out=o, in_=i)), waits=C.bank_data[4])
        C.bank_free[4] = t1
        prod = sq[0:1, 1536:2048]
        t2 = P.op("dve", (lambda E, o=prod, i=nk_sb, j=C.pb[5][0:1, 0:512]:
                          E.tensor_mul(out=o, in0=i, in1=j)),
                  waits=U(t1, C.bank_data[5]))
        C.bank_free[5] = t2
        t3 = P.op("act", (lambda E, o=prod:
                          E.activation(out=o, in_=o, func=AF.Sqrt)), waits=t2)
        t4 = P.op("dve", (lambda E, o=prod: E.reciprocal(out=o, in_=o)), waits=t3)
        w0 = U(C.bank_free[6], t4, C.t_consts)
        tbc = P.op("pe", (lambda E, o=C.pb[6][:, 0:512], l=ones_rr(C), r=prod:
                          E.matmul(out=o, lhsT=l, rhs=r, start=True, stop=True)),
                   waits=w0)
        C.bank_data[6] = tbc
        Gsb = sq[:, 512:768].bitcast(BF16)
        t5 = P.op("dve", (lambda E, o=Gsb, i=C.pb[6][:, 0:512]:
                          E.tensor_copy(out=o, in_=i)), waits=tbc)
        C.bank_free[6] = t5
        for ecs in range(16):
            psl = Pslab[:, ecs * 512:(ecs + 1) * 512]
            acol = C.slab[:, sm["Apart"] + s * 16 + ecs:
                          sm["Apart"] + s * 16 + ecs + 1]
            tA = P.op("dve", (lambda E, o=psl, i=psl, gg=Gsb, a=acol:
                              E.scalar_tensor_tensor(out=o, in0=i, scalar=1.0,
                                                     in1=gg, op0=ALU.mult,
                                                     op1=ALU.mult, accum_out=a)),
                      waits=U(P_toks[ecs], t5))
            Apart_toks.append(tA)
        tab_busy[ptab] = U(*[P_toks[e_] for e_ in range(8)])
        t_slab2 = U(*Apart_toks)
    t_Ap = U(*Apart_toks)
    wb_busy[0] = wbL_busy[0]
    wb_busy[1] = wbL_busy[1]
    C.t_wbL_free = U(wbL_busy[2], wbL_busy[3])
    Af = f32v(C, sm["Afin"], 16)
    tAs = P.op("dve", (lambda E, o=Af,
                       i=C.slab[:, sm["Apart"]:sm["Apart"] + 16],
                       j=C.slab[:, sm["Apart"] + 16:sm["Apart"] + 32]:
                       E.tensor_add(out=o, in0=i, in1=j)), waits=t_Ap)
    c2i, c2o = C.cc["c2i"], C.cc["c2o"]
    td = P.dma("pool", c2i.ap(), Af, waits=tAs, sem="cci")
    if SIM_NOCC:
        tl = []
        for r in range(8):
            tl.append(P.dma("pool", c2o.ap()[r * 128:(r + 1) * 128, :],
                            c2i.ap(), waits=td, sem="agl"))
        tcc = U(*tl)
    else:
        tcc = P.op("pool", (lambda E: E.collective_compute(
            "AllGather", ALU.bypass, replica_groups=RG,
            ins=[c2i.ap().opt()], outs=[c2o.ap().opt()])), waits=td)

    emit_xup()

    moe(C, t_Ap)

    s_ag = f32v(C, SC_OFF, 2048, p=8)
    tld = P.dma("pool", s_ag, c2o.ap().rearrange("(r p) c -> r (p c)", p=128),
                waits=U(tcc, t_Ap, getattr(C, "t_thr", None)), sem="agl")
    mask1 = C.slab[0:8, sm["mask"] + 1:sm["mask"] + 2]
    w0 = U(tld, C.bank_free[6], C.t_consts)
    tmo = None
    s_ag3 = s_ag.rearrange("r (p c) -> r p c", c=16)
    for cch in range(16):
        tmo = P.op("pe",
                   (lambda E, o=C.pb[6][:, cch:cch + 1],
                    l=s_ag3[:, :, cch], r=mask1:
                    E.matmul(out=o, lhsT=l, rhs=r, start=True, stop=True)),
                   waits=(w0 if cch == 0 else None), inc=(1 if cch == 15 else 0))
    C.bank_data[6] = tmo
    tof = P.op("dve", (lambda E, o=Af, i=C.pb[6][:, 0:16]:
                       E.tensor_copy(out=o, in_=i)), waits=U(tmo, tAs))
    C.bank_free[6] = tof
    tof2 = P.op("dve", (lambda E, o=Af:
                        E.tensor_scalar(out=o, in0=o, scalar1=0.5, scalar2=0.5,
                                        op0=ALU.mult, op1=ALU.add)), waits=tof)
    t_Afin = tof2

    YF = bf16v(C, SLAB2_OFF, 8192)
    bout_row = C.slab[0:1, ROWS_OFF:ROWS_OFF + 1024].bitcast(BF16)
    t_borow = P.dma("act", bout_row, io["bout_r"], waits=t_rows, sem="brow")
    ring_busy = [C.t_moe_xnf_done] * 8
    ru = [0]

    def ring_take(units):
        u = ru[0]
        if u % units:
            u += units - (u % units)
        if u + units > 8:
            u = 0
        ru[0] = u + units
        off = XNF_OFF + u * 1024
        tok = U(*[ring_busy[i] for i in range(u, u + units)])
        def mark(t):
            for i in range(u, u + units):
                ring_busy[i] = t
        return off, tok, mark, "rg%d" % u

    x_final = []
    xdc = [[] for _ in range(8)]
    for s in range(2):
        y_toks = []
        for ecp in range(8):
            ec0 = ecp * 2
            off, wtok, mark, rsem = ring_take(2)
            gr = bf16v(C, off, 4096).rearrange("p (e n) -> p e n", n=2048)
            tdq = P.dma("sp", gr,
                        io["WinQX"][ec0:ec0 + 2].rearrange(
                            "e p d f -> p e (d f)"),
                        waits=wtok, sem=rsem)
            consumed = []
            for ei in range(2):
                ec = ec0 + ei
                bq_b = (2, 3)[ec % 2]
                mms = [(gr[:, ei, dc * 128:(dc + 1) * 128],
                        znf[:, dc * 1024 + s * 512: dc * 1024 + s * 512 + 512])
                       for dc in range(16)]
                tq = mm_group(C, bq_b, mms, waits=U(tdq, C.t_Xnf),
                              out_slice=C.pb[bq_b][:, 0:512])
                ysl = YF[:, ec * 512:(ec + 1) * 512]
                bqc = C.slab[:, sm["binQ"] + ec: sm["binQ"] + ec + 1]
                aac = C.slab[:, sm["Afin"] + ec: sm["Afin"] + ec + 1]
                ty = P.op("dve", (lambda E, o=ysl, i=C.pb[bq_b][:, 0:512],
                                  a=bqc, b=aac:
                                  E.tensor_scalar(out=o, in0=i, scalar1=a,
                                                  scalar2=b, op0=ALU.add,
                                                  op1=ALU.mult)),
                          waits=U(tq, t_Afin, t_Ap, C.t_moe_slab2))
                bank_read(C, bq_b, ty)
                y_toks.append(ty)
                consumed.append(tq)
            mark(U(*consumed))
        t_Y = U(*y_toks)
        for egp in range(2):
            for dcp in range(8):
                off, wtok, mark, rsem = ring_take(1)
                gr = bf16v(C, off, 2048).rearrange("p (d e) -> p d e", e=1024)
                tdw = P.dma("sp", gr,
                            io["WoutR"][dcp * 2:dcp * 2 + 2, :,
                                        egp * 1024:(egp + 1) * 1024]
                            .rearrange("d p e -> p d e"),
                            waits=wtok, sem=rsem)
                last = None
                for dcl in range(2):
                    dc = dcp * 2 + dcl
                    for bidx in range(8):
                        tch, eg = bidx // 2, bidx % 2
                        st = (dc == 0)
                        w = U(C.bank_free[bidx] if st else None,
                              t_Y if st else None, tdw)
                        last = P.op("pe",
                            (lambda E, o=C.pb[bidx][:, 0:512],
                             l=YF[:, dc * 512 + tch * 128:
                                   dc * 512 + (tch + 1) * 128],
                             r=gr[:, dcl, eg * 512:(eg + 1) * 512], s_=st:
                             E.matmul(out=o, lhsT=l, rhs=r, start=s_, stop=False)),
                            waits=w, inc=(1 if bidx == 7 else 0))
                mark(last)
            for bidx in range(8):
                tch, eg = bidx // 2, bidx % 2
                tokmm = P.op("pe", (lambda E, o=C.pb[bidx][:, 0:512],
                                    l=ones_bf_r(C),
                                    r=bout_row[:, egp * 1024 + eg * 512:
                                               egp * 1024 + eg * 512 + 512]:
                                    E.matmul(out=o, lhsT=l, rhs=r, start=False,
                                             stop=True)),
                             waits=U(t_borow, C.t_consts), inc=1)
                C.bank_data[bidx] = tokmm
                xsl = f32v(C, X_OFF + (s * 4 + tch) * 2048 + egp * 1024 + eg * 512,
                           512)
                tev = P.op("dve", (lambda E, o=xsl, i=C.pb[bidx][:, 0:512]:
                                   E.tensor_add(out=o, in0=o, in1=i)),
                           waits=U(tokmm, C.t_moe_x[s * 4 + tch],
                                   C.t_xup[s * 4 + tch]))
                C.bank_free[bidx] = tev
                x_final.append(tev)
                xdc[s * 4 + tch].append(tev)
    C.t_xdone = U(*x_final)
    C.t_xdone_c = [U(*xdc[c]) for c in range(8)]


def moe(C, t_sq_free):
    P, io, sm = C.P, C.io, C.sm
    xnf = bf16v(C, XNF_OFF, 16384)
    sq = f32v(C, SP2_OFF, 2048)
    hf = f32v(C, HF_OFF, 1024)
    whb = bf16v(C, WH_OFF, 1024)
    wb, wb_busy = C.wb, C.wb_busy
    ident = f32v(C, sm["ident"], 128)

    trt = P.dma("sp", wb[0][:, 0:128].rearrange("p (dc e) -> p dc e", e=8),
                io["RtX"], waits=wb_busy[0], sem="rt")
    rtv = wb[0][:, 0:128].rearrange("p (dc e) -> p dc e", e=8)
    tdd = P.dma("sp", wb[1].rearrange("p (dc e) -> p dc e", e=128),
                io["DnX"], waits=wb_busy[1], sem="dn")
    dnb = wb[1].rearrange("p (dc e) -> p dc e", e=128)

    lg_toks = []
    for s in range(2):
        w0 = U(C.bank_free[7], trt, C.t_xnf)
        tok = None
        for i in range(16):
            tok = P.op("pe", (lambda E, o=C.pb[7][0:8, 0:512],
                              l=rtv[:, i, :],
                              r=xnf[:, i * 1024 + s * 512: i * 1024 + s * 512 + 512],
                              st=(i == 0), sp=(i == 15):
                              E.matmul(out=o, lhsT=l, rhs=r, start=st, stop=sp)),
                       waits=(w0 if i == 0 else None), inc=(1 if i == 15 else 0))
        C.bank_data[7] = tok
        lsb = sq[0:8, s * 512: s * 512 + 512]
        te = P.op("dve", (lambda E, o=lsb, i=C.pb[7][0:8, 0:512]:
                          E.tensor_copy(out=o, in_=i)), waits=U(tok, t_sq_free))
        C.bank_free[7] = te
        lg_toks.append(te)
    wb_busy[0] = U(wb_busy[0], *lg_toks)
    t_lg = U(*lg_toks)

    ltok_toks = []
    for tcch in range(8):
        b = tcch % 2
        w0 = U(C.bank_free[b], t_lg, C.t_consts)
        tt_ = P.op("pe", (lambda E, o=C.pb[b][:, 0:8],
                          i=sq[0:8, tcch * 128:(tcch + 1) * 128],
                          idn=ident[0:8, 0:8]:
                          E.transpose(out=o, in_=i, identity=idn)), waits=w0)
        C.bank_data[b] = tt_
        lcol = C.slab[:, sm["ltok"] + tcch * 8: sm["ltok"] + tcch * 8 + 8]
        te = P.op("dve", (lambda E, o=lcol, i=C.pb[b][:, 0:8]:
                          E.tensor_copy(out=o, in_=i)), waits=tt_)
        C.bank_free[b] = te
        ltok_toks.append(te)

    wf_toks = []
    for tcch in range(8):
        lcol = C.slab[:, sm["ltok"] + tcch * 8: sm["ltok"] + tcch * 8 + 8]
        wcol = C.slab[:, sm["wful"] + tcch * 8: sm["wful"] + tcch * 8 + 8]
        e8 = C.slab[:, sm["e8"]:sm["e8"] + 8]
        m1 = sm1(C, "tiny", 8)
        m2 = sm1(C, "tiny", 9)
        t0 = ltok_toks[tcch]
        t1 = P.op("dve", (lambda E, o=m1, i=lcol:
                          E.reduce_max(out=o, in_=i, axis=AX.X)), waits=t0)
        t2 = P.op("dve", (lambda E, o=wcol, i=lcol, a=m1:
                          E.tensor_scalar(out=o, in0=i, scalar1=a, scalar2=None,
                                          op0=ALU.is_ge)), waits=t1)
        t3 = P.op("dve", (lambda E, o=wcol, i=lcol:
                          E.scalar_tensor_tensor(out=o, in0=o, scalar=-BIG,
                                                 in1=i, op0=ALU.mult,
                                                 op1=ALU.add)), waits=t2)
        t4 = P.op("dve", (lambda E, o=m2, i=wcol:
                          E.reduce_max(out=o, in_=i, axis=AX.X)), waits=t3)
        t5 = P.op("dve", (lambda E, o=wcol, i=lcol, a=m2:
                          E.tensor_scalar(out=o, in0=i, scalar1=a, scalar2=None,
                                          op0=ALU.is_ge)), waits=t4)
        negm1 = sm1(C, "tiny", 10)
        t6 = P.op("dve", (lambda E, o=negm1, i=m1:
                          E.tensor_scalar(out=o, in0=i, scalar1=-1.0,
                                          scalar2=None, op0=ALU.mult)), waits=t1)
        t7 = P.op("act", (lambda E, o=e8, i=lcol, a=negm1:
                          E.activation(out=o, in_=i, func=AF.Exp, bias=a)),
                  waits=U(t6, t5))
        t8 = P.op("dve", (lambda E, o=wcol, i=e8:
                          E.tensor_mul(out=o, in0=o, in1=i)), waits=t7)
        den = sm1(C, "tiny", 8)
        t9 = P.op("dve", (lambda E, o=den, i=wcol:
                          E.reduce_sum(out=o, in_=i, axis=AX.X)), waits=t8)
        rd = sm1(C, "tiny", 9)
        t10 = P.op("dve", (lambda E, o=rd, i=den: E.reciprocal(out=o, in_=i)),
                   waits=t9)
        t11 = P.op("dve", (lambda E, o=wcol, a=rd:
                           E.tensor_scalar(out=o, in0=o, scalar1=a, scalar2=None,
                                           op0=ALU.mult)), waits=t10)
        wf_toks.append(t11)

    wrow = sq[0:8, 1024:1536].bitcast(BF16)
    wT_toks = []
    for tcch in range(8):
        b = tcch % 2
        wcol = C.slab[:, sm["wful"] + tcch * 8: sm["wful"] + tcch * 8 + 8]
        w0 = U(C.bank_free[b], wf_toks[tcch], C.t_consts, t_sq_free)
        tt_ = P.op("pe", (lambda E, o=C.pb[b][0:8, 0:128], i=wcol, idn=ident:
                          E.transpose(out=o, in_=i, identity=idn)), waits=w0)
        C.bank_data[b] = tt_
        te = P.op("dve", (lambda E, o=wrow[:, tcch * 128:(tcch + 1) * 128],
                          i=C.pb[b][0:8, 0:128]:
                          E.tensor_copy(out=o, in_=i)), waits=tt_)
        C.bank_free[b] = te
        wT_toks.append(te)
    t_wT = U(*wT_toks)

    h_toks = []
    for s in range(2):
        mms = [(dnb[:, dc, :],
                xnf[:, dc * 1024 + s * 512: dc * 1024 + s * 512 + 512])
               for dc in range(16)]
        tm = mm_group(C, 7, mms, waits=U(tdd, C.t_xnf),
                      out_slice=C.pb[7][:, 0:512])
        te = P.op("dve", (lambda E, o=hf[:, s * 512:(s + 1) * 512],
                          i=C.pb[7][:, 0:512]:
                          E.tensor_copy(out=o, in_=i)),
                  waits=U(tm, C.t_kv_l1_free, C.t_wbL_free))
        bank_read(C, 7, te)
        h_toks.append(te)
    t_h = U(*h_toks)
    wb_busy[1] = t_h

    esl = C.slab[0:8, sm["esel"]:sm["esel"] + 64].bitcast(BF16)
    wrep_toks = []
    for s in range(2):
        w0 = U(C.bank_free[6], t_wT, C.t_consts)
        tm = P.op("pe", (lambda E, o=C.pb[6][:, 0:512], l=esl,
                         r=wrow[:, s * 512:(s + 1) * 512]:
                         E.matmul(out=o, lhsT=l, rhs=r, start=True, stop=True)),
                  waits=w0)
        C.bank_data[6] = tm
        twh = P.op("dve", (lambda E, o=whb[:, s * 512:(s + 1) * 512],
                           i=hf[:, s * 512:(s + 1) * 512], j=C.pb[6][:, 0:512]:
                           E.tensor_mul(out=o, in0=i, in1=j)),
                   waits=U(tm, t_h))
        C.bank_free[6] = twh
        wrep_toks.append(twh)
    t_wh = U(*wrep_toks)

    tdu = P.dma("sp", wb[0][:, 0:2048], io["UpA"],
                waits=U(wb_busy[0], t_lg), sem="up")
    upv = wb[0][:, 0:2048]
    C.t_moe_x = [None] * 8
    up_toks = []
    for eg in range(4):
        for tch in range(8):
            b = tch
            w0 = U(C.bank_free[b], t_wh, tdu)
            tm = P.op("pe", (lambda E, o=C.pb[b][:, 0:512],
                             l=whb[:, tch * 128:(tch + 1) * 128],
                             r=upv[:, eg * 512:(eg + 1) * 512]:
                             E.matmul(out=o, lhsT=l, rhs=r, start=True,
                                      stop=True)), waits=w0)
            C.bank_data[b] = tm
            xsl = f32v(C, X_OFF + tch * 2048 + eg * 512, 512)
            tev = P.op("dve", (lambda E, o=xsl, i=C.pb[b][:, 0:512]:
                               E.tensor_add(out=o, in0=o, in1=i)),
                       waits=U(tm, C.t_xup[tch]))
            C.bank_free[b] = tev
            C.t_moe_x[tch] = U(C.t_moe_x[tch], tev)
            up_toks.append(tev)
    wb_busy[0] = U(*up_toks)
    C.t_moe_slab2 = None
    C.t_moe_xnf_done = U(t_lg, t_h)


def kernel(**inputs):
    g = stage_host(inputs)
    cores = [stage_core(inputs, cid, g) for cid in range(NCORE)]
    if FUSED:
        nc = build_nc("ALL")
        in_maps = []
        for cid in range(NCORE):
            m = {k: (cores[cid][k] if k in cores[cid] else g[k])
                 for k in KEYS_ALL}
            in_maps.append(m)
        res = run_bass_kernel_spmd(nc, in_maps, core_ids=list(range(NCORE)))
        outs = [res.results[c]["out"] for c in range(NCORE)]
    else:
        nc0 = build_nc("L0")
        in_maps = []
        for cid in range(NCORE):
            m = {k: (cores[cid][k] if k in cores[cid] else g[k])
                 for k in KEYS_L0}
            in_maps.append(m)
        r0 = run_bass_kernel_spmd(nc0, in_maps, core_ids=list(range(NCORE)))
        nc1 = build_nc("L1")
        in_maps1 = []
        for cid in range(NCORE):
            m = {k: (cores[cid][k] if k in cores[cid] else g[k])
                 for k in KEYS_L1}
            m["xin"] = np.ascontiguousarray(np.asarray(r0.results[cid]["out"]))
            in_maps1.append(m)
        r1 = run_bass_kernel_spmd(nc1, in_maps1, core_ids=list(range(NCORE)))
        outs = [r1.results[c]["out"] for c in range(NCORE)]
    full = np.concatenate([np.asarray(o) for o in outs], axis=0)
    return full.reshape(1, 8192, 2048).astype(np.float32)


# version 19
# speedup vs baseline: 1.1671x; 1.0296x over previous
import numpy as np
import ml_dtypes
from contextlib import ExitStack

import concourse.bass as bass
import concourse.mybir as mybir
from concourse.bass_utils import run_bass_kernel_spmd

F32 = mybir.dt.float32
BF16 = mybir.dt.bfloat16
AF = mybir.ActivationFunctionType
ALU = mybir.AluOpType
AX = mybir.AxisListType
BF = ml_dtypes.bfloat16

NCORE = 8
TOK = 1024
TC = 8
D = 2048
DFF = 8192
SCALE = float(128.0 ** -0.5)
INVSCALE = float(128.0 ** 0.5)
EPS_LN = 1e-5
EPS_BIT = 1e-8
BIG = 30000.0
RG = [[0, 1, 2, 3, 4, 5, 6, 7]]
FUSED = True
SELF_WAIT = False
SIM_NOCC = False
DBG_STOP = None
RACE_DETECT = False

X_OFF = 0
XNF_OFF = 16384
KV_OFF = 24576
SLAB2_OFF = 40960
SC_OFF = 45056
WF32_OFF = 47104
WBF_OFF = 48128
ROWS_OFF = 50176
SMALL_OFF = 52224
SLAB_COLS = 53200

ZNF_OFF = KV_OFF
TAB_OFF = KV_OFF + 8192
HF_OFF = KV_OFF + 10240
WH_OFF = KV_OFF + 11264
SC2_OFF = KV_OFF + 12288
SP2_OFF = KV_OFF + 14336

ENGINES = ("pe", "dve", "act", "pool", "sp")
BLOCK_ATTR = {"pe": "tensor", "dve": "vector", "act": "scalar",
              "pool": "gpsimd", "sp": "sync"}
SEM_NAMES = ("pe", "dve", "act", "pool", "sp",
             "pool_dma", "sp_dma", "act_dma",
             "wfp0", "wfp1", "wfp2", "wfp3", "wfp4",
             "wb0", "wb1", "tab", "xio", "oio",
             "cst", "cci", "agl", "b2s", "brow", "rt", "dn", "up",
             "rg0", "rg1", "rg2", "rg3", "rg4", "rg5", "rg6", "rg7",
             "wk0", "wk1", "wk2", "wk3", "wk4", "wk5", "wi2", "wi3",
             "wq0", "wq1", "wq2", "wq3", "wq4", "wq5",
             "xo0", "xo1", "xo2", "xo3", "xo4", "xo5", "xo6", "xo7")


def U(*toks):
    out = {}
    for t in toks:
        if not t:
            continue
        for s, v in t.items():
            if out.get(s, -1) < v:
                out[s] = v
    return out


class Prog:
    def __init__(self, nc):
        self.nc = nc
        self.ops = {e: [] for e in ENGINES}
        self.sems = {}
        self.cnt = {s: 0 for s in SEM_NAMES}
        self.waited = {e: {} for e in ENGINES}

    def op(self, eng, fn, waits=None, inc=1, dma=False, sem=None):
        semname = (sem or eng + "_dma") if dma else eng
        need = []
        wt = self.waited[eng]
        if SELF_WAIT and not dma and eng in ("dve", "act", "pool"):
            prev = self.cnt[eng]
            if prev and wt.get(eng, 0) < prev:
                need.append((eng, prev))
                wt[eng] = prev
        if waits:
            for s, v in waits.items():
                if s.endswith("_dma"):
                    v = self.cnt[s]
                if wt.get(s, 0) >= v:
                    continue
                need.append((s, v))
                wt[s] = v
        step = (16 if dma else inc) if inc else 0
        if step:
            self.cnt[semname] += step
        target = self.cnt[semname]

        def emit(engine, _need=tuple(need), _fn=fn, _step=step, _sem=semname):
            for s, v in _need:
                engine.wait_ge(self.sems[s], v)
            instr = _fn(engine)
            if _step:
                instr.then_inc(self.sems[_sem], _step)
        self.ops[eng].append(emit)
        return {semname: target} if step else None

    def dma(self, eng, out, in_, waits=None, sem=None):
        return self.op(eng, lambda E, o=out, i=in_: E.dma_start(out=o, in_=i),
                       waits=waits, dma=True, sem=sem)

    def wait_only(self, eng, waits):
        need = []
        wt = self.waited[eng]
        for s, v in waits.items():
            if wt.get(s, 0) >= v:
                continue
            need.append((s, v))
            wt[s] = v

        def emit(engine, _need=tuple(need)):
            for s, v in _need:
                engine.wait_ge(self.sems[s], v)
        self.ops[eng].append(emit)

    def finalize(self, block):
        for e in ENGINES:
            fns = self.ops[e]
            if not fns:
                continue

            def body(engine, _fns=tuple(fns)):
                for f in _fns:
                    f(engine)
            getattr(block, BLOCK_ATTR[e])(body)


def _xform(W):
    return np.ascontiguousarray(W.reshape(16, 128, 16, 128).transpose(0, 3, 2, 1))


def _xform_de(Wde):
    ecn = Wde.shape[1] // 128
    return np.ascontiguousarray(
        Wde.reshape(16, 128, ecn, 128).transpose(2, 1, 0, 3))


def _rhsform(Wde):
    return np.ascontiguousarray(Wde.reshape(-1, 128, Wde.shape[1]))


def _pvec(v):
    return np.ascontiguousarray(np.asarray(v).reshape(-1, 128).T)


def stage_host(inputs):
    g = {}
    perm = np.concatenate([np.arange(0, 1024, 2), np.arange(1, 1024, 2),
                           np.arange(1024, 2048)])
    g["WqX"] = _xform(np.asarray(inputs["L0_attn_Wq"])).astype(BF)
    g["WkX"] = _xform(np.asarray(inputs["L0_attn_Wk"])).astype(BF)
    g["WvX"] = _xform(np.asarray(inputs["L0_attn_Wv"])).astype(BF)
    g["WoR"] = _rhsform(np.ascontiguousarray(
        np.asarray(inputs["L0_attn_Wo"]).T)).astype(BF)
    g["bq"] = _pvec(inputs["L0_attn_bq"])
    g["bk"] = _pvec(inputs["L0_attn_bk"])
    g["bv"] = _pvec(inputs["L0_attn_bv"])
    g["bo_r"] = np.ascontiguousarray(
        np.asarray(inputs["L0_attn_bo"])[None, :]).astype(BF)
    w1 = np.asarray(inputs["L0_ffn_w1"])
    w2 = np.asarray(inputs["L0_ffn_w2"])
    g["W1X"] = np.ascontiguousarray(
        w1.reshape(64, 128, 16, 128).transpose(0, 3, 2, 1))
    g["W2R"] = np.ascontiguousarray(w2.T.reshape(64, 128, 2048))
    g["b1t"] = _pvec(inputs["L0_ffn_b1"])
    g["b2_r"] = np.ascontiguousarray(
        np.asarray(inputs["L0_ffn_b2"])[None, :]).astype(BF)
    Win = np.asarray(inputs["L1_attn_W_in"])
    bin_ = np.asarray(inputs["L1_attn_b_in"])
    g["WinQX"] = _xform_de(np.ascontiguousarray(Win[:, 0:2048][:, perm])).astype(BF)
    g["WinKX"] = _xform_de(np.ascontiguousarray(Win[:, 2048:4096][:, perm])).astype(BF)
    g["WinVX"] = _xform_de(np.ascontiguousarray(Win[:, 4096:6144][:, perm])).astype(BF)
    g["binQ"] = _pvec(np.ascontiguousarray(bin_[0:2048][perm]))
    g["binK"] = _pvec(np.ascontiguousarray(bin_[2048:4096][perm]))
    g["binV"] = _pvec(np.ascontiguousarray(bin_[4096:6144][perm]))
    g["WoutR"] = _rhsform(np.ascontiguousarray(
        np.asarray(inputs["L1_attn_W_out"])[perm, :])).astype(BF)
    g["bout_r"] = np.ascontiguousarray(
        np.asarray(inputs["L1_attn_b_out"])[None, :]).astype(BF)
    g["gam"] = _pvec(inputs["L1_attn_gamma"])
    g["bet"] = _pvec(inputs["L1_attn_beta"])
    rw = np.asarray(inputs["L1_ffn_router_w"])
    g["RtX"] = np.ascontiguousarray(
        rw.T.reshape(16, 128, 8).transpose(1, 0, 2)).astype(BF)
    dn = np.asarray(inputs["L1_ffn_experts_down"])
    dall = dn.transpose(1, 0, 2).reshape(2048, 128)
    g["DnX"] = np.ascontiguousarray(
        dall.reshape(16, 128, 128).transpose(1, 0, 2)).astype(BF)
    g["UpA"] = np.ascontiguousarray(
        np.asarray(inputs["L1_ffn_experts_up"]).reshape(128, 2048)).astype(BF)
    g["ident"] = np.eye(128, dtype=np.float32)
    esel = np.zeros((8, 128), dtype=np.float32)
    for ei in range(8):
        esel[ei, ei * 16:(ei + 1) * 16] = 1.0
    g["esel"] = esel.astype(BF)
    g["mask1"] = np.ones((8, 1), dtype=np.float32)
    return g


def stage_core(inputs, cid, g=None):
    c = {}
    x = np.asarray(inputs["L0_attn_x"]).reshape(8192, 2048)
    c["xin"] = np.ascontiguousarray(x[cid * TOK:(cid + 1) * TOK])
    if g is None:
        w1 = np.asarray(inputs["L0_ffn_w1"])
        w2 = np.asarray(inputs["L0_ffn_w2"])
        W1X = np.ascontiguousarray(
            w1.reshape(64, 128, 16, 128).transpose(0, 3, 2, 1))
        W2R = np.ascontiguousarray(w2.T.reshape(64, 128, 2048))
    else:
        W1X, W2R = g["W1X"], g["W2R"]
    gn = 16 * 131072
    c["W1M"] = np.ascontiguousarray(
        W1X.reshape(-1)[cid * gn:(cid + 1) * gn].reshape(16, 128, 1024))
    c["W2M"] = np.ascontiguousarray(
        W2R.reshape(-1)[cid * gn:(cid + 1) * gn].reshape(16, 128, 1024))
    inv = 1.0 / (10000.0 ** (np.arange(0, 1024, 2, dtype=np.float64) / 1024.0))
    pos = np.arange(cid * TOK, (cid + 1) * TOK, dtype=np.float64)
    ang = np.outer(inv, pos)
    c["ropec"] = np.ascontiguousarray(
        np.cos(ang).astype(np.float32).astype(BF)
        .reshape(4, 128, 1024).transpose(1, 0, 2))
    c["ropes"] = np.ascontiguousarray(
        np.sin(ang).astype(np.float32).astype(BF)
        .reshape(4, 128, 1024).transpose(1, 0, 2))
    mp = np.zeros((8, 1), dtype=np.float32)
    mp[:cid] = 1.0
    c["maskp"] = mp
    return c


KEYS_L0 = ["xin", "WqX", "WkX", "WvX", "WoR", "bq", "bk", "bv", "bo_r",
           "W1X", "W2R", "W1M", "W2M", "b1t", "b2_r", "maskp", "ident"]
KEYS_L1 = ["xin", "WinQX", "WinKX", "WinVX", "WoutR", "binQ", "binK", "binV",
           "bout_r", "gam", "bet", "RtX", "DnX", "UpA", "ropec", "ropes",
           "mask1", "ident", "esel"]
KEYS_ALL = sorted(set(KEYS_L0) | set(KEYS_L1))


class Ctx:
    pass


def small_alloc():
    m = {}
    off = SMALL_OFF

    def take(name, n):
        nonlocal off
        m[name] = off
        off += n
    take("stat", 16)
    take("bq", 16); take("bk", 16); take("bv", 16)
    take("binQ", 16); take("binK", 16); take("binV", 16)
    take("b1t", 64)
    take("gam", 16); take("bet", 16)
    take("carry", 18)
    take("offs", 18)
    take("Apart", 32)
    take("Afin", 16)
    take("thr", 4)
    take("ltok", 64)
    take("wful", 64)
    take("tiny", 16)
    take("e8", 8)
    take("ones_r", 128)
    take("invs_r", 128)
    take("ident", 128)
    take("esel", 64)
    take("mask", 4)
    take("onesb", 2)
    take("mean", 4)
    take("half", 2)
    take("lnb", 56)
    assert off <= SLAB_COLS, off
    return m


def f32v(C, off, n, p0=0, p=128):
    return C.slab[p0:p0 + p, off:off + n]


def bf16v(C, off, nbf):
    return C.slab[:, off:off + nbf // 2].bitcast(BF16)


def sm1(C, name, idx=0):
    o = C.sm[name] + idx
    return C.slab[:, o:o + 1]


def bank_read(C, b, tok):
    C.bank_free[b] = U(C.bank_free[b], tok)


def mm_group(C, b, mms, waits=None, out_slice=None):
    P = C.P
    n = len(mms)
    w0 = U(C.bank_free[b], waits)
    tok = None
    for i, (lh, rh) in enumerate(mms):
        tok = P.op("pe",
                   (lambda E, o=out_slice, l=lh, r=rh, st=(i == 0),
                    sp=(i == n - 1):
                    E.matmul(out=o, lhsT=l, rhs=r, start=st, stop=sp)),
                   waits=(w0 if i == 0 else None),
                   inc=(1 if i == n - 1 else 0))
    C.bank_data[b] = tok
    C.bank_free[b] = None
    return tok


def ones_rr(C):
    return f32v(C, C.sm["ones_r"], 128, p=1)


def invs_bf(C):
    return C.slab[0:1, C.sm["invs_r"]:C.sm["invs_r"] + 64].bitcast(BF16)


def ones_bf_r(C):
    return C.slab[0:1, C.sm["invs_r"] + 64:C.sm["invs_r"] + 128].bitcast(BF16)


def build_nc(part):
    nc = bass.Bass("TRN2", target_bir_lowering=False, debug=False,
                   detect_race_conditions=RACE_DETECT)
    C = Ctx()
    C.part = part
    io = {}

    def din(name, shape, dt=F32):
        io[name] = nc.dram_tensor(name, list(shape), dt, kind="ExternalInput").ap()

    do_l0 = part in ("L0", "ALL")
    do_l1 = part in ("L1", "ALL")
    din("xin", [TOK, D])
    din("ident", [128, 128])
    if do_l0:
        din("WqX", [16, 128, 16, 128], BF16); din("WkX", [16, 128, 16, 128], BF16)
        din("WvX", [16, 128, 16, 128], BF16); din("WoR", [16, 128, 2048], BF16)
        din("bq", [128, 16]); din("bk", [128, 16]); din("bv", [128, 16])
        din("bo_r", [1, 2048], BF16)
        din("W1X", [64, 128, 16, 128]); din("W2R", [64, 128, 2048])
        din("W1M", [16, 128, 1024]); din("W2M", [16, 128, 1024])
        din("b1t", [128, 64]); din("b2_r", [1, 2048], BF16)
        din("maskp", [8, 1])
    if do_l1:
        din("WinQX", [16, 128, 16, 128], BF16); din("WinKX", [16, 128, 16, 128], BF16)
        din("WinVX", [16, 128, 16, 128], BF16); din("WoutR", [16, 128, 2048], BF16)
        din("binQ", [128, 16]); din("binK", [128, 16]); din("binV", [128, 16])
        din("bout_r", [1, 2048], BF16)
        din("gam", [128, 16]); din("bet", [128, 16])
        din("RtX", [128, 16, 8], BF16); din("DnX", [128, 16, 128], BF16)
        din("UpA", [128, 2048], BF16)
        din("ropec", [128, 4, 1024], BF16); din("ropes", [128, 4, 1024], BF16)
        din("mask1", [8, 1]); din("esel", [8, 128], BF16)
    io["out"] = nc.dram_tensor("out", [TOK, D], F32, kind="ExternalOutput").ap()

    cc = {}
    if do_l0:
        cc["c1i"] = nc.dram_tensor("c1i", [128, 18], F32, kind="Internal")
        cc["c1o"] = nc.dram_tensor("c1o", [1024, 18], F32, kind="Internal")
    if do_l1:
        cc["c2i"] = nc.dram_tensor("c2i", [128, 16], F32, kind="Internal")
        cc["c2o"] = nc.dram_tensor("c2o", [1024, 16], F32, kind="Internal")

    with ExitStack() as ctx:
        e = ctx.enter_context
        slab = e(nc.sbuf_tensor("slab", [128, SLAB_COLS], F32))
        pb = [e(nc.psum_tensor("pb%d" % i, [128, 512], F32)) for i in range(8)]
        P = Prog(nc)
        for s in SEM_NAMES:
            P.sems[s] = e(nc.semaphore("sem_" + s))
        block = e(nc.Block())
        C.nc, C.P, C.slab, C.pb, C.io, C.cc = nc, P, slab, pb, io, cc
        C.sm = small_alloc()
        C.bank_data = [None] * 8
        C.bank_free = [None] * 8
        C.wb = [bf16v(C, WBF_OFF, 2048), bf16v(C, WBF_OFF + 1024, 2048)]
        C.wb_busy = [None, None]
        build_body(C)
        P.finalize(block)
    nc.finalize()
    return nc


def build_body(C):
    P, io, sm = C.P, C.io, C.sm
    do_l0 = C.part in ("L0", "ALL")
    do_l1 = C.part in ("L1", "ALL")

    consts = [P.dma("act", f32v(C, sm["ident"], 128), io["ident"], sem="cst")]
    xin3 = io["xin"].rearrange("(tc p) e -> tc p e", p=128)
    t_xc = []
    for tc8 in range(TC):
        t_xc.append(P.dma("sp" if tc8 % 2 == 0 else "act",
                          f32v(C, X_OFF + tc8 * 2048, 2048),
                          xin3[tc8], sem="xo%d" % tc8))
    t_x = U(*t_xc)
    C.t_xc = t_xc
    if do_l0:
        for nm in ("bq", "bk", "bv"):
            consts.append(P.dma("act", f32v(C, sm[nm], 16), io[nm], sem="cst"))
        consts.append(P.dma("act", f32v(C, sm["b1t"], 64), io["b1t"], sem="cst"))
        consts.append(P.dma("act", f32v(C, sm["mask"], 1, p=8), io["maskp"], sem="cst"))
    if do_l1:
        for nm in ("binQ", "binK", "binV", "gam", "bet"):
            consts.append(P.dma("act", f32v(C, sm[nm], 16), io[nm], sem="cst"))
        consts.append(P.dma("act", C.slab[0:8, sm["esel"]:sm["esel"] + 64].bitcast(BF16), io["esel"], sem="cst"))
        consts.append(P.dma("act", f32v(C, sm["mask"] + 1, 1, p=8), io["mask1"], sem="cst"))
    consts.append(P.op("dve", lambda E: E.memset(f32v(C, sm["ones_r"], 128, p=1), 1.0)))
    consts.append(P.op("dve", lambda E: E.memset(invs_bf(C), INVSCALE)))
    consts.append(P.op("dve", lambda E: E.memset(ones_bf_r(C), 1.0)))
    consts.append(P.op("dve", lambda E: E.memset(
        C.slab[:, sm["onesb"]:sm["onesb"] + 1].bitcast(BF16), 1.0)))
    consts.append(P.op("dve", lambda E: E.memset(
        C.slab[:, sm["onesb"] + 1:sm["onesb"] + 2], 1.0)))
    consts.append(P.op("dve", lambda E: E.memset(f32v(C, sm["carry"], 16), 0.0)))
    consts.append(P.op("dve", lambda E: E.memset(sm1(C, "half", 0), -0.5)))
    consts.append(P.op("dve", lambda E: E.memset(sm1(C, "half", 1), 0.5)))
    C.t_consts = U(*consts)
    C.t_x = t_x

    if do_l0:
        layer0(C)
    if do_l1:
        layer1(C)

    out3 = io["out"].rearrange("(tc p) e -> tc p e", p=128)
    xdc = getattr(C, "t_xdone_c", None)
    touts = []
    for tc8 in range(TC):
        touts.append(P.dma("sp", out3[tc8], f32v(C, X_OFF + tc8 * 2048, 2048),
                           waits=(xdc[tc8] if xdc else C.t_xdone),
                           sem="xo%d" % tc8))
    P.wait_only("sp", U(*touts))


def ln_stats(C, x_chunk, slot_negmu, slot_rstd, sq_scr, waits):
    P = C.P
    s1 = sm1(C, "tiny", 0)
    s2 = sm1(C, "tiny", 1)
    t1 = P.op("dve", (lambda E, o=s1, i=x_chunk:
                      E.reduce_sum(out=o, in_=i, axis=AX.X)), waits=waits)
    t2 = P.op("act", (lambda E, o=sq_scr, i=x_chunk, a=s2:
                      E.activation(out=o, in_=i, func=AF.Square, accum_out=a)),
              waits=waits)
    mu = sm1(C, "tiny", 2)
    t3 = P.op("dve", (lambda E, o=mu, i=s1:
                      E.tensor_scalar(out=o, in0=i, scalar1=1.0 / D, scalar2=None,
                                      op0=ALU.mult)), waits=t1)
    m2 = sm1(C, "tiny", 3)
    t4 = P.op("dve", (lambda E, o=m2, i=mu: E.tensor_mul(out=o, in0=i, in1=i)),
              waits=t3)
    var = sm1(C, "tiny", 4)
    t5 = P.op("dve", (lambda E, o=var, i=s2:
                      E.tensor_scalar(out=o, in0=i, scalar1=1.0 / D, scalar2=None,
                                      op0=ALU.mult)), waits=t2)
    t6 = P.op("dve", (lambda E, o=var, i=m2:
                      E.tensor_sub(out=o, in0=o, in1=i)), waits=U(t4, t5))
    t7 = P.op("dve", (lambda E, o=var:
                      E.tensor_scalar(out=o, in0=o, scalar1=EPS_LN, scalar2=None,
                                      op0=ALU.add)), waits=t6)
    sd = sm1(C, "tiny", 5)
    t8 = P.op("act", (lambda E, o=sd, i=var:
                      E.activation(out=o, in_=i, func=AF.Sqrt)), waits=t7)
    t9 = P.op("dve", (lambda E, o=slot_rstd, i=sd: E.reciprocal(out=o, in_=i)),
              waits=t8)
    t10 = P.op("dve", (lambda E, o=slot_negmu, i=mu:
                       E.tensor_scalar(out=o, in0=i, scalar1=-1.0, scalar2=None,
                                       op0=ALU.mult)), waits=t3)
    return U(t9, t10)



def ln_batch(C, waits, per_chunk=None):
    P, sm = C.P, C.sm
    lb = sm["lnb"]
    s1 = C.slab[:, lb:lb + 8]
    s2 = C.slab[:, lb + 8:lb + 16]
    negmu = C.slab[:, lb + 16:lb + 24]
    rstd = C.slab[:, lb + 24:lb + 32]
    tmp = C.slab[:, lb + 32:lb + 40]
    dump = f32v(C, SC_OFF, 2048)
    half_toks = []
    prev_sq = None
    for h in range(2):
        toks = []
        for tc in range(h * 4, h * 4 + 4):
            xc = f32v(C, X_OFF + tc * 2048, 2048)
            wc = U(waits, per_chunk[tc] if per_chunk else None)
            t1 = P.op("dve", (lambda E, o=s1[:, tc:tc + 1], i=xc:
                              E.reduce_sum(out=o, in_=i, axis=AX.X)), waits=wc)
            t2 = P.op("act", (lambda E, o=dump, i=xc, a=s2[:, tc:tc + 1]:
                              E.activation(out=o, in_=i, func=AF.Square,
                                           accum_out=a)),
                      waits=U(wc, prev_sq))
            prev_sq = t2
            toks.append(U(t1, t2))
        t0 = U(*toks)
        c0, c1 = h * 4, h * 4 + 4
        ta = P.op("dve", (lambda E, o=negmu[:, c0:c1], i=s1[:, c0:c1]:
                          E.tensor_scalar(out=o, in0=i, scalar1=-1.0 / D,
                                          scalar2=None, op0=ALU.mult)), waits=t0)
        tb = P.op("dve", (lambda E, o=tmp[:, c0:c1], i=negmu[:, c0:c1]:
                          E.tensor_mul(out=o, in0=i, in1=i)), waits=ta)
        tc_ = P.op("dve", (lambda E, o=rstd[:, c0:c1], i=s2[:, c0:c1]:
                           E.tensor_scalar(out=o, in0=i, scalar1=1.0 / D,
                                           scalar2=None, op0=ALU.mult)), waits=t0)
        td = P.op("dve", (lambda E, o=rstd[:, c0:c1], i=tmp[:, c0:c1]:
                          E.tensor_sub(out=o, in0=o, in1=i)), waits=U(tb, tc_))
        te = P.op("dve", (lambda E, o=rstd[:, c0:c1]:
                          E.tensor_scalar(out=o, in0=o, scalar1=EPS_LN,
                                          scalar2=None, op0=ALU.add)), waits=td)
        tf = P.op("act", (lambda E, o=rstd[:, c0:c1]:
                          E.activation(out=o, in_=o, func=AF.Sqrt)), waits=te)
        tg = P.op("dve", (lambda E, o=rstd[:, c0:c1]:
                          E.reciprocal(out=o, in_=o)), waits=tf)
        half_toks.append(U(tg, prev_sq))
    return half_toks


def transpose_chunk(C, src_f32, dst_bf_off, tc, waits, evac=None):
    P = C.P
    toks = []
    ident = f32v(C, C.sm["ident"], 128)
    dst = bf16v(C, dst_bf_off, 16384).rearrange("p (dc t) -> p dc t", t=1024)
    for grp in range(4):
        b = grp % 2
        w0 = U(C.bank_free[b], waits, C.t_consts)
        tmm = None
        for j in range(4):
            dc = grp * 4 + j
            tmm = P.op("pe",
                       (lambda E, o=C.pb[b][:, j * 128:(j + 1) * 128],
                        i=src_f32[:, dc * 128:(dc + 1) * 128], idn=ident:
                        E.transpose(out=o, in_=i, identity=idn)),
                       waits=(w0 if j == 0 else None), inc=(1 if j == 3 else 0))
        C.bank_data[b] = tmm
        if evac is None:
            dst_ap = dst[:, grp * 4:grp * 4 + 4, tc * 128:(tc + 1) * 128]
            src_ap = C.pb[b][:, 0:512].rearrange("p (dc t) -> p dc t", t=128)
            tev = P.op("dve", (lambda E, o=dst_ap, i=src_ap:
                               E.tensor_copy(out=o, in_=i)), waits=tmm)
        else:
            tes = []
            for j in range(4):
                dc = grp * 4 + j
                s_c, b_c = evac[dc]
                tes.append(P.op("act",
                    (lambda E, o=dst[:, dc, tc * 128:(tc + 1) * 128],
                     i=C.pb[b][:, j * 128:(j + 1) * 128], s_=s_c, bb=b_c:
                     E.activation(out=o, in_=i, func=AF.Identity,
                                  scale=s_, bias=bb)),
                    waits=(U(tmm, C.t_consts) if j == 0 else None)))
            tev = U(*tes)
        C.bank_free[b] = tev
        toks.append(tev)
    return U(*toks)


def layer0(C):
    P, io, sm = C.P, C.io, C.sm
    sc = f32v(C, SC_OFF, 2048)
    xnf = bf16v(C, XNF_OFF, 16384)
    wb, wb_busy = C.wb, C.wb_busy
    tsth = ln_batch(C, C.t_consts, per_chunk=C.t_xc)
    negmu8 = C.slab[:, sm["lnb"] + 16:sm["lnb"] + 24]
    rstd8 = C.slab[:, sm["lnb"] + 24:sm["lnb"] + 32]
    nb8 = C.slab[:, sm["lnb"] + 48:sm["lnb"] + 56]
    tnb_h = []
    for h in range(2):
        c0, c1 = h * 4, h * 4 + 4
        tnb_h.append(P.op("dve", (lambda E, o=nb8[:, c0:c1],
                                  i=negmu8[:, c0:c1], j=rstd8[:, c0:c1]:
                                  E.tensor_mul(out=o, in0=i, in1=j)),
                          waits=tsth[h]))
    xnf_toks = []
    for tc in range(TC):
        xc = f32v(C, X_OFF + tc * 2048, 2048)
        buf = sc if tc % 2 == 0 else f32v(C, SP2_OFF, 2048)
        tap = P.op("act", (lambda E, o=buf, i=xc,
                           a=rstd8[:, tc:tc + 1], b=nb8[:, tc:tc + 1]:
                           E.activation(out=o, in_=i, func=AF.Identity,
                                        scale=a, bias=b)),
                   waits=U(tnb_h[tc // 4],
                           xnf_toks[tc - 2] if tc >= 2 else None))
        xnf_toks.append(transpose_chunk(C, buf, XNF_OFF, tc, tap))
    C.t_xnf = U(*xnf_toks)
    C.t_xnf_h = [U(*xnf_toks[0:4]), U(*xnf_toks[4:8])]

    wfp = [f32v(C, ROWS_OFF, 1024), f32v(C, ROWS_OFF + 1024, 1024),
           f32v(C, WF32_OFF, 1024)]
    wfp_busy = [None, None, None]
    C.wfp, C.wfp_busy = wfp, wfp_busy
    NGM = 16
    mean_toks = []
    absdump = f32v(C, SLAB2_OFF + 512, 1024)
    w1run = C.slab[:, sm["carry"] + 16:sm["carry"] + 17]
    w2run = C.slab[:, sm["carry"] + 17:sm["carry"] + 18]
    wtmp = [sm1(C, "tiny", 11), sm1(C, "tiny", 12)]
    t_wz = P.op("dve", lambda E: E.memset(
        C.slab[:, sm["carry"] + 16:sm["carry"] + 18], 0.0))
    mean_state = {"i": 0, "p": 0, "pa": None,
                  "prun": [t_wz, t_wz], "pend": []}

    def pump_load(n):
        for _ in range(n):
            i = mean_state["i"]
            if i >= 2 * NGM:
                return
            mean_state["i"] += 1
            p = mean_state["p"]
            mean_state["p"] = (p + 1) % 3
            mi, gidx = (0, i) if i < NGM else (1, i - NGM)
            src = io["W1M"] if mi == 0 else io["W2M"]
            td = P.dma("sp", wfp[p], src[gidx], waits=wfp_busy[p],
                       sem="wfp%d" % p)
            mean_state["pend"].append((p, td, mi, i))

    def pump_drain():
        for (p, td, mi, i) in mean_state["pend"]:
            tb = i % 2
            th = P.op("act", (lambda E, i_=wfp[p], o=absdump, a=wtmp[tb]:
                              E.activation(out=o, in_=i_, func=AF.Abs,
                                           accum_out=a)),
                      waits=U(td, mean_state["pa"], mean_state["prun"][tb]))
            mean_state["pa"] = th
            run = w1run if mi == 0 else w2run
            tr2 = P.op("dve", (lambda E, o=run, i_=wtmp[tb]:
                               E.tensor_add(out=o, in0=o, in1=i_)),
                       waits=U(th, *mean_state["prun"]))
            mean_state["prun"][tb] = tr2
            wfp_busy[p] = th
            mean_toks.append(tr2)
        mean_state["pend"] = []

    zeros = f32v(C, SLAB2_OFF, 512)
    tz = P.op("dve", lambda E: E.memset(zeros, 0.0))
    kvv = f32v(C, KV_OFF, 16384)
    rslots = [WBF_OFF, WBF_OFF + 512, WBF_OFF + 1024, WBF_OFF + 1536,
              SLAB2_OFF + 2816, SLAB2_OFF + 3328]
    rbusy = [None] * 6
    ridx = [0]

    def wslot():
        i = ridx[0] % 6
        ridx[0] += 1
        return i, bf16v(C, rslots[i], 1024), "wk%d" % i

    def wload(WX, ec):
        views = []
        for hh in range(2):
            i, vw, sname = wslot()
            td = P.dma("pool", vw,
                       io[WX][ec, :, hh * 8:(hh + 1) * 8, :].rearrange(
                           "p dc f -> p (dc f)"),
                       waits=rbusy[i], sem=sname)
            views.append((i, vw, td))
        return views

    def wtiles(views):
        mms = []
        tds = []
        for hh, (i, vw, td) in enumerate(views):
            tds.append(td)
            for dcl in range(8):
                mms.append(vw[:, dcl * 128:(dcl + 1) * 128])
        return mms, U(*tds)

    gi = 0
    scan_prev = {}
    t_scan_all = []
    sc_reads = []
    prev_exp = None
    prev_t4 = None
    prev_tu = [None, None]
    for ec in range(16):
        pump_load(2)
        kviews = wload("WkX", ec)
        vviews = wload("WvX", ec)
        ktiles, tdk = wtiles(kviews)
        vtiles, tdv = wtiles(vviews)
        phi = [None, None]
        k_reads = []
        k_mms = []
        exps = []
        for s in range(2):
            bk_k = 2 + 2 * s
            mms = [(ktiles[dc],
                    xnf[:, dc * 1024 + s * 512: dc * 1024 + s * 512 + 512])
                   for dc in range(16)]
            tk = mm_group(C, bk_k, mms, waits=U(tdk, C.t_xnf_h[s]),
                          out_slice=C.pb[bk_k][:, 0:512])
            k_mms.append(tk)
            bkc = sm1(C, "bk", ec)
            mslot = sc[:, s * 2048 // 8: s * 256 + 256] if False else                 sc[:, 0:512] if s == 0 else sc[:, 512:1024]
            eslot = sc[:, 1536 + 0:1536 + 256].bitcast(BF16) if False else None
            eslot = (sc[:, 0:512], sc[:, 512:1024])[s]
            t1 = P.op("dve", (lambda E, o=mslot, i=C.pb[bk_k][:, 0:512], a=bkc:
                              E.tensor_scalar(out=o, in0=i, scalar1=a, scalar2=0.0,
                                              op0=ALU.add, op1=ALU.min)),
                      waits=U(tk, C.t_consts, prev_exp, prev_t4))
            t2 = P.op("act", (lambda E, o=eslot, i=mslot:
                              E.activation(out=o, in_=i, func=AF.Exp)),
                      waits=U(t1,))
            prev_exp = t2
            exps.append((tk, t2))
        for s in range(2):
            bk_k = 2 + 2 * s
            tk, t2 = exps[s]
            bkc = sm1(C, "bk", ec)
            eslot = (sc[:, 0:512], sc[:, 512:1024])[s]
            phis = sc[:, 1024 + s * 512: 1536 + s * 512]
            t3 = P.op("dve", (lambda E, o=phis, i=C.pb[bk_k][:, 0:512], a=bkc:
                              E.tensor_scalar(out=o, in0=i, scalar1=a, scalar2=0.0,
                                              op0=ALU.add, op1=ALU.max)),
                      waits=U(tk, prev_tu[s]))
            t4 = P.op("dve", (lambda E, o=phis, i=eslot:
                              E.tensor_add(out=o, in0=o, in1=i)), waits=U(t2, t3))
            prev_t4 = t4
            bank_read(C, bk_k, U(t3, t4))
            phi[s] = (phis, t4)
            k_reads.append(t4)
        for (i, vw, td) in kviews:
            rbusy[i] = U(*k_mms)
        v_reads = []
        for s in range(2):
            bk_v = 3 + 2 * s
            mms = [(vtiles[dc],
                    xnf[:, dc * 1024 + s * 512: dc * 1024 + s * 512 + 512])
                   for dc in range(16)]
            tv = mm_group(C, bk_v, mms, waits=U(tdv, C.t_xnf_h[s]),
                          out_slice=C.pb[bk_v][:, 0:512])
            v_reads.append(tv)
            kvsl = kvv[:, ec * 1024 + s * 512: ec * 1024 + s * 512 + 512]
            bvc = sm1(C, "bv", ec)
            phis, tphi = phi[s]
            tu = P.op("dve", (lambda E, o=kvsl, i=C.pb[bk_v][:, 0:512], a=bvc,
                              h=phis:
                              E.scalar_tensor_tensor(out=o, in0=i, scalar=a, in1=h,
                                                     op0=ALU.add, op1=ALU.mult)),
                      waits=U(tv, tphi, C.t_consts))
            prev_tu[s] = tu
            bank_read(C, bk_v, tu)
            sc_reads.append(tu)
            csl = C.slab[:, sm["carry"] + ec: sm["carry"] + ec + 1]
            tsc = P.op("dve", (lambda E, o=kvsl, z=zeros, ini=csl:
                               E.tensor_tensor_scan(out=o, data0=o, data1=z,
                                                    initial=ini, op0=ALU.add,
                                                    op1=ALU.add)),
                       waits=U(tu, tz, C.t_consts, scan_prev.get(ec)))
            tcar = P.op("dve", (lambda E, o=csl,
                                i=kvv[:, ec * 1024 + s * 512 + 511:
                                      ec * 1024 + s * 512 + 512]:
                                E.tensor_copy(out=o, in_=i)), waits=tsc)
            scan_prev[ec] = tcar
            t_scan_all.append(tcar)
        for (i, vw, td) in vviews:
            rbusy[i] = U(*v_reads)
        pump_drain()
    C.t_scan = U(*t_scan_all)
    C.t_sc_attn = U(*sc_reads)
    t_meandone = U(*mean_toks)

    carry = f32v(C, sm["carry"], 18)
    c1i, c1o = C.cc["c1i"], C.cc["c1o"]
    td = P.dma("pool", c1i.ap(), carry, waits=U(C.t_scan, t_meandone),
               sem="cci")
    if SIM_NOCC:
        tl = []
        for r in range(8):
            tl.append(P.dma("pool", c1o.ap()[r * 128:(r + 1) * 128, :],
                            c1i.ap(), waits=td, sem="agl"))
        C.t_cc1 = U(*tl)
    else:
        C.t_cc1 = P.op("pool", (lambda E: E.collective_compute(
            "AllGather", ALU.bypass, replica_groups=RG,
            ins=[c1i.ap().opt()], outs=[c1o.ap().opt()])), waits=td)

    rbusy = list(rbusy)
    ridx = [0]

    def q_mm(ec):
        qtiles = []
        tds = []
        for hh in range(2):
            i = ridx[0] % 6
            ridx[0] += 1
            vw = bf16v(C, rslots[i], 1024)
            tdg = P.dma("sp", vw,
                        io["WqX"][ec, :, hh * 8:(hh + 1) * 8, :].rearrange(
                            "p dc f -> p (dc f)"),
                        waits=rbusy[i], sem="wq%d" % i)
            tds.append((i, tdg))
            for dcl in range(8):
                qtiles.append(vw[:, dcl * 128:(dcl + 1) * 128])
        tdq = U(*[t for _, t in tds])
        q_mms = []
        banks = []
        for s in range(2):
            bq_b = (0, 1, 2, 3, 4, 5)[(2 * ec + s) % 6]
            banks.append(bq_b)
            mms = [(qtiles[dc],
                    xnf[:, dc * 1024 + s * 512: dc * 1024 + s * 512 + 512])
                   for dc in range(16)]
            tq = mm_group(C, bq_b, mms, waits=U(tdq, C.t_xnf_h[s]),
                          out_slice=C.pb[bq_b][:, 0:512])
            q_mms.append(tq)
        for i, _ in tds:
            rbusy[i] = U(*q_mms)
        return q_mms, banks

    q_pre = {}
    for ec in range(3):
        q_pre[ec] = q_mm(ec)

    s_ag = f32v(C, SC_OFF, 2304, p=8)
    tld = P.dma("pool", s_ag, c1o.ap().rearrange("(r p) c -> r (p c)", p=128),
                waits=U(C.t_cc1, C.t_sc_attn), sem="agl")
    maskp = C.slab[0:8, sm["mask"]:sm["mask"] + 1]
    ones8 = C.slab[0:8, sm["onesb"] + 1:sm["onesb"] + 2]
    w0 = U(tld, C.bank_free[6], C.t_consts)
    tmo = None
    s_ag3 = s_ag.rearrange("r (p c) -> r p c", c=18)
    for cch in range(18):
        rr = maskp if cch < 16 else ones8
        tmo = P.op("pe",
                   (lambda E, o=C.pb[6][:, cch:cch + 1],
                    l=s_ag3[:, :, cch], r=rr:
                    E.matmul(out=o, lhsT=l, rhs=r, start=True, stop=True)),
                   waits=(w0 if cch == 0 else None), inc=(1 if cch == 17 else 0))
    C.bank_data[6] = tmo
    offs = f32v(C, sm["offs"], 18)
    tof = P.op("dve", (lambda E, o=offs, i=C.pb[6][:, 0:18]:
                       E.tensor_copy(out=o, in_=i)), waits=tmo)
    C.bank_free[6] = tof
    C.t_offs = tof
    C.t_agread = U(tof, tld)
    wfp_busy[2] = U(wfp_busy[2], C.t_agread)

    onesc = C.slab[:, sm["onesb"] + 1:sm["onesb"] + 2]
    w1vec = C.slab[:, sm["offs"] + 16:sm["offs"] + 17]
    w2vec = C.slab[:, sm["offs"] + 17:sm["offs"] + 18]
    w0 = U(C.bank_free[6], C.t_consts)
    tmm = P.op("pe", (lambda E, o=C.pb[6][0:1, 0:1], l=w1vec, r=onesc:
                      E.matmul(out=o, lhsT=l, rhs=r, start=True, stop=True)),
               waits=w0)
    C.bank_data[6] = tmm
    mean_s = C.slab[0:1, sm["mean"]:sm["mean"] + 1]
    t1m = P.op("dve", (lambda E, o=mean_s, i=C.pb[6][0:1, 0:1]:
                       E.tensor_scalar(out=o, in0=i, scalar1=1.0 / (DFF * D),
                                       scalar2=EPS_BIT, op0=ALU.mult,
                                       op1=ALU.max)), waits=tmm)
    C.bank_free[6] = t1m
    t2m = P.op("dve", (lambda E, o=mean_s:
                       E.tensor_scalar(out=o, in0=o, scalar1=0.5, scalar2=None,
                                       op0=ALU.mult)), waits=t1m)
    w0 = U(C.bank_free[7], t2m, C.t_consts)
    tbc = P.op("pe", (lambda E, o=C.pb[7][:, 0:1], l=ones_rr(C), r=mean_s:
                      E.matmul(out=o, lhsT=l, rhs=r, start=True, stop=True)),
               waits=w0)
    C.bank_data[7] = tbc
    thrp1 = C.slab[:, sm["thr"]:sm["thr"] + 1]
    thrn1 = C.slab[:, sm["thr"] + 1:sm["thr"] + 2]
    t3m = P.op("dve", (lambda E, o=thrp1, i=C.pb[7][:, 0:1]:
                       E.tensor_copy(out=o, in_=i)), waits=tbc)
    C.bank_free[7] = t3m
    t4m = P.op("dve", (lambda E, o=thrn1, i=thrp1:
                       E.tensor_scalar(out=o, in0=i, scalar1=-1.0, scalar2=None,
                                       op0=ALU.mult)), waits=t3m)
    C.t_thr1 = U(t3m, t4m)

    w0 = U(C.bank_free[6], C.t_consts)
    tmm2 = P.op("pe", (lambda E, o=C.pb[6][0:1, 0:1], l=w2vec, r=onesc:
                       E.matmul(out=o, lhsT=l, rhs=r, start=True, stop=True)),
                waits=w0)
    C.bank_data[6] = tmm2
    mean2_s = C.slab[0:1, sm["mean"] + 1: sm["mean"] + 2]
    t1i = P.op("dve", (lambda E, o=mean2_s, i=C.pb[6][0:1, 0:1]:
                       E.tensor_scalar(out=o, in0=i, scalar1=1.0 / (DFF * D),
                                       scalar2=EPS_BIT, op0=ALU.mult,
                                       op1=ALU.max)), waits=tmm2)
    C.bank_free[6] = t1i
    t2i = P.op("dve", (lambda E, o=mean2_s:
                       E.reciprocal(out=o, in_=o)), waits=t1i)
    w0 = U(C.bank_free[7], t2i, C.t_consts)
    tbc2 = P.op("pe", (lambda E, o=C.pb[7][:, 0:1], l=ones_rr(C), r=mean2_s:
                       E.matmul(out=o, lhsT=l, rhs=r, start=True, stop=True)),
                waits=w0)
    C.bank_data[7] = tbc2
    inv2s = C.slab[:, sm["thr"] + 2:sm["thr"] + 3]
    t3i = P.op("dve", (lambda E, o=inv2s, i=C.pb[7][:, 0:1]:
                       E.tensor_copy(out=o, in_=i)), waits=tbc2)
    C.bank_free[7] = t3i
    C.t_thr = U(C.t_thr1, t3i)

    def ctx_tile(dc, tch):
        s, i0 = tch // 4, (tch % 4) * 64
        return bf16v(C, KV_OFF + dc * 1024 + s * 512 + i0, 128)
    prev_exp = None
    prev_t4q = None
    prev_t5 = None
    ctx_t = []
    for ec in range(16):
        q_mms, q_banks = q_pre.pop(ec) if ec in q_pre else q_mm(ec)
        for s in range(2):
            bq_b = q_banks[s]
            tq = q_mms[s]
            bqc = sm1(C, "bq", ec)
            mslot = sc[:, 0:512]; eslot = sc[:, 512:1024]; phis = sc[:, 1024:1536]
            t1 = P.op("dve", (lambda E, o=mslot, i=C.pb[bq_b][:, 0:512], a=bqc:
                              E.tensor_scalar(out=o, in0=i, scalar1=a, scalar2=0.0,
                                              op0=ALU.add, op1=ALU.min)),
                      waits=U(tq, C.t_agread, prev_exp))
            t2 = P.op("act", (lambda E, o=eslot, i=mslot:
                              E.activation(out=o, in_=i, func=AF.Exp)),
                      waits=U(t1, prev_t4q))
            prev_exp = t2
            t3 = P.op("dve", (lambda E, o=phis, i=C.pb[bq_b][:, 0:512], a=bqc:
                              E.tensor_scalar(out=o, in0=i, scalar1=a, scalar2=0.0,
                                              op0=ALU.add, op1=ALU.max)),
                      waits=U(tq, prev_t5))
            t4 = P.op("dve", (lambda E, o=phis, i=eslot:
                              E.tensor_add(out=o, in0=o, in1=i)), waits=U(t2, t3))
            prev_t4q = t4
            bank_read(C, bq_b, U(t1, t3))
            kvsl = kvv[:, ec * 1024 + s * 512: ec * 1024 + s * 512 + 512]
            ctxsl = kvsl.bitcast(BF16)[:, 0:512]
            oc = C.slab[:, sm["offs"] + ec: sm["offs"] + ec + 1]
            t5 = P.op("dve", (lambda E, o=ctxsl, i=kvsl, a=oc, h=phis:
                              E.scalar_tensor_tensor(out=o, in0=i, scalar=a,
                                                     in1=h, op0=ALU.add,
                                                     op1=ALU.mult)),
                      waits=U(t4, C.t_offs))
            prev_t5 = t5
            ctx_t.append(t5)
    t_ctx = U(*ctx_t)
    bo_seg = C.slab[0:1, SLAB2_OFF:SLAB2_OFF + 256].bitcast(BF16)
    invs = invs_bf(C)
    t_boseg = None
    t_boseg_rd = None
    wo_x = [None] * 8
    t_wo_last = []
    for eg in range(4):
        t_boseg = P.dma("act", bo_seg, io["bo_r"][:, eg * 512:(eg + 1) * 512],
                        waits=U(t_ctx, t_boseg_rd), sem="brow")
        for ecp in range(8):
            i = ridx[0] % 6
            ridx[0] += 1
            vw = bf16v(C, rslots[i], 1024).rearrange("p (d e) -> p d e", e=512)
            tdw = P.dma("sp", vw,
                        io["WoR"][ecp * 2:ecp * 2 + 2, :, eg * 512:(eg + 1) * 512]
                        .rearrange("d p e -> p d e"),
                        waits=rbusy[i], sem="wq%d" % i)
            last = None
            for dcl in range(2):
                dc = ecp * 2 + dcl
                for tch in range(8):
                    st = (dc == 0)
                    w = U(C.bank_free[tch] if st else None,
                          t_ctx if st else None, tdw)
                    last = P.op("pe",
                        (lambda E, o=C.pb[tch][:, 0:512],
                         l=ctx_tile(dc, tch),
                         r=vw[:, dcl, :], s_=st:
                         E.matmul(out=o, lhsT=l, rhs=r, start=s_, stop=False)),
                        waits=w, inc=(1 if tch == 7 else 0))
            rbusy[i] = last
        seg_rd = []
        for tch in range(8):
            tokmm = P.op("pe", (lambda E, o=C.pb[tch][:, 0:512], l=invs,
                                r=bo_seg:
                                E.matmul(out=o, lhsT=l, rhs=r, start=False,
                                         stop=True)),
                         waits=U(t_boseg, C.t_consts), inc=1)
            seg_rd.append(tokmm)
            C.bank_data[tch] = tokmm
            xsl = f32v(C, X_OFF + tch * 2048 + eg * 512, 512)
            tev = P.op("dve", (lambda E, o=xsl, i=C.pb[tch][:, 0:512]:
                               E.scalar_tensor_tensor(out=o, in0=i, scalar=SCALE,
                                                      in1=o, op0=ALU.mult,
                                                      op1=ALU.add)),
                       waits=U(tokmm, C.t_xnf, wo_x[tch]))
            C.bank_free[tch] = tev
            wo_x[tch] = tev
            t_wo_last.append(tev)
        t_boseg_rd = U(*seg_rd)
    t_wo_done = U(*t_wo_last)
    C.t_ffn_x = wo_x
    C.t_hfree0 = t_wo_done
    C.t_sc_q = U(t_ctx, t_wo_done)
    C.t_twos = P.op("dve", lambda E: E.memset(invs_bf(C), 2.0),
                    waits=t_wo_done)

    if DBG_STOP == "wo":
        C.t_xdone = t_wo_done
        C.t_rows_free = C.t_xdone
        C.t_kv_read = U(t_ctx, t_wo_done)
        return
    ffn(C)
    C.t_xdone = U(*C.t_ffn_x)
    if DBG_STOP == "ffn":
        pass
    C.t_rows_free = C.t_xdone
    C.t_kv_read = U(t_ctx, t_wo_done)


def ffn(C):
    P, io, sm = C.P, C.io, C.sm
    xnf = bf16v(C, XNF_OFF, 16384)
    wfp = list(C.wfp) + [f32v(C, KV_OFF + 3072, 1024),
                         f32v(C, KV_OFF + 4096, 1024)]
    wfp_busy = list(C.wfp_busy) + [C.t_sc_q, C.t_sc_q]
    wbX = [bf16v(C, KV_OFF + 5120, 2048), bf16v(C, KV_OFF + 6144, 2048)]
    wbX_busy = [C.t_sc_q, C.t_sc_q]
    f2w = [bf16v(C, WBF_OFF + i * 512, 1024) for i in range(4)] + \
          [bf16v(C, KV_OFF + 1024 + i * 512, 1024) for i in range(4)]
    f2w_busy = [U(C.wb_busy[0], C.t_sc_q), U(C.wb_busy[0], C.t_sc_q),
                U(C.wb_busy[1], C.t_sc_q), U(C.wb_busy[1], C.t_sc_q),
                C.t_sc_q, C.t_sc_q, C.t_sc_q, C.t_sc_q]
    thrp1 = C.slab[:, sm["thr"]:sm["thr"] + 1]
    thrn1 = C.slab[:, sm["thr"] + 1:sm["thr"] + 2]
    inv2 = C.slab[:, sm["thr"] + 2:sm["thr"] + 3]
    hslab = bf16v(C, SLAB2_OFF, 8192)
    bscrF = bf16v(C, KV_OFF + 512, 1024)
    sgn = [bf16v(C, SC_OFF + i * 512, 1024) for i in range(4)]
    sgn_busy = [C.t_sc_q] * 4
    b2seg = C.slab[0:1, KV_OFF:KV_OFF + 256].bitcast(BF16)

    st_ = {"gfp": 0, "prev_tc": C.t_sc_q, "t_b2seg_rd": None,
           "t1": {}, "t2": {}}
    t_hfree = C.t_hfree0

    def wfp_slot():
        pf = st_["gfp"] % 5
        st_["gfp"] += 1
        return pf

    def tern1(g, fcl):
        fc = g * 8 + fcl
        pw = fcl % 2
        w_prev = wbX_busy[pw]
        tern_tok = []
        for hh in range(2):
            pf = wfp_slot()
            hsl_in = io["W1X"][fc, :, hh * 8:(hh + 1) * 8, :].rearrange(
                "p dc f -> p (dc f)")
            td = P.dma("sp", wfp[pf], hsl_in, waits=wfp_busy[pf],
                       sem="wfp%d" % pf)
            dst = wbX[pw][:, hh * 1024:(hh + 1) * 1024]
            ta = P.op("dve", (lambda E, o=dst, i=wfp[pf], a=thrp1:
                              E.tensor_scalar(out=o, in0=i, scalar1=a,
                                              scalar2=None, op0=ALU.is_gt)),
                      waits=U(td, C.t_thr, w_prev))
            tb = P.op("dve", (lambda E, o=bscrF, i=wfp[pf], a=thrn1:
                              E.tensor_scalar(out=o, in0=i, scalar1=a,
                                              scalar2=None, op0=ALU.is_lt)),
                      waits=U(td, C.t_thr, st_["prev_tc"]))
            tc_ = P.op("dve", (lambda E, o=dst, i=bscrF:
                               E.tensor_sub(out=o, in0=o, in1=i)),
                       waits=U(ta, tb))
            st_["prev_tc"] = tc_
            wfp_busy[pf] = U(ta, tb)
            tern_tok.append(tc_)
        wbX_busy[pw] = None
        st_["t1"][(g, fcl)] = U(*tern_tok)

    def tern2(g, eg):
        toks = []
        for pair in range(4):
            fc = g * 8 + pair * 2
            pf = wfp_slot()
            gr = io["W2R"][fc:fc + 2, :, eg * 512:(eg + 1) * 512].rearrange(
                "f p e -> p f e")
            td = P.dma("sp", wfp[pf].rearrange("p (f e) -> p f e", e=512),
                       gr, waits=wfp_busy[pf], sem="wfp%d" % pf)
            ws = 4 * (eg % 2) + pair
            sa, sb = 2 * (pair % 2), 2 * (pair % 2) + 1
            ts1 = P.op("act", (lambda E, o=sgn[sa], i=wfp[pf], a=inv2,
                               b=sm1(C, "half", 0):
                               E.activation(out=o, in_=i, func=AF.Sign,
                                            scale=a, bias=b)),
                       waits=U(td, C.t_thr, sgn_busy[sa]))
            ts2 = P.op("act", (lambda E, o=sgn[sb], i=wfp[pf], a=inv2,
                               b=sm1(C, "half", 1):
                               E.activation(out=o, in_=i, func=AF.Sign,
                                            scale=a, bias=b)),
                       waits=U(td, C.t_thr, sgn_busy[sb]))
            tsum = P.op("dve", (lambda E, o=f2w[ws], i=sgn[sa],
                                j=sgn[sb]:
                                E.tensor_add(out=o, in0=i, in1=j)),
                        waits=U(ts1, ts2, f2w_busy[ws]))
            sgn_busy[sa] = tsum
            sgn_busy[sb] = tsum
            wfp_busy[pf] = U(ts1, ts2)
            f2w_busy[ws] = None
            toks.append((ws, tsum))
        st_["t2"][(g, eg)] = toks

    def mm2(g, eg, t_h, ev_all):
        toks = st_["t2"].pop((g, eg))
        if g == 0:
            tseg = P.dma("act", b2seg, io["b2_r"][:, eg * 512:(eg + 1) * 512],
                         waits=U(C.t_thr, st_["t_b2seg_rd"], C.t_sc_q),
                         sem="b2s")
        for pair in range(4):
            ws, tsum = toks[pair]
            dst = f2w[ws]
            last = None
            for half in range(2):
                fcl = pair * 2 + half
                laststop = (fcl == 7 and g > 0)
                for tch in range(8):
                    stt = (fcl == 0)
                    w = U(C.bank_free[tch] if stt else None,
                          t_h if stt else None, tsum)
                    last = P.op("pe",
                        (lambda E, o=C.pb[tch][:, 0:512],
                         l=hslab[:, fcl * 1024 + tch * 128:
                                 fcl * 1024 + (tch + 1) * 128],
                         r=dst[:, half * 512:(half + 1) * 512],
                         s_=stt, sp=laststop:
                         E.matmul(out=o, lhsT=l, rhs=r, start=s_, stop=sp)),
                        waits=w, inc=(1 if tch == 7 else 0))
            f2w_busy[ws] = last
        seg_rd = []
        for tch in range(8):
            if g == 0:
                tokb = P.op("pe", (lambda E, o=C.pb[tch][:, 0:512],
                                   l=invs_bf(C),
                                   r=b2seg:
                                   E.matmul(out=o, lhsT=l, rhs=r, start=False,
                                            stop=True)),
                            waits=U(tseg, C.t_twos), inc=1)
                seg_rd.append(tokb)
            else:
                tokb = last
            C.bank_data[tch] = tokb
            xsl = f32v(C, X_OFF + tch * 2048 + eg * 512, 512)
            tev = P.op("dve", (lambda E, o=xsl, i=C.pb[tch][:, 0:512]:
                               E.scalar_tensor_tensor(out=o, in0=i, scalar=0.5,
                                                      in1=o, op0=ALU.mult,
                                                      op1=ALU.add)),
                       waits=U(tokb, C.t_ffn_x[tch]))
            C.bank_free[tch] = tev
            C.t_ffn_x[tch] = tev
            ev_all.append(tev)
        if g == 0:
            st_["t_b2seg_rd"] = U(*seg_rd)
    tern1(0, 0)
    tern1(0, 1)
    for g in range(8):
        h_toks = []
        for fcl in range(8):
            fc = g * 8 + fcl
            pw = fcl % 2
            ttern = st_["t1"].pop((g, fcl))
            for s in range(2):
                b = (2 * fcl + s) % 4
                mms = [(wbX[pw][:, dc * 128:(dc + 1) * 128],
                        xnf[:, dc * 1024 + s * 512: dc * 1024 + s * 512 + 512])
                       for dc in range(16)]
                tm = mm_group(C, b, mms, waits=U(ttern, C.t_xnf),
                              out_slice=C.pb[b][:, 0:512])
                wbX_busy[pw] = U(wbX_busy[pw], tm)
                hsl = hslab[:, fcl * 1024 + s * 512: fcl * 1024 + s * 512 + 512]
                b1c = C.slab[:, sm["b1t"] + fc: sm["b1t"] + fc + 1]
                tg = P.op("act", (lambda E, o=hsl, i=C.pb[b][:, 0:512], a=b1c:
                                  E.activation(out=o, in_=i, func=AF.Gelu, bias=a)),
                          waits=U(tm, C.t_consts, t_hfree))
                bank_read(C, b, tg)
                h_toks.append(tg)
            if fcl < 6:
                tern1(g, fcl + 2)
            if fcl == 5:
                tern2(g, 0)
        t_h = U(*h_toks)
        ev_all = []
        tern2(g, 1)
        mm2(g, 0, t_h, ev_all)
        tern2(g, 2)
        mm2(g, 1, t_h, ev_all)
        tern2(g, 3)
        mm2(g, 2, t_h, ev_all)
        if g < 7:
            tern1(g + 1, 0)
            tern1(g + 1, 1)
        mm2(g, 3, t_h, ev_all)
        t_hfree = U(*ev_all)
    C.wb_busy[0] = U(f2w_busy[0], f2w_busy[1])
    C.wb_busy[1] = U(f2w_busy[2], f2w_busy[3])
    C.t_sc_free = t_hfree
    C.t_slab2_free = t_hfree
    C.t_rows_free_ffn = t_hfree


def layer1(C):
    P, io, sm = C.P, C.io, C.sm
    sc = f32v(C, SC_OFF, 2048)
    sc2 = f32v(C, SC2_OFF, 2048)
    sq = f32v(C, SP2_OFF, 2048)
    wb, wb_busy = C.wb, C.wb_busy
    if C.part == "ALL":
        base_wait = C.t_xdone
        wkv = C.t_kv_read
        t_slab2 = C.t_xdone
        t_sc_free = C.t_xdone
        t_rows = C.t_rows_free
    else:
        base_wait = U(C.t_x, C.t_consts)
        wkv = None
        t_slab2 = None
        t_sc_free = None
        t_rows = C.t_consts
    C.t_kv_l1_free = wkv

    if C.part == "ALL":
        tsth = ln_batch(C, C.t_consts, per_chunk=C.t_ffn_x)
    else:
        tsth = ln_batch(C, U(base_wait, C.t_consts))
    lb = sm["lnb"]
    negmu8 = C.slab[:, lb + 16:lb + 24]
    rstd8 = C.slab[:, lb + 24:lb + 32]
    tmp8 = C.slab[:, lb + 32:lb + 40]
    rstd2_8 = C.slab[:, lb + 40:lb + 48]
    zscale8 = C.slab[:, sm["stat"]:sm["stat"] + 8]
    mA8 = C.slab[:, sm["stat"] + 8:sm["stat"] + 16]
    nb8 = C.slab[:, sm["lnb"] + 48:sm["lnb"] + 56]
    tnb_h, tsc_h = [], []
    for hh_ in range(2):
        c0, c1 = hh_ * 4, hh_ * 4 + 4
        tst = tsth[hh_]
        ta = P.op("dve", (lambda E, o=tmp8[:, c0:c1], i=rstd8[:, c0:c1]:
                          E.tensor_mul(out=o, in0=i, in1=i)), waits=tst)
        tb = P.op("dve", (lambda E, o=rstd2_8[:, c0:c1], i=tmp8[:, c0:c1]:
                          E.tensor_scalar(out=o, in0=i, scalar1=-EPS_LN,
                                          scalar2=1.0 + EPS_LN, op0=ALU.mult,
                                          op1=ALU.add)), waits=ta)
        tc0 = P.op("act", (lambda E, o=rstd2_8[:, c0:c1]:
                           E.activation(out=o, in_=o, func=AF.Sqrt)), waits=tb)
        td = P.op("dve", (lambda E, o=rstd2_8[:, c0:c1]:
                          E.reciprocal(out=o, in_=o)), waits=tc0)
        te = P.op("dve", (lambda E, o=zscale8[:, c0:c1], i=rstd8[:, c0:c1],
                          j=rstd2_8[:, c0:c1]:
                          E.tensor_mul(out=o, in0=i, in1=j)), waits=td)
        tf = P.op("dve", (lambda E, o=mA8[:, c0:c1], i=rstd8[:, c0:c1]:
                          E.tensor_scalar(out=o, in0=i, scalar1=2.0, scalar2=1.0,
                                          op0=ALU.mult, op1=ALU.add)), waits=tst)
        tg = P.op("dve", (lambda E, o=tmp8[:, c0:c1], i=negmu8[:, c0:c1],
                          j=rstd8[:, c0:c1]:
                          E.tensor_mul(out=o, in0=i, in1=j)), waits=U(tst, te))
        th = P.op("dve", (lambda E, o=tmp8[:, c0:c1]:
                          E.tensor_scalar(out=o, in0=o, scalar1=2.0, scalar2=None,
                                          op0=ALU.mult)), waits=tg)
        tsc_h.append(U(te, tf, th))
        tnb_h.append(P.op("dve", (lambda E, o=nb8[:, c0:c1],
                                  i=negmu8[:, c0:c1], j=rstd8[:, c0:c1]:
                                  E.tensor_mul(out=o, in0=i, in1=j)), waits=tst))
    tsc_all = U(*tsc_h)
    gb = [(C.slab[:, sm["gam"] + dc: sm["gam"] + dc + 1],
           C.slab[:, sm["bet"] + dc: sm["bet"] + dc + 1]) for dc in range(16)]
    xnf_toks, znf_toks = [], []
    tap_toks, tz_toks = [], []
    C.t_xup = [None] * 8

    def emit_tap(tc):
        xc = f32v(C, X_OFF + tc * 2048, 2048)
        bx = sc if tc % 2 == 0 else sq
        tap = P.op("act", (lambda E, o=bx, i=xc, a=rstd8[:, tc:tc + 1],
                           b=nb8[:, tc:tc + 1]:
                           E.activation(out=o, in_=i, func=AF.Identity,
                                        scale=a, bias=b)),
                   waits=U(tnb_h[tc // 4], t_sc_free, wkv,
                           xnf_toks[tc - 2] if tc >= 2 else None))
        tap_toks.append(tap)

    def emit_tz(tc):
        xc = f32v(C, X_OFF + tc * 2048, 2048)
        bz = sc2 if tc % 2 == 0 else f32v(C, TAB_OFF, 2048)
        tz = P.op("dve", (lambda E, o=bz, i=xc, a=negmu8[:, tc:tc + 1],
                          b=zscale8[:, tc:tc + 1]:
                          E.tensor_scalar(out=o, in0=i, scalar1=a, scalar2=b,
                                          op0=ALU.add, op1=ALU.mult)),
                  waits=U(tsc_h[tc // 4], wkv,
                          znf_toks[tc - 2] if tc >= 2 else None))
        tz_toks.append(tz)

    emit_tap(0)
    emit_tz(0)
    for tc in range(TC):
        bx = sc if tc % 2 == 0 else sq
        bz = sc2 if tc % 2 == 0 else f32v(C, TAB_OFF, 2048)
        if tc + 1 < TC:
            emit_tap(tc + 1)
        xnf_toks.append(transpose_chunk(C, bx, XNF_OFF, tc, tap_toks[tc]))
        if tc + 1 < TC:
            emit_tz(tc + 1)
        znf_toks.append(transpose_chunk(C, bz, ZNF_OFF, tc, tz_toks[tc],
                                        evac=gb))
    C.t_xnf = U(*xnf_toks)
    def emit_xup_one(tc):
        xc = f32v(C, X_OFF + tc * 2048, 2048)
        tx = P.op("dve", (lambda E, o=xc, a=mA8[:, tc:tc + 1],
                          b=tmp8[:, tc:tc + 1]:
                          E.tensor_scalar(out=o, in0=o, scalar1=a, scalar2=b,
                                          op0=ALU.mult, op1=ALU.add)),
                  waits=U(tap_toks[tc], tz_toks[tc], tsc_h[tc // 4]))
        C.t_xup[tc] = tx
    C.emit_xup_one = emit_xup_one
    t_sq_guard = U(xnf_toks[1], xnf_toks[3], xnf_toks[5], xnf_toks[7])
    t_tab_guard = U(znf_toks[1], znf_toks[3], znf_toks[5], znf_toks[7])
    xnf = bf16v(C, XNF_OFF, 16384)
    znf = bf16v(C, ZNF_OFF, 16384)
    C.t_Xnf = U(*znf_toks)

    tab = [bf16v(C, TAB_OFF, 4096), bf16v(C, TAB_OFF, 4096)]
    tab_busy = [t_tab_guard, t_tab_guard]
    Pslab = bf16v(C, SLAB2_OFF, 8192)
    onesbf = C.slab[:, sm["onesb"]:sm["onesb"] + 1].bitcast(BF16)[:, 0:1]
    ident = f32v(C, sm["ident"], 128)

    def rope_ap(p, half):
        return tab[p][:, half * 2048: half * 2048 + 2048].rearrange(
            "p (cc t) -> p cc t", t=512)

    gi = 0
    Apart_toks = []
    C.prev_scrd = None
    C.prev_rot = None
    wbL = [wb[0], wb[1],
           bf16v(C, HF_OFF, 2048), bf16v(C, WH_OFF, 2048)]
    wbL_busy = [wb_busy[0], wb_busy[1],
                U(wkv, base_wait), U(wkv, base_wait)]
    wbL_sem = ["wb0", "wb1", "wi2", "wi3"]
    for s in range(2):
        ptab = 0
        tdc = P.dma("sp", rope_ap(ptab, 0),
                    io["ropec"][:, :, s * 512:(s + 1) * 512],
                    waits=U(tab_busy[ptab], wkv), sem="tab")
        tds = P.dma("sp", rope_ap(ptab, 1),
                    io["ropes"][:, :, s * 512:(s + 1) * 512],
                    waits=U(tab_busy[ptab], wkv), sem="tab")
        t_tab = U(tdc, tds)
        nrm_open = {"k": False, "v": False}
        g0 = t_sq_guard if s == 0 else None
        nrm_last = {"k": g0, "v": g0}
        sq_last = {"k": None, "v": None}

        def proj(WX, bank, ec, pgi):
            tdw = P.dma("pool", wbL[pgi].rearrange("p (dc f) -> p dc f", f=128),
                        io[WX][ec], waits=wbL_busy[pgi], sem=wbL_sem[pgi])
            mms = [(wbL[pgi][:, dc * 128:(dc + 1) * 128],
                    znf[:, dc * 1024 + s * 512: dc * 1024 + s * 512 + 512])
                   for dc in range(16)]
            tk = mm_group(C, bank, mms, waits=U(tdw, C.t_Xnf),
                          out_slice=C.pb[bank][:, 0:512])
            return tk

        def evac(bank, dstsl, bias_name, ec, tk, extra=None):
            bc = C.slab[:, sm[bias_name] + ec: sm[bias_name] + ec + 1]
            te = P.op("act", (lambda E, o=dstsl, i=C.pb[bank][:, 0:512], a=bc:
                              E.activation(out=o, in_=i, func=AF.Identity,
                                           bias=a)),
                      waits=U(tk, C.t_consts, extra))
            bank_read(C, bank, te)
            return te

        def sqsum(src_sl, which, tprev, stop):
            bank = 4 if which == "k" else 5
            sq_off = 0 if which == "k" else 256
            sq_sl = sq[:, sq_off:sq_off + 256].bitcast(BF16)
            t1 = P.op("act", (lambda E, o=sq_sl, i=src_sl:
                              E.activation(out=o, in_=i, func=AF.Square)),
                      waits=U(tprev, nrm_last[which]))
            st = not nrm_open[which]
            nrm_open[which] = True
            w0 = U(t1, C.bank_free[bank]) if st else U(t1)
            t2 = P.op("pe", (lambda E, o=C.pb[bank][0:1, 0:512], l=onesbf,
                             r=sq_sl, s_=st, sp=stop:
                             E.matmul(out=o, lhsT=l, rhs=r, start=s_, stop=sp)),
                      waits=w0, inc=1)
            nrm_last[which] = t2
            if stop:
                C.bank_data[bank] = t2
                C.bank_free[bank] = None
            return t2

        P_toks = {}

        def do_chunk_post(ksl, vsl, tks, tvs, ecs):
            tqk = sqsum(ksl, "k", tks, stop=(ecs == 15))
            tqv = sqsum(vsl, "v", tvs, stop=(ecs == 15))
            psl = Pslab[:, ecs * 512:(ecs + 1) * 512]
            tp = P.op("dve", (lambda E, o=psl, i=ksl, j=vsl:
                              E.tensor_mul(out=o, in0=i, in1=j)),
                      waits=U(tks, tvs, t_slab2))
            P_toks[ecs] = U(tp, tqk, tqv)
            return U(tp, tqk, tqv)

        pend_post = []

        def flush_post():
            toks = [fn() for fn in pend_post]
            pend_post.clear()
            return U(*toks) if toks else None

        for pr in range(4):
            ktoks, vtoks = {}, {}
            for half, ecs in ((0, pr), (1, pr + 4)):
                kb = (2, 6)[ecs % 2]
                vb = (3, 7)[ecs % 2]
                pgi = gi % 4; gi += 1
                tk = proj("WinKX", kb, ecs, pgi)
                ksl = sc[:, half * 512: half * 512 + 512]
                ktoks[half] = evac(kb, ksl, "binK", ecs, tk,
                                   extra=U(t_sc_free if s == 0 and pr == 0
                                           else None, C.prev_scrd,
                                           C.prev_rot))
                wbL_busy[pgi] = tk
                pgi = gi % 4; gi += 1
                tv = proj("WinVX", vb, ecs, pgi)
                vsl = sc2[:, half * 512: half * 512 + 512]
                vtoks[half] = evac(vb, vsl, "binV", ecs, tv,
                                   extra=U(C.prev_scrd, C.prev_rot))
                wbL_busy[pgi] = tv
            t_post = flush_post()
            if t_post:
                C.prev_scrd = t_post
            cosc = rope_ap(ptab, 0)[:, pr, :]
            sinc = rope_ap(ptab, 1)[:, pr, :]
            rots = []
            for scr, toks in ((sc, ktoks), (sc2, vtoks)):
                x1 = scr[:, 0:512]; x2 = scr[:, 512:1024]
                t1s = scr[:, 1024:1536]; t2s = scr[:, 1536:2048]
                ta = P.op("dve", (lambda E, o=t1s, i=x1, cc=cosc:
                                  E.tensor_mul(out=o, in0=i, in1=cc)),
                          waits=U(toks[0], toks[1], t_tab, t_post))
                tb = P.op("dve", (lambda E, o=t2s, i=x2, ss=sinc:
                                  E.tensor_mul(out=o, in0=i, in1=ss)),
                          waits=t_post)
                tcx = P.op("dve", (lambda E, o=t1s, i=t2s:
                                   E.tensor_sub(out=o, in0=o, in1=i)),
                           waits=U(ta, tb))
                td_ = P.op("dve", (lambda E, o=t2s, i=x2, cc=cosc:
                                   E.tensor_mul(out=o, in0=i, in1=cc)),
                           waits=U(tb, tcx))
                te_ = P.op("dve", (lambda E, o=x2, i=x1, ss=sinc:
                                   E.tensor_mul(out=o, in0=i, in1=ss)),
                           waits=td_)
                tf_ = P.op("dve", (lambda E, o=t2s, i=x2:
                                   E.tensor_add(out=o, in0=o, in1=i)),
                           waits=U(td_, te_))
                rots.append((t1s, tcx, t2s, tf_))
            (k1, tk1, k2, tk2) = rots[0]
            (v1, tv1, v2, tv2) = rots[1]
            C.prev_rot = U(tk1, tk2, tv1, tv2)
            pend_post.append(lambda a=k1, b=v1, c=U(tk1, tv1), e=pr:
                             do_chunk_post(a, b, c, c, e))
            pend_post.append(lambda a=k2, b=v2, c=U(tk2, tv2), e=pr + 4:
                             do_chunk_post(a, b, c, c, e))
        for ecs in range(8, 16):
            kb = (2, 6)[ecs % 2]
            vb = (3, 7)[ecs % 2]
            half = ecs % 2
            pgi = gi % 4; gi += 1
            tk = proj("WinKX", kb, ecs, pgi)
            ksl = sc[:, half * 512: half * 512 + 512]
            tke = evac(kb, ksl, "binK", ecs, tk,
                       extra=U(C.prev_scrd, C.prev_rot))
            wbL_busy[pgi] = tk
            pgi = gi % 4; gi += 1
            tv = proj("WinVX", vb, ecs, pgi)
            vsl = sc2[:, half * 512: half * 512 + 512]
            tve = evac(vb, vsl, "binV", ecs, tv,
                       extra=U(C.prev_scrd, C.prev_rot))
            wbL_busy[pgi] = tv
            t_post = flush_post()
            if t_post:
                C.prev_scrd = t_post
            pend_post.append(lambda a=ksl, b=vsl, c=U(tke, tve), e=ecs:
                             do_chunk_post(a, b, c, c, e))
        C.prev_scrd = flush_post()
        nk_sb = sq[0:1, 1024:1536]
        t1 = P.op("dve", (lambda E, o=nk_sb, i=C.pb[4][0:1, 0:512]:
                          E.tensor_copy(out=o, in_=i)), waits=C.bank_data[4])
        C.bank_free[4] = t1
        prod = sq[0:1, 1536:2048]
        t2 = P.op("dve", (lambda E, o=prod, i=nk_sb, j=C.pb[5][0:1, 0:512]:
                          E.tensor_mul(out=o, in0=i, in1=j)),
                  waits=U(t1, C.bank_data[5]))
        C.bank_free[5] = t2
        t3 = P.op("act", (lambda E, o=prod:
                          E.activation(out=o, in_=o, func=AF.Sqrt)), waits=t2)
        t4 = P.op("dve", (lambda E, o=prod: E.reciprocal(out=o, in_=o)),
                  waits=t3)
        w0 = U(C.bank_free[6], t4, C.t_consts)
        tbc = P.op("pe", (lambda E, o=C.pb[6][:, 0:512], l=ones_rr(C), r=prod:
                          E.matmul(out=o, lhsT=l, rhs=r, start=True, stop=True)),
                   waits=w0)
        C.bank_data[6] = tbc
        Gsb = sq[:, 512:768].bitcast(BF16)
        t5 = P.op("dve", (lambda E, o=Gsb, i=C.pb[6][:, 0:512]:
                          E.tensor_copy(out=o, in_=i)), waits=tbc)
        C.bank_free[6] = t5
        for ecs in range(16):
            psl = Pslab[:, ecs * 512:(ecs + 1) * 512]
            acol = C.slab[:, sm["Apart"] + s * 16 + ecs:
                          sm["Apart"] + s * 16 + ecs + 1]
            tA = P.op("dve", (lambda E, o=psl, i=psl, gg=Gsb, a=acol:
                              E.scalar_tensor_tensor(out=o, in0=i, scalar=1.0,
                                                     in1=gg, op0=ALU.mult,
                                                     op1=ALU.mult, accum_out=a)),
                      waits=U(P_toks[ecs], t5))
            Apart_toks.append(tA)
        tab_busy[ptab] = U(*[P_toks[e_] for e_ in range(8)])
        t_slab2 = U(*Apart_toks)
    t_Ap = U(*Apart_toks)
    wb_busy[0] = wbL_busy[0]
    wb_busy[1] = wbL_busy[1]
    C.t_wbL_free = U(wbL_busy[2], wbL_busy[3])
    Af = f32v(C, sm["Afin"], 16)
    tAs = P.op("dve", (lambda E, o=Af,
                       i=C.slab[:, sm["Apart"]:sm["Apart"] + 16],
                       j=C.slab[:, sm["Apart"] + 16:sm["Apart"] + 32]:
                       E.tensor_add(out=o, in0=i, in1=j)), waits=t_Ap)
    c2i, c2o = C.cc["c2i"], C.cc["c2o"]
    td = P.dma("pool", c2i.ap(), Af, waits=tAs, sem="cci")
    if SIM_NOCC:
        tl = []
        for r in range(8):
            tl.append(P.dma("pool", c2o.ap()[r * 128:(r + 1) * 128, :],
                            c2i.ap(), waits=td, sem="agl"))
        tcc = U(*tl)
    else:
        tcc = P.op("pool", (lambda E: E.collective_compute(
            "AllGather", ALU.bypass, replica_groups=RG,
            ins=[c2i.ap().opt()], outs=[c2o.ap().opt()])), waits=td)

    moe(C, t_Ap, tAs)

    s_ag = f32v(C, SC_OFF, 2048, p=8)
    tld = P.dma("pool", s_ag, c2o.ap().rearrange("(r p) c -> r (p c)", p=128),
                waits=U(tcc, t_Ap, getattr(C, "t_thr", None)), sem="agl")
    mask1 = C.slab[0:8, sm["mask"] + 1:sm["mask"] + 2]
    w0 = U(tld, C.bank_free[6], C.t_consts)
    tmo = None
    s_ag3 = s_ag.rearrange("r (p c) -> r p c", c=16)
    for cch in range(16):
        tmo = P.op("pe",
                   (lambda E, o=C.pb[6][:, cch:cch + 1],
                    l=s_ag3[:, :, cch], r=mask1:
                    E.matmul(out=o, lhsT=l, rhs=r, start=True, stop=True)),
                   waits=(w0 if cch == 0 else None), inc=(1 if cch == 15 else 0))
    C.bank_data[6] = tmo
    tof = P.op("dve", (lambda E, o=Af, i=C.pb[6][:, 0:16]:
                       E.tensor_copy(out=o, in_=i)), waits=U(tmo, tAs))
    C.bank_free[6] = tof
    tof2 = P.op("dve", (lambda E, o=Af:
                        E.tensor_scalar(out=o, in0=o, scalar1=0.5, scalar2=0.5,
                                        op0=ALU.mult, op1=ALU.add)), waits=tof)
    t_Afin = tof2

    YF = bf16v(C, SLAB2_OFF, 8192)
    bout_row = C.slab[0:1, ROWS_OFF:ROWS_OFF + 1024].bitcast(BF16)
    t_borow = P.dma("act", bout_row, io["bout_r"], waits=t_rows, sem="brow")
    ring_busy = [C.t_moe_xnf_done] * 8
    ru = [0]

    def ring_take(units):
        u = ru[0]
        if u % units:
            u += units - (u % units)
        if u + units > 8:
            u = 0
        ru[0] = u + units
        off = XNF_OFF + u * 1024
        tok = U(*[ring_busy[i] for i in range(u, u + units)])
        def mark(t):
            for i in range(u, u + units):
                ring_busy[i] = t
        return off, tok, mark, "rg%d" % u

    x_final = []
    xdc = [[] for _ in range(8)]
    for s in range(2):
        y_toks = []
        for ecp in range(8):
            ec0 = ecp * 2
            off, wtok, mark, rsem = ring_take(2)
            gr = bf16v(C, off, 4096).rearrange("p (e n) -> p e n", n=2048)
            tdq = P.dma("sp", gr,
                        io["WinQX"][ec0:ec0 + 2].rearrange(
                            "e p d f -> p e (d f)"),
                        waits=wtok, sem=rsem)
            consumed = []
            for ei in range(2):
                ec = ec0 + ei
                bq_b = (2, 3)[ec % 2]
                mms = [(gr[:, ei, dc * 128:(dc + 1) * 128],
                        znf[:, dc * 1024 + s * 512: dc * 1024 + s * 512 + 512])
                       for dc in range(16)]
                tq = mm_group(C, bq_b, mms, waits=U(tdq, C.t_Xnf),
                              out_slice=C.pb[bq_b][:, 0:512])
                ysl = YF[:, ec * 512:(ec + 1) * 512]
                bqc = C.slab[:, sm["binQ"] + ec: sm["binQ"] + ec + 1]
                aac = C.slab[:, sm["Afin"] + ec: sm["Afin"] + ec + 1]
                ty = P.op("dve", (lambda E, o=ysl, i=C.pb[bq_b][:, 0:512],
                                  a=bqc, b=aac:
                                  E.tensor_scalar(out=o, in0=i, scalar1=a,
                                                  scalar2=b, op0=ALU.add,
                                                  op1=ALU.mult)),
                          waits=U(tq, t_Afin, t_Ap, C.t_moe_slab2))
                bank_read(C, bq_b, ty)
                y_toks.append(ty)
                consumed.append(tq)
            mark(U(*consumed))
        t_Y = U(*y_toks)
        for egp in range(2):
            for dcp in range(8):
                off, wtok, mark, rsem = ring_take(1)
                gr = bf16v(C, off, 2048).rearrange("p (d e) -> p d e", e=1024)
                tdw = P.dma("sp", gr,
                            io["WoutR"][dcp * 2:dcp * 2 + 2, :,
                                        egp * 1024:(egp + 1) * 1024]
                            .rearrange("d p e -> p d e"),
                            waits=wtok, sem=rsem)
                last = None
                for dcl in range(2):
                    dc = dcp * 2 + dcl
                    for bidx in range(8):
                        tch, eg = bidx // 2, bidx % 2
                        st = (dc == 0)
                        w = U(C.bank_free[bidx] if st else None,
                              t_Y if st else None, tdw)
                        last = P.op("pe",
                            (lambda E, o=C.pb[bidx][:, 0:512],
                             l=YF[:, dc * 512 + tch * 128:
                                   dc * 512 + (tch + 1) * 128],
                             r=gr[:, dcl, eg * 512:(eg + 1) * 512], s_=st:
                             E.matmul(out=o, lhsT=l, rhs=r, start=s_, stop=False)),
                            waits=w, inc=(1 if bidx == 7 else 0))
                mark(last)
            for bidx in range(8):
                tch, eg = bidx // 2, bidx % 2
                tokmm = P.op("pe", (lambda E, o=C.pb[bidx][:, 0:512],
                                    l=ones_bf_r(C),
                                    r=bout_row[:, egp * 1024 + eg * 512:
                                               egp * 1024 + eg * 512 + 512]:
                                    E.matmul(out=o, lhsT=l, rhs=r, start=False,
                                             stop=True)),
                             waits=U(t_borow, C.t_consts), inc=1)
                C.bank_data[bidx] = tokmm
                xsl = f32v(C, X_OFF + (s * 4 + tch) * 2048 + egp * 1024 + eg * 512,
                           512)
                tev = P.op("dve", (lambda E, o=xsl, i=C.pb[bidx][:, 0:512]:
                                   E.tensor_add(out=o, in0=o, in1=i)),
                           waits=U(tokmm, C.t_moe_x[s * 4 + tch],
                                   C.t_xup[s * 4 + tch]))
                C.bank_free[bidx] = tev
                x_final.append(tev)
                xdc[s * 4 + tch].append(tev)
    C.t_xdone = U(*x_final)
    C.t_xdone_c = [U(*xdc[c]) for c in range(8)]


def moe(C, t_sq_free, tAs):
    P, io, sm = C.P, C.io, C.sm
    xnf = bf16v(C, XNF_OFF, 16384)
    sq = f32v(C, SP2_OFF, 2048)
    hf = f32v(C, HF_OFF, 1024)
    whb = bf16v(C, WH_OFF, 1024)
    wb, wb_busy = C.wb, C.wb_busy
    ident = f32v(C, sm["ident"], 128)

    trt = P.dma("sp", wb[0][:, 0:128].rearrange("p (dc e) -> p dc e", e=8),
                io["RtX"], waits=wb_busy[0], sem="rt")
    rtv = wb[0][:, 0:128].rearrange("p (dc e) -> p dc e", e=8)
    tdd = P.dma("sp", wb[1].rearrange("p (dc e) -> p dc e", e=128),
                io["DnX"], waits=wb_busy[1], sem="dn")
    dnb = wb[1].rearrange("p (dc e) -> p dc e", e=128)

    lg_toks = []
    for s in range(2):
        w0 = U(C.bank_free[7], trt, C.t_xnf)
        tok = None
        for i in range(16):
            tok = P.op("pe", (lambda E, o=C.pb[7][32:40, 0:512],
                              l=rtv[:, i, :],
                              r=xnf[:, i * 1024 + s * 512: i * 1024 + s * 512 + 512],
                              st=(i == 0), sp=(i == 15):
                              E.matmul(out=o, lhsT=l, rhs=r, start=st, stop=sp)),
                       waits=(w0 if i == 0 else None), inc=(1 if i == 15 else 0))
        C.bank_data[7] = tok
        lsb = f32v(C, SP2_OFF + 1024 + s * 512, 512, p0=32, p=8)
        te = P.op("dve", (lambda E, o=lsb, i=C.pb[7][32:40, 0:512]:
                          E.tensor_copy(out=o, in_=i)), waits=tok)
        C.bank_free[7] = te
        lg_toks.append(te)
    wb_busy[0] = U(wb_busy[0], *lg_toks)
    t_lg = U(*lg_toks)

    h_toks = []
    for s in range(2):
        mms = [(dnb[:, dc, :],
                xnf[:, dc * 1024 + s * 512: dc * 1024 + s * 512 + 512])
               for dc in range(16)]
        tm = mm_group(C, 7, mms, waits=U(tdd, C.t_xnf),
                      out_slice=C.pb[7][:, 0:512])
        te = P.op("dve", (lambda E, o=hf[:, s * 512:(s + 1) * 512],
                          i=C.pb[7][:, 0:512]:
                          E.tensor_copy(out=o, in_=i)),
                  waits=U(tm, C.t_kv_l1_free, C.t_wbL_free))
        bank_read(C, 7, te)
        h_toks.append(te)
    t_h = U(*h_toks)
    wb_busy[1] = t_h
    tdu = P.dma("sp", wb[0][:, 0:2048], io["UpA"],
                waits=U(wb_busy[0], t_lg), sem="up")
    upv = wb[0][:, 0:2048]

    ltok_toks = []
    for tcch in range(8):
        b = tcch % 2
        w0 = U(C.bank_free[b], t_lg, C.t_consts)
        tt_ = P.op("pe", (lambda E, o=C.pb[b][:, 0:8],
                          i=sq[32:40, 1024 + tcch * 128:1024 + (tcch + 1) * 128],
                          idn=ident[32:40, 32:40]:
                          E.transpose(out=o, in_=i, identity=idn)), waits=w0)
        C.bank_data[b] = tt_
        lcol = C.slab[:, sm["ltok"] + tcch * 8: sm["ltok"] + tcch * 8 + 8]
        te = P.op("dve", (lambda E, o=lcol, i=C.pb[b][:, 0:8]:
                          E.tensor_copy(out=o, in_=i)), waits=tt_)
        C.bank_free[b] = te
        ltok_toks.append(te)

    aoff = sm["Apart"]

    def m1c(t):
        return C.slab[:, aoff + t: aoff + t + 1]

    def ngc(t):
        return C.slab[:, aoff + 8 + t: aoff + 8 + t + 1]

    def dnc(t):
        return C.slab[:, aoff + 16 + t: aoff + 16 + t + 1]

    def rdc(t):
        return C.slab[:, aoff + 24 + t: aoff + 24 + t + 1]

    def lc(t):
        return C.slab[:, sm["ltok"] + t * 8: sm["ltok"] + t * 8 + 8]

    def wc(t):
        return C.slab[:, sm["wful"] + t * 8: sm["wful"] + t * 8 + 8]

    def e8b(t):
        return sq[:, 768 + 8 * t: 776 + 8 * t]

    ph1 = [None] * 8
    for t in range(8):
        t1 = P.op("dve", (lambda E, o=m1c(t), i=lc(t):
                          E.reduce_max(out=o, in_=i, axis=AX.X)),
                  waits=U(ltok_toks[t], tAs))
        t2 = P.op("dve", (lambda E, o=wc(t), i=lc(t), a=m1c(t):
                          E.tensor_scalar(out=o, in0=i, scalar1=a, scalar2=None,
                                          op0=ALU.is_ge)), waits=t1)
        t3 = P.op("dve", (lambda E, o=wc(t), i=lc(t):
                          E.scalar_tensor_tensor(out=o, in0=o, scalar=-BIG,
                                                 in1=i, op0=ALU.mult,
                                                 op1=ALU.add)), waits=t2)
        t4 = P.op("dve", (lambda E, o=dnc(t), i=wc(t):
                          E.reduce_max(out=o, in_=i, axis=AX.X)), waits=t3)
        t5 = P.op("dve", (lambda E, o=wc(t), i=lc(t), a=dnc(t):
                          E.tensor_scalar(out=o, in0=i, scalar1=a, scalar2=None,
                                          op0=ALU.is_ge)), waits=t4)
        t6 = P.op("dve", (lambda E, o=ngc(t), i=m1c(t):
                          E.tensor_scalar(out=o, in0=i, scalar1=-1.0,
                                          scalar2=None, op0=ALU.mult)), waits=t1)
        ph1[t] = U(t5, t6)
    ph2 = [None] * 8
    for t in range(8):
        ph2[t] = P.op("act", (lambda E, o=e8b(t), i=lc(t), a=ngc(t):
                              E.activation(out=o, in_=i, func=AF.Exp, bias=a)),
                      waits=ph1[t])
    wf_toks = []
    for t in range(8):
        t8 = P.op("dve", (lambda E, o=wc(t), i=e8b(t):
                          E.tensor_mul(out=o, in0=o, in1=i)), waits=ph2[t])
        t9 = P.op("dve", (lambda E, o=dnc(t), i=wc(t):
                          E.reduce_sum(out=o, in_=i, axis=AX.X)), waits=t8)
        t10 = P.op("dve", (lambda E, o=rdc(t), i=dnc(t):
                           E.reciprocal(out=o, in_=i)), waits=t9)
        t11 = P.op("dve", (lambda E, o=wc(t), a=rdc(t):
                           E.tensor_scalar(out=o, in0=o, scalar1=a, scalar2=None,
                                           op0=ALU.mult)), waits=t10)
        wf_toks.append(t11)
        C.emit_xup_one(t)

    wrow = sq[0:8, 1024:1536].bitcast(BF16)
    wT_toks = []
    for tcch in range(8):
        b = tcch % 2
        wcol = C.slab[:, sm["wful"] + tcch * 8: sm["wful"] + tcch * 8 + 8]
        w0 = U(C.bank_free[b], wf_toks[tcch], C.t_consts, t_sq_free)
        tt_ = P.op("pe", (lambda E, o=C.pb[b][0:8, 0:128], i=wcol, idn=ident:
                          E.transpose(out=o, in_=i, identity=idn)), waits=w0)
        C.bank_data[b] = tt_
        te = P.op("dve", (lambda E, o=wrow[:, tcch * 128:(tcch + 1) * 128],
                          i=C.pb[b][0:8, 0:128]:
                          E.tensor_copy(out=o, in_=i)), waits=tt_)
        C.bank_free[b] = te
        wT_toks.append(te)
    t_wT = U(*wT_toks)

    esl = C.slab[0:8, sm["esel"]:sm["esel"] + 64].bitcast(BF16)
    wrep_toks = []
    for s in range(2):
        w0 = U(C.bank_free[6], t_wT, C.t_consts)
        tm = P.op("pe", (lambda E, o=C.pb[6][:, 0:512], l=esl,
                         r=wrow[:, s * 512:(s + 1) * 512]:
                         E.matmul(out=o, lhsT=l, rhs=r, start=True, stop=True)),
                  waits=w0)
        C.bank_data[6] = tm
        twh = P.op("dve", (lambda E, o=whb[:, s * 512:(s + 1) * 512],
                           i=hf[:, s * 512:(s + 1) * 512], j=C.pb[6][:, 0:512]:
                           E.tensor_mul(out=o, in0=i, in1=j)),
                   waits=U(tm, t_h))
        C.bank_free[6] = twh
        wrep_toks.append(twh)
    t_wh = U(*wrep_toks)

    C.t_moe_x = [None] * 8
    up_toks = []
    for eg in range(4):
        for tch in range(8):
            b = tch
            w0 = U(C.bank_free[b], t_wh, tdu)
            tm = P.op("pe", (lambda E, o=C.pb[b][:, 0:512],
                             l=whb[:, tch * 128:(tch + 1) * 128],
                             r=upv[:, eg * 512:(eg + 1) * 512]:
                             E.matmul(out=o, lhsT=l, rhs=r, start=True,
                                      stop=True)), waits=w0)
            C.bank_data[b] = tm
            xsl = f32v(C, X_OFF + tch * 2048 + eg * 512, 512)
            tev = P.op("dve", (lambda E, o=xsl, i=C.pb[b][:, 0:512]:
                               E.tensor_add(out=o, in0=o, in1=i)),
                       waits=U(tm, C.t_xup[tch]))
            C.bank_free[b] = tev
            C.t_moe_x[tch] = U(C.t_moe_x[tch], tev)
            up_toks.append(tev)
    wb_busy[0] = U(*up_toks)
    C.t_moe_slab2 = None
    C.t_moe_xnf_done = U(t_lg, t_h)


def kernel(**inputs):
    g = stage_host(inputs)
    cores = [stage_core(inputs, cid, g) for cid in range(NCORE)]
    if FUSED:
        nc = build_nc("ALL")
        in_maps = []
        for cid in range(NCORE):
            m = {k: (cores[cid][k] if k in cores[cid] else g[k])
                 for k in KEYS_ALL}
            in_maps.append(m)
        res = run_bass_kernel_spmd(nc, in_maps, core_ids=list(range(NCORE)))
        outs = [res.results[c]["out"] for c in range(NCORE)]
    else:
        nc0 = build_nc("L0")
        in_maps = []
        for cid in range(NCORE):
            m = {k: (cores[cid][k] if k in cores[cid] else g[k])
                 for k in KEYS_L0}
            in_maps.append(m)
        r0 = run_bass_kernel_spmd(nc0, in_maps, core_ids=list(range(NCORE)))
        nc1 = build_nc("L1")
        in_maps1 = []
        for cid in range(NCORE):
            m = {k: (cores[cid][k] if k in cores[cid] else g[k])
                 for k in KEYS_L1}
            m["xin"] = np.ascontiguousarray(np.asarray(r0.results[cid]["out"]))
            in_maps1.append(m)
        r1 = run_bass_kernel_spmd(nc1, in_maps1, core_ids=list(range(NCORE)))
        outs = [r1.results[c]["out"] for c in range(NCORE)]
    full = np.concatenate([np.asarray(o) for o in outs], axis=0)
    return full.reshape(1, 8192, 2048).astype(np.float32)
